# version 03
# baseline (speedup 1.0000x reference)
import math
from contextlib import ExitStack

import numpy as np
import ml_dtypes

import concourse.bass as bass
import concourse.mybir as mybir
from concourse.bass_utils import run_bass_kernel_spmd

F32 = mybir.dt.float32
BF16 = mybir.dt.bfloat16
AF = mybir.ActivationFunctionType
ALU = mybir.AluOpType
AXL = mybir.AxisListType
MAGIC = 12582912.0
NEG = -1.0e38

DEBUG = False
LIMIT = ""
HWDBG = False
LAYOUT = {}
NCORES = 8
NSEQ = 2
T = 2048
C = 1024
H = 16
D = 64
FF = 4096
NTT = 4
TTOK = 512
NIC = 8
NFC = 32
EPS_LN = 1e-5
EPS_Q = 1e-8



def _rotary_tables():
    inv = 1.0 / (10000.0 ** (np.arange(0, D, 2, dtype=np.float64) / D))
    t = np.arange(T, dtype=np.float64)
    fr = np.einsum("i,j->ij", t, inv)
    fr = np.concatenate([fr, fr], axis=1)
    cos = np.cos(fr).astype(np.float32)
    sin = np.sin(fr).astype(np.float32)
    ssw = sin.copy()
    ssw[:, 32:] = -ssw[:, 32:]
    cosTM = cos.reshape(16, 128, 64).transpose(1, 0, 2).reshape(128, 16 * 64).copy()
    sswTM = ssw.reshape(16, 128, 64).transpose(1, 0, 2).reshape(128, 16 * 64).copy()
    return cosTM, sswTM


def _bucket(n):
    n = np.asarray(n, dtype=np.int64)
    max_exact = 16
    out = np.where(n < max_exact, n, 0).astype(np.int64)
    big = n >= max_exact
    nsafe = np.maximum(n, 1).astype(np.float64)
    val = max_exact + (np.log(nsafe / max_exact) / math.log(128.0 / max_exact) * 16.0)
    val = val.astype(np.int32)
    val = np.minimum(val, 31)
    out = np.where(big, val, out)
    return out


def _wt_tables(rel_bias_table):
    table = np.asarray(rel_bias_table, dtype=np.float32)
    n = np.arange(-2047, 2048, dtype=np.int64)
    g = np.empty((H, n.size), dtype=np.float32)
    pos = n >= 0
    buck = _bucket(np.maximum(n, 0))
    for h in range(H):
        g[h] = np.where(pos, table[buck, h], np.float32(NEG))
    M = 2688
    p = np.arange(128)[:, None]
    m = np.arange(M)[None, :]
    idx = m - p - 511 + 2047
    idx = np.clip(idx, 0, n.size - 1)
    wt = g[:, idx]
    return np.ascontiguousarray(wt)



class Sem:
    def __init__(self, handle):
        self.h = handle
        self.n = 0

    def inc(self, amount=1):
        self.n += amount
        return self.n

    def tok(self):
        return (self, self.n)


class Prog:

    ENGS = ("sync", "vector", "scalar", "tensor", "gpsimd")

    def __init__(self):
        self.streams = {e: [] for e in self.ENGS}

    def wait(self, eng, tok):
        if tok is None:
            return
        sem, val = tok
        if val <= 0:
            return
        self.streams[eng].append(lambda e, s=sem.h, v=val: e.wait_ge(s, v))

    def waits(self, eng, toks):
        best = {}
        for t in toks:
            if t is None:
                continue
            s, v = t
            if v > 0 and (id(s) not in best or best[id(s)][1] < v):
                best[id(s)] = (s, v)
        for s, v in best.values():
            self.wait(eng, (s, v))

    def emit(self, eng, fn, inc=None, inc_amt=1):
        if inc is None:
            self.streams[eng].append(fn)
            return None
        sem = inc
        val = sem.inc(inc_amt)

        def run(e, fn=fn, s=sem.h, a=inc_amt):
            inst = fn(e)
            inst.then_inc(s, a)
        self.streams[eng].append(run)
        return (sem, val)


class PsumBank:
    def __init__(self, tile):
        self.t = tile
        self.wr = None
        self.rd = None



def build_nc():
    nc = bass.Bass("TRN2", target_bir_lowering=False, detect_race_conditions=False)

    d_x = nc.dram_tensor("x_fm", [NSEQ * C, T], F32, kind="ExternalInput").ap()
    d_wq = nc.dram_tensor("wq", [128, NIC * C], F32, kind="ExternalInput").ap()
    d_wk = nc.dram_tensor("wk", [128, NIC * C], F32, kind="ExternalInput").ap()
    d_wv = nc.dram_tensor("wv", [128, NIC * C], F32, kind="ExternalInput").ap()
    d_wp = nc.dram_tensor("wp", [128, NIC * C], F32, kind="ExternalInput").ap()
    d_brows = nc.dram_tensor("brows", [1, 3 * C], F32, kind="ExternalInput").ap()
    d_bp = nc.dram_tensor("bp_c", [128, NIC], F32, kind="ExternalInput").ap()
    d_w1 = nc.dram_tensor("w1", [FF, C], F32, kind="ExternalInput").ap()
    d_b1 = nc.dram_tensor("b1_c", [128, NFC], F32, kind="ExternalInput").ap()
    d_w2 = nc.dram_tensor("w2", [C, FF], F32, kind="ExternalInput").ap()
    d_b2 = nc.dram_tensor("b2_c", [128, NIC], F32, kind="ExternalInput").ap()
    d_wt = nc.dram_tensor("wt", [H * 128, 2688], F32, kind="ExternalInput").ap()
    d_cos = nc.dram_tensor("cosTM", [128, 16 * 64], F32, kind="ExternalInput").ap()
    d_ssw = nc.dram_tensor("sswTM", [128, 16 * 64], F32, kind="ExternalInput").ap()
    d_idb = nc.dram_tensor("identb", [128, 128], BF16, kind="ExternalInput").ap()
    d_idf = nc.dram_tensor("identf", [128, 128], F32, kind="ExternalInput").ap()
    d_eps = nc.dram_tensor("epscol", [128, 1], F32, kind="ExternalInput").ap()

    d_out = nc.dram_tensor("out_fm", [NSEQ * C, T], F32, kind="ExternalOutput").ap()
    if HWDBG:
        dbg_o1 = nc.dram_tensor("dbg_o1", [NSEQ * C, T], F32, kind="ExternalOutput").ap()
        dbg_y = nc.dram_tensor("dbg_y", [NSEQ * 2 * C, T], BF16, kind="ExternalOutput").ap()
        dbg_h = nc.dram_tensor("dbg_h", [FF, NSEQ * T], F32, kind="ExternalOutput").ap()
        dbg_s = nc.dram_tensor("dbg_s", [1, 8], F32, kind="ExternalOutput").ap()
        dbg_g = nc.dram_tensor("dbg_g", [128, 16], F32, kind="ExternalOutput").ap()
        dbg_q = nc.dram_tensor("dbg_q", [NSEQ * 2 * T, C], BF16, kind="ExternalOutput").ap()

    d_wsp = nc.dram_tensor("wsplit", [4 * 2 * 128, NIC * C], BF16)
    d_qs = nc.dram_tensor("q_tm", [NSEQ * 2 * T, C], BF16)
    d_ks = nc.dram_tensor("k_tm", [NSEQ * 2 * T, C], BF16)
    d_vs = nc.dram_tensor("v_tm", [NSEQ * T, H * 65], F32)
    d_yt = nc.dram_tensor("y_fm", [NSEQ * 2 * C, T], BF16)
    d_o1 = nc.dram_tensor("out1_fm", [NSEQ * C, T], F32)
    d_mr = nc.dram_tensor("m2r2", [NSEQ * NTT, 2 * TTOK], F32)
    d_h = nc.dram_tensor("h_fm", [FF, NSEQ * T], F32)
    d_cci = nc.dram_tensor("cc_in", [1, 8], F32)
    d_cco = nc.dram_tensor("cc_out", [1, 8], F32)
    d_cci2 = nc.dram_tensor("cc_in2", [1, 8], F32)
    d_cco2 = nc.dram_tensor("cc_out2", [1, 8], F32)
    if DEBUG:
        d_dbg = nc.dram_tensor("dbg", [128, 8192], F32)
        d_dbgb = nc.dram_tensor("dbgb", [128, 8192], BF16)
        d_dbgw = nc.dram_tensor("dbgw", [128, 16384], BF16)
        d_dbgb2 = nc.dram_tensor("dbgb2", [128, 16384], BF16)

    prog = Prog()

    with ExitStack() as ctx:
        AW = 52992
        arena = ctx.enter_context(nc.sbuf_tensor("arena", [128, AW], F32))

        alloc_off = [0]

        def alloc32(n, align=16):
            o = (alloc_off[0] + align - 1) // align * align
            alloc_off[0] = o + n
            assert alloc_off[0] <= AW, f"arena overflow: {alloc_off[0]} > {AW}"
            return o

        def f32(n, align=16):
            o = alloc32(n, align)
            return arena[:, o:o + n]

        def b16(n, align=16):
            o = alloc32((n + 1) // 2, align)
            return arena[:, o:o + (n + 1) // 2].bitcast(BF16)

        s_ones = f32(128)
        s_i1024 = f32(128)
        s_idf = f32(128)
        s_idb = b16(128)
        s_eps = f32(1)
        s_bp = f32(NIC)
        s_b1 = f32(NFC)
        s_b2 = f32(NIC)
        s_sw1 = f32(NFC)
        s_rw1 = f32(NFC)
        s_sw2 = f32(NIC)
        s_rw2 = f32(NIC)
        s_sc1 = f32(NFC)
        s_sb1 = f32(NFC)
        s_sc2 = f32(NIC)
        s_sb2 = f32(NIC)
        s_gmax = f32(2)
        s_s1r = f32(2)
        s_s2r = f32(2)
        s_rows = f32(2048)
        s_park = f32(16)
        s_cc = f32(8)

        PERSIST_END = alloc_off[0]

        pbanks = [PsumBank(ctx.enter_context(nc.psum_tensor(f"ps{i}", [128, 512], F32)))
                  for i in range(7)]
        psbf_t = ctx.enter_context(nc.psum_tensor("psbf", [128, 1024], BF16))
        psbf = PsumBank(psbf_t)

        LD = Sem(ctx.enter_context(nc.semaphore("ld")))
        GQ = Sem(ctx.enter_context(nc.semaphore("gq")))
        PE = Sem(ctx.enter_context(nc.semaphore("pe")))
        DV = Sem(ctx.enter_context(nc.semaphore("dv")))
        AC = Sem(ctx.enter_context(nc.semaphore("ac")))
        CCS1 = Sem(ctx.enter_context(nc.semaphore("ccs1")))
        CCS2 = Sem(ctx.enter_context(nc.semaphore("ccs2")))

        em = prog.emit
        wt = prog.waits

        def ld_dma(dst, src, waits=()):
            wt("sync", waits)
            return em("sync", lambda e, d=dst, s=src: e.dma_start(out=d, in_=s),
                      inc=LD, inc_amt=16)

        def ld_dma_tr(dst, src, waits=()):
            wt("sync", waits)
            return em("sync", lambda e, d=dst, s=src: e.dma_start(out=d, in_=s, transpose=True),
                      inc=LD, inc_amt=16)

        def st_dma(dst, src, waits=()):
            wt("gpsimd", waits)
            return em("gpsimd", lambda e, d=dst, s=src: e.dma_start(out=d, in_=s),
                      inc=GQ, inc_amt=16)

        def v_op(fn, waits=(), inc=False, drain=False):
            wt("vector", waits)
            if drain:
                tok = em("vector", fn, inc=DV)
                prog.wait("vector", tok)
                return tok
            return em("vector", fn, inc=DV if inc else None)

        def a_op(fn, waits=(), inc=False):
            wt("scalar", waits)
            return em("scalar", fn, inc=AC if inc else None)

        def split_hilo(src32, dst_hi, dst_lo, waits=(), inc=True):
            v_op(lambda e, a=dst_hi, b=src32: e.tensor_copy(a, b), waits=waits)
            return v_op(lambda e, o=dst_lo, a=src32, b=dst_hi:
                        e.tensor_tensor(o, a, b, ALU.subtract), inc=inc)

        tk_idf = ld_dma(s_idf, d_idf)
        tk_idb = ld_dma(s_idb, d_idb)
        tk_eps = ld_dma(s_eps, d_eps)
        tk_bp = ld_dma(s_bp, d_bp)
        tk_b1 = ld_dma(s_b1, d_b1)
        tk_b2 = ld_dma(s_b2, d_b2)
        v_op(lambda e: e.memset(s_ones, 1.0))
        v_op(lambda e: e.memset(s_i1024, 1.0 / 1024.0))

        def pool():
            toks = [(PE, PE.n), (DV, DV.n), (AC, AC.n), (GQ, GQ.n), (LD, LD.n)]
            for eng in ("sync", "vector", "scalar", "tensor", "gpsimd"):
                wt(eng, toks)
            alloc_off[0] = PERSIST_END
            return alloc_off[0]

        pool()
        wtmp = [f32(NIC * C) for _ in range(2)]
        whi_st = b16(NIC * C)
        wlo_st = b16(NIC * C)
        wload = [None, None]
        wins = [d_wq, d_wk, d_wv, d_wp]
        prev_done = None
        prev_stores = []
        for wi in range(4):
            par = wi % 2
            if wi == 0:
                wload[0] = ld_dma(wtmp[0], wins[0])
            if wi + 1 < 4:
                wload[(wi + 1) % 2] = ld_dma(wtmp[(wi + 1) % 2], wins[wi + 1],
                                             waits=[] if wi == 0 else [prev_done])
            tk = split_hilo(wtmp[par], whi_st, wlo_st,
                            waits=[wload[par]] + prev_stores)
            t1 = st_dma(d_wsp[(wi * 2 + 0) * 128:(wi * 2 + 1) * 128, :], whi_st, waits=[tk])
            t2 = st_dma(d_wsp[(wi * 2 + 1) * 128:(wi * 2 + 2) * 128, :], wlo_st, waits=[tk])
            prev_done = tk
            prev_stores = [t1, t2]
        wsp_tok = prev_stores[1]

        def emit_ln(chunks, xsq, m_sb, r_sb, negm2, st_b1, st_b2, src_toks,
                    r_free=None):
            pe_sq = [(PE, PE.n), (PE, PE.n)]
            for ic in range(8):
                par = ic % 2
                wl = [src_toks[ic], pe_sq[par]]
                tk = a_op(lambda e, o=xsq[par], i=chunks[ic]:
                          e.activation(o, i, AF.Square), waits=wl, inc=True)
                wt("tensor", [st_b1.rd if ic == 0 else None, src_toks[ic]])
                em("tensor", lambda e, p=st_b1.t, r=chunks[ic], s=(ic == 0), t=(ic == 7):
                   e.matmul(p[0:1, :], s_ones[:, 0:1], r, start=s, stop=t))
                wt("tensor", [st_b2.rd if ic == 0 else None, tk])
                tk2 = em("tensor", lambda e, p=st_b2.t, r=xsq[par], s=(ic == 0), t=(ic == 7):
                         e.matmul(p[0:1, :], s_ones[:, 0:1], r, start=s, stop=t),
                         inc=PE)
                pe_sq[par] = tk2
            st_b1.wr = tk2
            st_b2.wr = tk2
            r1 = s_rows[0:1, 0:512]
            r2 = s_rows[0:1, 512:1024]
            v_op(lambda e: e.tensor_copy(r1, st_b1.t[0:1, :]), waits=[st_b1.wr])
            tk_rows = v_op(lambda e: e.tensor_copy(r2, st_b2.t[0:1, :]), inc=True)
            st_b1.rd = tk_rows
            st_b2.rd = tk_rows
            wt("tensor", [tk_rows, st_b1.rd])
            em("tensor", lambda e: e.matmul(st_b1.t[:, :], s_i1024[0:1, 0:128], r1,
                                            start=True, stop=True))
            tkb = em("tensor", lambda e: e.matmul(st_b2.t[:, :], s_i1024[0:1, 0:128], r2,
                                                  start=True, stop=True), inc=PE)
            st_b1.wr = tkb
            st_b2.wr = tkb
            v_op(lambda e: e.tensor_copy(m_sb, st_b1.t[:, :]), waits=[tkb])
            v_op(lambda e: e.scalar_tensor_tensor(negm2, st_b1.t[:, :], -1.0, m_sb,
                                                  ALU.mult, ALU.mult))
            tk_var = v_op(lambda e: e.tensor_tensor(negm2, st_b2.t[:, :], negm2, ALU.add),
                          inc=True)
            st_b1.rd = tk_var
            st_b2.rd = tk_var
            a_op(lambda e: e.activation(r_sb, negm2, AF.Ln, bias=s_eps),
                 waits=[tk_var, r_free])
            tk_r = a_op(lambda e: e.activation(r_sb, r_sb, AF.Exp, scale=-0.5), inc=True)
            return tk_r

        yt_store_toks = []
        o1_store_toks = []
        mr_store_toks = []
        cmax_tok = None

        for s in range(NSEQ if LIMIT in ("", "D") else (0 if LIMIT == "W" else 1)):
            pool()
            s_cos = f32(1024)
            s_ssw = f32(1024)
            s_brow = f32(3 * C)
            xb = [f32(NIC * 512) for _ in range(2)]
            xsq = [f32(512) for _ in range(2)]
            m_sb = f32(512); r_sb = f32(512); negm2 = f32(512)
            xlnh = b16(NIC * 512); xlnl = b16(NIC * 512)
            t1b = f32(1024); ub = f32(1024); qp32 = f32(1024)
            sth = b16(1024); stl = b16(1024)
            vstage = f32(H * 65)
            wqh = b16(NIC * C); wql = b16(NIC * C)
            wkh = b16(NIC * C); wkl = b16(NIC * C)
            wvh = b16(NIC * C); wvl = b16(NIC * C)
            a_end = alloc_off[0]

            wspw = [wsp_tok] if s == 0 else []
            def wsp_rows(wi, hl):
                r0 = (wi * 2 + hl) * 128
                return d_wsp[r0:r0 + 128, :]
            tk_wq = [ld_dma(wqh, wsp_rows(0, 0), waits=wspw),
                     ld_dma(wql, wsp_rows(0, 1))]
            tk_wk = [ld_dma(wkh, wsp_rows(1, 0)),
                     ld_dma(wkl, wsp_rows(1, 1))]
            tk_wv = [ld_dma(wvh, wsp_rows(2, 0)),
                     ld_dma(wvl, wsp_rows(2, 1))]
            tk_cos = ld_dma(s_cos, d_cos)
            tk_ssw = ld_dma(s_ssw, d_ssw)
            tk_brow = ld_dma(s_brow[0:1, :], d_brows)

            v_op(lambda e: e.memset(vstage, 1.0))

            x_tok = [None, None]
            x_free = [None, None]

            def load_x(tt, dst_par):
                sap = d_x[s * C:(s + 1) * C, tt * 512:(tt + 1) * 512] \
                    .rearrange("(ic p) j -> p ic j", p=128)
                return ld_dma(xb[dst_par][:, :].rearrange("p (ic j) -> p ic j", ic=8),
                              sap, waits=[x_free[dst_par]] if x_free[dst_par] else [])

            x_tok[0] = load_x(0, 0)

            for tt in range(NTT):
                par = tt % 2
                if tt + 1 < NTT:
                    x_tok[(tt + 1) % 2] = load_x(tt + 1, (tt + 1) % 2)
                chunks = [xb[par][:, ic * 512:(ic + 1) * 512] for ic in range(8)]
                src_toks = [x_tok[par]] * 8
                tk_r = emit_ln(chunks, xsq, m_sb, r_sb, negm2,
                               pbanks[5], pbanks[6], src_toks,
                               r_free=(DV, DV.n))
                if DEBUG and s == 0 and tt == 0:
                    dtk = DV.tok()
                    atk = AC.tok()
                    st_dma(d_dbg[:, 0:4096], xb[par], waits=[x_tok[par]])
                    st_dma(d_dbg[:, 4096:4608], m_sb, waits=[dtk, atk])
                    st_dma(d_dbg[:, 4608:5120], r_sb, waits=[dtk, atk])
                    st_dma(d_dbg[:, 5120:5632], negm2, waits=[dtk, atk])
                if tt > 0:
                    wt("vector", [(PE, PE.n)])
                last_split = None
                for ic in range(8):
                    xc = chunks[ic]
                    xh = xlnh[:, ic * 512:(ic + 1) * 512]
                    xl = xlnl[:, ic * 512:(ic + 1) * 512]
                    v_op(lambda e, o=qp32[:, 0:512], a=xc, b=m_sb:
                         e.tensor_tensor(o, a, b, ALU.subtract),
                         waits=[tk_r] if ic == 0 else [])
                    v_op(lambda e, o=qp32[:, 0:512], a=qp32[:, 0:512], b=r_sb:
                         e.tensor_tensor(o, a, b, ALU.mult))
                    last_split = split_hilo(qp32[:, 0:512], xh, xl,
                                            inc=(ic == 7))
                x_free[par] = last_split
                xln_tok = last_split

                if DEBUG and s == 0 and tt == 0:
                    st_dma(d_dbgb[:, 0:4096], xlnh, waits=[xln_tok])
                    st_dma(d_dbgb[:, 4096:8192], xlnl, waits=[xln_tok])
                    st_dma(d_dbgw[:, 0:8192], wvh, waits=tk_wv)
                    st_dma(d_dbgw[:, 8192:16384], wvl, waits=tk_wv)
                for tc in range(4):
                    tcg = tt * 4 + tc
                    for pi, (wh, wl, wtk, brow_off, is_v) in enumerate([
                            (wqh, wql, tk_wq, 0, False),
                            (wkh, wkl, tk_wk, C, False),
                            (wvh, wvl, tk_wv, 2 * C, True)]):
                        for fh in range(2):
                            bank = pbanks[((tc * 3 + pi) * 2 + fh) % 4]
                            wt("tensor", [bank.rd, xln_tok] + wtk + [tk_brow])
                            nmm = 0
                            for ic in range(8):
                                lh = xlnh[:, ic * 512 + tc * 128: ic * 512 + tc * 128 + 128]
                                ll = xlnl[:, ic * 512 + tc * 128: ic * 512 + tc * 128 + 128]
                                rh = wh[:, ic * 1024 + fh * 512: ic * 1024 + fh * 512 + 512]
                                rl = wl[:, ic * 1024 + fh * 512: ic * 1024 + fh * 512 + 512]
                                for (a_, b_) in ((lh, rh), (lh, rl), (ll, rh)):
                                    em("tensor", lambda e, p=bank.t, A=a_, B=b_, st=(nmm == 0):
                                       e.matmul(p[:, :], A, B, start=st, stop=False))
                                    nmm += 1
                            brow = s_brow[0:1, brow_off + fh * 512: brow_off + fh * 512 + 512]
                            bank.wr = em("tensor", lambda e, p=bank.t, r=brow:
                                         e.matmul(p[:, :], s_ones[0:1, 0:128], r,
                                                  start=False, stop=True), inc=PE)
                            if not is_v:
                                cosv = s_cos[:, tcg * 64:(tcg + 1) * 64] \
                                    .unsqueeze(1).broadcast_to((128, 8, 64))
                                sswv = s_ssw[:, tcg * 64:(tcg + 1) * 64] \
                                    .unsqueeze(1).broadcast_to((128, 8, 64))
                                t1v = t1b[:, fh * 512:(fh + 1) * 512] \
                                    .rearrange("p (h m) -> p h m", h=8)
                                uv = ub[:, fh * 512:(fh + 1) * 512] \
                                    .rearrange("p (h m) -> p h m", h=8)
                                pv = bank.t[:, :].rearrange("p (h m) -> p h m", h=8)
                                v_op(lambda e, o=t1v, a=pv, b=cosv:
                                     e.tensor_tensor(o, a, b, ALU.mult),
                                     waits=[bank.wr, tk_cos, tk_ssw])
                                bank.rd = v_op(lambda e, o=uv, a=pv, b=sswv:
                                               e.tensor_tensor(o, a, b, ALU.mult), inc=True)
                                t4 = t1b[:, fh * 512:(fh + 1) * 512] \
                                    .rearrange("p (h z m) -> p h z m", h=8, z=2)
                                u4 = ub[:, fh * 512:(fh + 1) * 512] \
                                    .rearrange("p (h z m) -> p h z m", h=8, z=2)
                                q4 = qp32[:, fh * 512:(fh + 1) * 512] \
                                    .rearrange("p (h z m) -> p h z m", h=8, z=2)
                                v_op(lambda e, o=q4[:, :, 0, :], a=t4[:, :, 0, :],
                                     b=u4[:, :, 1, :]: e.tensor_tensor(o, a, b, ALU.add))
                                v_op(lambda e, o=q4[:, :, 1, :], a=t4[:, :, 1, :],
                                     b=u4[:, :, 0, :]: e.tensor_tensor(o, a, b, ALU.add))
                            else:
                                dstv = vstage[:, fh * 520: fh * 520 + 520] \
                                    .rearrange("p (h m) -> p h m", h=8)[:, :, 0:64]
                                bank.rd = v_op(lambda e, o=dstv,
                                               a=bank.t[:, :].rearrange("p (h m) -> p h m", h=8):
                                               e.tensor_copy(o, a), waits=[bank.wr], inc=True)
                        if not is_v:
                            tkq = split_hilo(qp32, sth, stl, inc=True)
                            dram = d_qs if pi == 0 else d_ks
                            row0 = tt * 512 + tc * 128
                            tk1 = st_dma(dram[(s * 2 + 0) * T + row0:
                                               (s * 2 + 0) * T + row0 + 128, :],
                                         sth, waits=[tkq])
                            tk2 = st_dma(dram[(s * 2 + 1) * T + row0:
                                               (s * 2 + 1) * T + row0 + 128, :],
                                         stl, waits=[tkq])
                            wt("vector", [tk1, tk2])
                        else:
                            row0 = s * T + tt * 512 + tc * 128
                            tkv = st_dma(d_vs[row0:row0 + 128, :], vstage,
                                         waits=[bank.rd])
                            wt("vector", [tkv])

            if LIMIT == "A":
                break
            pool()
            if DEBUG:
                LAYOUT[f"phaseB_start_s{s}"] = alloc_off[0]
            qf = [[b16(T), b16(T)] for _ in range(2)]
            if DEBUG:
                LAYOUT[f"qf_end_s{s}"] = alloc_off[0]
            kf = [[b16(T), b16(T)] for _ in range(2)]
            vsb = [f32(16 * 130) for _ in range(2)]
            wtp = [f32(2 * 2688) for _ in range(2)]
            tbuf = [f32(512) for _ in range(4)]
            ebuf = [f32(512) for _ in range(4)]
            bcb = f32(512)
            zb = f32(512)
            ysh = [b16(T), b16(T)]
            ysl = [b16(T), b16(T)]
            b_end = alloc_off[0]
            assert b_end <= AW

            def load_hp(hp, par, guards):
                toks = []
                for dram, buf in ((d_qs, qf[par]), (d_ks, kf[par])):
                    for hl in range(2):
                        srcap = dram[(s * 2 + hl) * T:(s * 2 + hl + 1) * T,
                                     hp * 128:(hp + 1) * 128]
                        toks.append(ld_dma_tr(buf[hl], srcap, waits=guards))
                sap = d_vs[s * T:(s + 1) * T, hp * 130:(hp + 1) * 130] \
                    .rearrange("(jt p) c -> p jt c", p=128)
                toks.append(ld_dma(vsb[par][:, :].rearrange("p (jt c) -> p jt c", jt=16),
                                   sap, waits=guards))
                toks.append(ld_dma(wtp[par][:, 0:2688],
                                   d_wt[2 * hp * 128:(2 * hp + 1) * 128, :], waits=guards))
                toks.append(ld_dma(wtp[par][:, 2688:5376],
                                   d_wt[(2 * hp + 1) * 128:(2 * hp + 2) * 128, :],
                                   waits=guards))
                return toks

            wt("sync", [GQ.tok()])
            hp_tok = [None, None]
            hp_done = [(), ()]
            av_pe = [None, None, None, None]
            tb_ac = [None, None, None, None]
            hp_tok[0] = load_hp(0, 0, [])

            for hp in range(8):
                par = hp % 2
                if hp + 1 < 8:
                    hp_tok[(hp + 1) % 2] = load_hp(hp + 1, (hp + 1) % 2,
                                                   list(hp_done[(hp + 1) % 2]))
                wt("tensor", hp_tok[par])
                qh, ql = qf[par]
                kh, kl = kf[par]
                av0, av1 = pbanks[4], pbanks[5]
                for qb in range(4):
                    q0 = qb * 512
                    njt = 4 * qb + 4
                    for jt in range(njt):
                        sb0 = pbanks[jt % 2]
                        sb1 = pbanks[2 + jt % 2]
                        wt("tensor", [sb0.rd, sb1.rd])
                        for ip, (a_, b_) in enumerate((("h", "h"), ("h", "l"), ("l", "h"))):
                            ka = kh if a_ == "h" else kl
                            qa = qh if b_ == "h" else ql
                            em("tensor", lambda e, p=sb0.t,
                               A=ka[0:64, jt * 128:(jt + 1) * 128],
                               B=qa[0:64, q0:q0 + 512], st=(ip == 0), sp=(ip == 2):
                               e.matmul(p[:, :], A, B, start=st, stop=sp,
                                        tile_position=(0, 0)))
                            tk_qk = em("tensor", lambda e, p=sb1.t,
                                       A=ka[64:128, jt * 128:(jt + 1) * 128],
                                       B=qa[64:128, q0:q0 + 512], st=(ip == 0), sp=(ip == 2):
                                       e.matmul(p[:, :], A, B, start=st, stop=sp,
                                                tile_position=(64, 0)),
                                       inc=PE if ip == 2 else None)
                        sb0.wr = tk_qk
                        sb1.wr = tk_qk
                        off = (q0 - 128 * jt) + 511
                        for hh, sb in ((0, sb0), (1, sb1)):
                            slot = 2 * (jt % 2) + hh
                            tb = tbuf[slot]
                            wslice = wtp[par][:, hh * 2688 + off: hh * 2688 + off + 512]
                            tk_stt = v_op(lambda e, o=tb, p=sb.t, w=wslice:
                                          e.scalar_tensor_tensor(o, p[:, :], 1.0, w,
                                                                 ALU.mult, ALU.add),
                                          waits=[sb.wr, tb_ac[slot]]
                                          + (hp_tok[par] if jt == 0 else []),
                                          inc=True)
                            sb.rd = tk_stt
                            ebx = ebuf[slot]
                            tk_exp = a_op(lambda e, o=ebx, i=tb:
                                          e.activation(o, i, AF.Exp, scale=0.125),
                                          waits=[tk_stt, av_pe[slot]],
                                          inc=True)
                            tb_ac[slot] = tk_exp
                            avb = av0 if hh == 0 else av1
                            vsl = vsb[par][:, jt * 130 + hh * 65: jt * 130 + hh * 65 + 65]
                            wt("tensor", [tk_exp] + ([avb.rd] if jt == 0 else []))
                            tk_av = em("tensor", lambda e, p=avb.t, A=vsl, B=ebx,
                                       st=(jt == 0), sp=(jt == njt - 1):
                                       e.matmul(p[0:65, :], A, B, start=st, stop=sp),
                                       inc=PE)
                            avb.wr = tk_av
                            av_pe[2 * (jt % 2) + hh] = tk_av
                    for hh, avb in ((0, av0), (1, av1)):
                        srow = s_rows[64:65, hh * 512: hh * 512 + 512]
                        tk_s = v_op(lambda e, o=srow, p=avb.t:
                                    e.tensor_copy(o, p[64:65, :]), waits=[avb.wr], inc=True)
                        a_op(lambda e, o=srow, i=srow: e.activation(o, i, AF.Ln),
                             waits=[tk_s])
                        tk_rr = a_op(lambda e, o=srow, i=srow:
                                     e.activation(o, i, AF.Exp, scale=-1.0), inc=True)
                        ub6 = pbanks[6]
                        wt("tensor", [tk_rr, ub6.rd])
                        ub6.wr = em("tensor", lambda e, p=ub6.t, r=srow:
                                    e.matmul(p[:, :], s_ones[64:65, 0:128], r,
                                             start=True, stop=True), inc=PE)
                        v_op(lambda e, o=bcb, p=ub6.t: e.tensor_copy(o, p[:, :]),
                             waits=[ub6.wr])
                        ub6.rd = v_op(lambda e, o=zb[0:64, :], a=avb.t[0:64, :], b=bcb[0:64, :]:
                                      e.tensor_tensor(o, a, b, ALU.mult), inc=True)
                        avb.rd = ub6.rd
                        ysh_t = ysh[hh][0:64, q0:q0 + 512]
                        ysl_t = ysl[hh][0:64, q0:q0 + 512]
                        v_op(lambda e, o=ysh_t, i=zb[0:64, :]: e.tensor_copy(o, i))
                        v_op(lambda e, o=ysl_t, a=zb[0:64, :], b=ysh_t:
                             e.tensor_tensor(o, a, b, ALU.subtract), inc=True)
                hp_done[par] = ((PE, PE.n), (DV, DV.n), (AC, AC.n))
                tk_y = DV.tok()
                for hh in range(2):
                    rows0 = hp * 128 + hh * 64
                    yh_d = d_yt[(s * 2 + 0) * C + rows0:(s * 2 + 0) * C + rows0 + 64, :]
                    yl_d = d_yt[(s * 2 + 1) * C + rows0:(s * 2 + 1) * C + rows0 + 64, :]
                    yt_store_toks.append(st_dma(yh_d, ysh[hh][0:64, :], waits=[tk_y]))
                    yt_store_toks.append(st_dma(yl_d, ysl[hh][0:64, :], waits=[tk_y]))
                wt("vector", [yt_store_toks[-1]])

            if LIMIT == "B":
                break
            pool()
            wptmp = f32(NIC * C)
            wph = b16(NIC * C); wpl = b16(NIC * C)
            ytb = [[b16(NIC * 512), b16(NIC * 512)] for _ in range(2)]
            aob = f32(NIC * 512)
            xcb = [f32(NIC * 512) for _ in range(2)]
            o1b = f32(NIC * 512)
            xsqc = [f32(512) for _ in range(2)]
            m_sbc = f32(512); r_sbc = f32(512); negm2c = f32(512)
            zt = f32(512)
            gmaxcol = s_gmax[:, 0:1]
            c_end = alloc_off[0]

            tk_wp = ld_dma(wptmp, d_wp)
            tk_wps = split_hilo(wptmp, wph, wpl, waits=[tk_wp], inc=True)
            if s == 0:
                v_op(lambda e: e.memset(gmaxcol, -1.0e38))
            wt("sync", [GQ.tok()])

            yt_tok = [None, None]
            pe_after_tt = [None, None]

            def load_yt(tt, p2, guards):
                toks = []
                for hl in range(2):
                    srcap = d_yt[(s * 2 + hl) * C:(s * 2 + hl + 1) * C,
                                 tt * 512:(tt + 1) * 512] \
                        .rearrange("(ic p) j -> p ic j", p=128)
                    toks.append(ld_dma(ytb[p2][hl][:, :].rearrange("p (ic j) -> p ic j", ic=8),
                                       srcap, waits=guards))
                return toks

            def load_xc(tt, p2, guards):
                return ld_dma(xcb[p2][:, :].rearrange("p (ic j) -> p ic j", ic=8),
                              d_x[s * C:(s + 1) * C, tt * 512:(tt + 1) * 512]
                              .rearrange("(ic p) j -> p ic j", p=128), waits=guards)

            yt_tok[0] = load_yt(0, 0, [])
            xc_tok = [None, None]
            xc_free = [None, None]
            xc_tok[0] = load_xc(0, 0, [])

            for tt in range(NTT):
                par = tt % 2
                if tt + 1 < NTT:
                    gpar = (tt + 1) % 2
                    yt_tok[gpar] = load_yt(tt + 1, gpar,
                                           [pe_after_tt[gpar]] if pe_after_tt[gpar] else [])
                    xc_tok[gpar] = load_xc(tt + 1, gpar,
                                           [xc_free[gpar]] if xc_free[gpar] else [])
                ao_toks = []
                for co in range(8):
                    bank = pbanks[co % 4]
                    wt("tensor", [bank.rd, tk_wps] + yt_tok[par])
                    nmm = 0
                    for ic in range(8):
                        for (A, B) in ((wph, ytb[par][0]), (wph, ytb[par][1]),
                                       (wpl, ytb[par][0])):
                            last = nmm == 23
                            tkmm = em("tensor", lambda e, p=bank.t,
                                      a=A[:, ic * 1024 + co * 128: ic * 1024 + co * 128 + 128],
                                      b=B[:, ic * 512:(ic + 1) * 512],
                                      st=(nmm == 0), sp=last:
                                      e.matmul(p[:, :], a, b, start=st, stop=sp),
                                      inc=PE if last else None)
                            nmm += 1
                    bank.wr = tkmm
                    tk_ao = v_op(lambda e, o=aob[:, co * 512:(co + 1) * 512], p=bank.t,
                                 b=s_bp[:, co:co + 1]:
                                 e.tensor_scalar(o, p[:, :], b, None, ALU.add, ALU.bypass),
                                 waits=[bank.wr], inc=True)
                    bank.rd = tk_ao
                    ao_toks.append(tk_ao)
                chunks = [aob[:, ic * 512:(ic + 1) * 512] for ic in range(8)]
                tk_r = emit_ln(chunks, xsqc, m_sbc, r_sbc, negm2c,
                               pbanks[5], pbanks[6], ao_toks, r_free=(DV, DV.n))
                o1_toks = []
                for ic in range(8):
                    oc = o1b[:, ic * 512:(ic + 1) * 512]
                    v_op(lambda e, o=zt, a=chunks[ic], b=m_sbc:
                         e.tensor_tensor(o, a, b, ALU.subtract),
                         waits=[tk_r] if ic == 0 else [])
                    v_op(lambda e, o=zt, a=zt, b=r_sbc:
                         e.tensor_tensor(o, a, b, ALU.mult))
                    tko = v_op(lambda e, o=oc, a=zt, b=xcb[par][:, ic * 512:(ic + 1) * 512]:
                               e.tensor_tensor(o, a, b, ALU.add),
                               waits=[xc_tok[par]], inc=True)
                    o1_toks.append(tko)
                xc_free[par] = o1_toks[-1]
                tk_o1 = st_dma(d_o1[s * C:(s + 1) * C, tt * 512:(tt + 1) * 512]
                               .rearrange("(ic p) j -> p ic j", p=128),
                               o1b[:, :].rearrange("p (ic j) -> p ic j", ic=8),
                               waits=[o1_toks[-1]])
                o1_store_toks.append(tk_o1)
                chunks2 = [o1b[:, ic * 512:(ic + 1) * 512] for ic in range(8)]
                tk_r2 = emit_ln(chunks2, xsqc, m_sbc, r_sbc, negm2c,
                                pbanks[5], pbanks[6], o1_toks, r_free=(DV, DV.n))
                row = s * NTT + tt
                tk_mr = st_dma(d_mr[row:row + 1, 0:512], m_sbc[0:1, :], waits=[tk_r2])
                tk_mr2 = st_dma(d_mr[row:row + 1, 512:1024], r_sbc[0:1, :], waits=[tk_r2])
                mr_store_toks += [tk_mr, tk_mr2]
                for ic in range(8):
                    v_op(lambda e, o=zt, a=chunks2[ic], b=m_sbc:
                         e.tensor_tensor(o, a, b, ALU.subtract),
                         waits=[tk_r2] if ic == 0 else [])
                    v_op(lambda e, o=zt, a=zt, b=r_sbc:
                         e.tensor_tensor(o, a, b, ALU.mult))
                    v_op(lambda e, o=zt[:, 0:1], i=zt:
                         e.tensor_reduce(o, i, AXL.X, ALU.max), drain=True)
                    tk_m = v_op(lambda e, o=gmaxcol, a=gmaxcol, b=zt[:, 0:1]:
                                e.tensor_tensor(o, a, b, ALU.max),
                                inc=(ic == 7), drain=True)
                cmax_tok = tk_m
                pe_after_tt[par] = (PE, PE.n)
                wt("vector", [tk_o1])

        if LIMIT and LIMIT != "D":
            pool()
            if LIMIT == "C":
                st_dma(d_out[:, :], d_o1[:, :], waits=[(GQ, GQ.n), (DV, DV.n)])
            elif LIMIT == "B":
                st_dma(d_out[0:1024, :].bitcast(BF16)[:, 0:2048], d_yt[0:1024, :],
                       waits=[(GQ, GQ.n), (DV, DV.n)])
                st_dma(d_out[1024:2048, :].bitcast(BF16)[:, 0:2048], d_yt[2048:3072, :])
            elif LIMIT == "A":
                st_dma(d_out[0:1024, 0:512].bitcast(BF16), d_qs[0:1024, :],
                       waits=[(GQ, GQ.n), (DV, DV.n)])
                st_dma(d_out[0:1024, 512:1024].bitcast(BF16), d_vs[0:1024, 0:1024],
                       waits=[])
                st_dma(d_out[1024:2048, 0:512].bitcast(BF16), d_qs[2048:3072, :])
                st_dma(d_out[1024:2048, 512:1024].bitcast(BF16), d_vs[2048:3072, 0:1024])
            wt("gpsimd", [GQ.tok()])
            wt("sync", [LD.tok()])
            with nc.Block() as block:
                @block.sync
                def _(e):
                    for fn in prog.streams["sync"]:
                        fn(e)

                @block.vector
                def _(e):
                    for fn in prog.streams["vector"]:
                        fn(e)

                @block.scalar
                def _(e):
                    for fn in prog.streams["scalar"]:
                        fn(e)

                @block.tensor
                def _(e):
                    for fn in prog.streams["tensor"]:
                        fn(e)

                @block.gpsimd
                def _(e):
                    for fn in prog.streams["gpsimd"]:
                        fn(e)
            return nc
        def emit_barrier(gcol, cci, cco, dst_pair, prev_tok, ccs):
            if not hasattr(emit_barrier, "park"):
                emit_barrier.park = 0
                emit_barrier.park2 = 0
            """gcol [128,1] local max -> AllReduce max -> dst_pair [128, 2] (s, 1/s)."""
            ub = pbanks[6]
            wt("tensor", [prev_tok, ub.rd, tk_idf])
            ub.wr = em("tensor", lambda e, g=gcol: e.transpose(ub.t[0:1, 0:128], g, s_idf),
                       inc=PE)
            row = s_rows[0:1, 0:128]
            tk1 = v_op(lambda e, o=row[0:1, 0:1], i=ub.t[0:1, 0:128]:
                       e.tensor_reduce(o, i, AXL.X, ALU.max), waits=[ub.wr], inc=True, drain=True)
            ub.rd = tk1
            if HWDBG:
                pk = emit_barrier.park
                v_op(lambda e, o=s_park[0:1, pk:pk + 1], i=row[0:1, 0:1]:
                     e.tensor_copy(o, i))
                v_op(lambda e, o=s_park[0:1, 4 + pk * 2: 6 + pk * 2],
                     i=ub.t[0:1, 0:2]: e.tensor_copy(o, i))
                emit_barrier.park += 1
            tkc = v_op(lambda e, o=s_cc[0:1, :], i=row[0:1, 0:1].broadcast_to((1, 8)):
                       e.tensor_copy(o, i), inc=True, drain=True)
            if HWDBG:
                pk2 = emit_barrier.park2
                v_op(lambda e, o=s_park[0:1, 2 + pk2:3 + pk2], i=s_cc[0:1, 0:1]:
                     e.tensor_copy(o, i))
                emit_barrier.park2 += 1
            tkw = st_dma(cci.ap(), s_cc[0:1, :], waits=[tkc])
            wt("gpsimd", [tkw])
            tk_cc = em("gpsimd", lambda e: e.collective_compute(
                "AllReduce", ALU.max, replica_groups=[list(range(NCORES))],
                ins=[cci.ap()], outs=[cco.ap()]), inc=ccs)
            wt("gpsimd", [tk_cc])
            tk_rd = em("gpsimd", lambda e: e.dma_start(out=s_cc[0:1, :], in_=cco.ap()),
                       inc=GQ, inc_amt=16)
            r0 = s_rows[0:1, 0:1]
            r1 = s_rows[0:1, 1:2]
            r2 = s_rows[0:1, 2:4]
            v_op(lambda e: e.tensor_scalar(r0, s_cc[0:1, 0:1], EPS_Q, 1.0 / 255.0,
                                           ALU.max, ALU.mult), waits=[tk_rd], drain=True)
            v_op(lambda e: e.tensor_scalar(r1, s_cc[0:1, 0:1], EPS_Q, None,
                                           ALU.max, ALU.bypass), drain=True)
            v_op(lambda e: e.reciprocal(r1, r1), drain=True)
            v_op(lambda e: e.tensor_scalar(r1, r1, 255.0, None, ALU.mult, ALU.bypass),
                 drain=True)
            v_op(lambda e: e.tensor_copy(r2[0:1, 0:1], r0), drain=True)
            tk2 = v_op(lambda e: e.tensor_copy(r2[0:1, 1:2], r1), inc=True, drain=True)
            wt("tensor", [tk2, ub.rd])
            ub.wr = em("tensor", lambda e: e.matmul(ub.t[:, 0:2], s_ones[0:1, 0:128], r2,
                                                    start=True, stop=True), inc=PE)
            tk3 = v_op(lambda e, o=dst_pair: e.tensor_copy(o, ub.t[:, 0:2]),
                       waits=[ub.wr], inc=True)
            ub.rd = tk3
            return tk3

        tk_s1 = emit_barrier(s_gmax[:, 0:1], d_cci, d_cco, s_s1r, cmax_tok, CCS1)

        pool()
        w1qT = b16(NIC * FF)
        xq2 = b16(NIC * NSEQ * T)
        wld = [f32(1024), f32(1024)]
        wqi = b16(1024)
        o1l = [f32(NIC * 512) for _ in range(2)]
        m2b = f32(512); r2b = f32(512)
        mrl = [f32(1024), f32(1024)]
        zq = f32(512)
        hst = [f32(512) for _ in range(2)]
        hst_free = [(), ()]
        d_end = alloc_off[0]
        assert d_end <= AW, d_end

        wl_tok = [None, None]
        wl_free = [None, None]
        wl_tok[0] = ld_dma(wld[0], d_w1[0:128, :])
        for fc in range(NFC):
            par = fc % 2
            if fc + 1 < NFC:
                wl_tok[(fc + 1) % 2] = ld_dma(wld[(fc + 1) % 2],
                                              d_w1[(fc + 1) * 128:(fc + 2) * 128, :],
                                              waits=[wl_free[(fc + 1) % 2]]
                                              if wl_free[(fc + 1) % 2] else [])
            amax = s_rows[:, 0:1]
            v_op(lambda e, o=amax, i=wld[par]:
                 e.tensor_reduce(o, i, AXL.X, ALU.max, apply_absolute_value=True),
                 waits=[wl_tok[par], tk_b1], drain=True)
            v_op(lambda e, o=s_sw1[:, fc:fc + 1], i=amax:
                 e.tensor_scalar(o, i, EPS_Q, (1.0 / 127.0), ALU.max, ALU.mult),
                 drain=True)
            v_op(lambda e, o=s_rw1[:, fc:fc + 1], i=amax:
                 e.tensor_scalar(o, i, EPS_Q, None, ALU.max, ALU.bypass), drain=True)
            v_op(lambda e, o=s_rw1[:, fc:fc + 1], i=s_rw1[:, fc:fc + 1]:
                 e.reciprocal(o, i), drain=True)
            v_op(lambda e, o=wld[par], i=wld[par], r=s_rw1[:, fc:fc + 1]:
                 e.tensor_scalar(o, i, r, 127.0, ALU.mult, ALU.mult))
            v_op(lambda e, o=wld[par], i=wld[par]:
                 e.tensor_scalar(o, i, MAGIC, MAGIC, ALU.add, ALU.subtract))
            v_op(lambda e, o=wld[par], i=wld[par]:
                 e.tensor_scalar(o, i, 127.0, -128.0, ALU.min, ALU.max))
            tk_q = v_op(lambda e, o=wqi, i=wld[par]: e.tensor_copy(o, i),
                        waits=[(PE, PE.n)], inc=True)
            wl_free[par] = tk_q
            v_op(lambda e, o=s_sc1[:, fc:fc + 1], a=s_sw1[:, fc:fc + 1], b=s_s1r[:, 0:1]:
                 e.tensor_tensor(o, a, b, ALU.mult), waits=[tk_s1], drain=True)
            v_op(lambda e, o=s_rows[:, 1:2], i=s_sc1[:, fc:fc + 1]: e.reciprocal(o, i), drain=True)
            v_op(lambda e, o=s_rows[:, 2:3], a=s_b1[:, fc:fc + 1], b=s_rows[:, 1:2]:
                 e.tensor_tensor(o, a, b, ALU.mult), drain=True)
            v_op(lambda e, o=s_rows[:, 2:3], i=s_rows[:, 2:3]:
                 e.tensor_scalar(o, i, MAGIC, MAGIC, ALU.add, ALU.subtract), drain=True)
            v_op(lambda e, o=s_sb1[:, fc:fc + 1], a=s_rows[:, 2:3], b=s_sc1[:, fc:fc + 1]:
                 e.tensor_tensor(o, a, b, ALU.mult), drain=True)
            for g in range(2):
                wt("tensor", [tk_q, psbf.rd, tk_idb])
                for q4 in range(4):
                    ic = g * 4 + q4
                    tktr = em("tensor", lambda e, i=wqi[:, ic * 128:(ic + 1) * 128],
                              o=psbf.t[:, q4 * 128:(q4 + 1) * 128]:
                              e.transpose(o, i, s_idb), inc=PE if q4 == 3 else None)
                psbf.wr = tktr
                tk_e = None
                for q4 in range(4):
                    ic = g * 4 + q4
                    tk_e = v_op(lambda e, o=w1qT[:, ic * FF + fc * 128: ic * FF + fc * 128 + 128],
                                i=psbf.t[:, q4 * 128:(q4 + 1) * 128]:
                                e.tensor_copy(o, i),
                                waits=[psbf.wr] if q4 == 0 else [], inc=(q4 == 3))
                psbf.rd = tk_e
        w1q_tok = DV.tok()

        o1_tok = [None, None]
        o1_free = [None, None]

        def load_o1(si, tti, p2):
            return ld_dma(o1l[p2][:, :].rearrange("p (ic j) -> p ic j", ic=8),
                          d_o1[si * C:(si + 1) * C, tti * 512:(tti + 1) * 512]
                          .rearrange("(ic p) j -> p ic j", p=128),
                          waits=[o1_free[p2]] if o1_free[p2] else [])

        wt("sync", [GQ.tok()])
        o1_tok[0] = load_o1(0, 0, 0)
        tiles = [(si, tti) for si in range(NSEQ) for tti in range(NTT)]
        mr_tok = None
        for i, (si, tti) in enumerate(tiles):
            par = i % 2
            if i + 1 < len(tiles):
                o1_tok[(i + 1) % 2] = load_o1(*tiles[i + 1], (i + 1) % 2)
            rrow = si * NTT + tti
            mr_tok = ld_dma(mrl[par][0:1, :], d_mr[rrow:rrow + 1, :],
                            waits=[pbanks[6].wr] if i >= 2 else [])
            ub = pbanks[6]
            wt("tensor", [mr_tok, ub.rd])
            em("tensor", lambda e, m=mrl[par]: e.matmul(ub.t[:, :], s_ones[0:1, 0:128],
                                            m[0:1, 0:512], start=True, stop=True))
            ub2 = pbanks[5]
            wt("tensor", [mr_tok, ub2.rd])
            tk_bc = em("tensor", lambda e, m=mrl[par]: e.matmul(ub2.t[:, :], s_ones[0:1, 0:128],
                                                    m[0:1, 512:1024], start=True, stop=True),
                       inc=PE)
            ub.wr = tk_bc
            ub2.wr = tk_bc
            v_op(lambda e: e.tensor_copy(m2b, ub.t[:, :]), waits=[tk_bc])
            tk_m2 = v_op(lambda e: e.tensor_copy(r2b, ub2.t[:, :]), inc=True)
            ub.rd = tk_m2
            ub2.rd = tk_m2
            for ic in range(8):
                col = ic * (NSEQ * T) + si * T + tti * 512
                v_op(lambda e, o=zq, a=o1l[par][:, ic * 512:(ic + 1) * 512], b=m2b:
                     e.tensor_tensor(o, a, b, ALU.subtract),
                     waits=[o1_tok[par], tk_m2] if ic == 0 else [])
                v_op(lambda e, o=zq, a=zq, b=r2b:
                     e.tensor_tensor(o, a, b, ALU.mult))
                v_op(lambda e, o=zq, i=zq, r=s_s1r[:, 1:2]:
                     e.tensor_scalar(o, i, r, MAGIC, ALU.mult, ALU.add))
                v_op(lambda e, o=zq, i=zq:
                     e.tensor_scalar(o, i, MAGIC, 255.0, ALU.subtract, ALU.min))
                tk_xq = v_op(lambda e, o=xq2[:, col:col + 512], i=zq:
                             e.tensor_scalar(o, i, 0.0, None, ALU.max, ALU.bypass),
                             inc=(ic == 7))
            o1_free[par] = tk_xq
        xq2_tok = DV.tok()

        v_op(lambda e: e.memset(s_gmax[:, 1:2], 0.0))
        dmax_tok = None
        for tt in range(NSEQ * NTT):
            for fc in range(NFC):
                bank = pbanks[(tt * NFC + fc) % 4]
                wt("tensor", [bank.rd, xq2_tok, w1q_tok])
                for ic in range(8):
                    tkmm = em("tensor", lambda e, p=bank.t,
                              A=w1qT[:, ic * FF + fc * 128: ic * FF + fc * 128 + 128],
                              B=xq2[:, ic * (NSEQ * T) + tt * 512:
                                    ic * (NSEQ * T) + tt * 512 + 512],
                              st=(ic == 0), sp=(ic == 7):
                              e.matmul(p[:, :], A, B, start=st, stop=sp),
                              inc=PE if ic == 7 else None)
                bank.wr = tkmm
                par = (tt * NFC + fc) % 2
                tk_h = a_op(lambda e, o=hst[par], p=bank.t, sc=s_sc1[:, fc:fc + 1],
                            b=s_sb1[:, fc:fc + 1]:
                            e.activation(o, p[:, :], AF.Relu, bias=b, scale=sc),
                            waits=[bank.wr, w1q_tok, tk_s1] + list(hst_free[par]),
                            inc=True)
                bank.rd = tk_h
                v_op(lambda e, o=s_rows[:, 4:5], i=hst[par]:
                     e.tensor_reduce(o, i, AXL.X, ALU.max), waits=[tk_h], drain=True)
                dmax_tok = v_op(lambda e, o=s_gmax[:, 1:2], a=s_gmax[:, 1:2],
                                b=s_rows[:, 4:5]:
                                e.tensor_tensor(o, a, b, ALU.max), inc=True, drain=True)
                tk_st = st_dma(d_h[fc * 128:(fc + 1) * 128, tt * 512:(tt + 1) * 512],
                               hst[par], waits=[tk_h])
                hst_free[par] = (tk_st, dmax_tok)

        tk_s2 = emit_barrier(s_gmax[:, 1:2], d_cci2, d_cco2, s_s2r, dmax_tok, CCS2)

        if LIMIT == "D":
            pool()
            st_dma(d_out[:, :], d_h[0:2048, 0:2048], waits=[(GQ, GQ.n), (DV, DV.n)])
            _t = st_dma(d_out[0:1, 0:2].bitcast(F32), None, waits=[]) if False else None
            st_dma(d_out[2047:2048, 0:2], s_s1r[0:1, 0:2], waits=[tk_s2])
        pool()
        w2qT = b16(NFC * C)
        hq = b16(NFC * 512)
        w2ld = [f32(FF // 2), f32(FF // 2)]
        w2qi = b16(FF)
        o2b = f32(NIC * 512)
        o1e = [f32(NIC * 512) for _ in range(2)]
        hqe = [f32(512) for _ in range(2)]
        xsqe = [f32(512) for _ in range(2)]
        m_e = f32(512); r_e = f32(512); negm2e = f32(512)
        zte = f32(512)
        amax2 = f32(2)
        e_end = alloc_off[0]
        assert e_end <= AW, e_end

        w2_free = None
        for cc in range(NIC if LIMIT != "D" else 0):
            tk_l0 = ld_dma(w2ld[0], d_w2[cc * 128:(cc + 1) * 128, 0:2048],
                           waits=[w2_free] if w2_free else [])
            tk_l1 = ld_dma(w2ld[1], d_w2[cc * 128:(cc + 1) * 128, 2048:4096],
                           waits=[w2_free] if w2_free else [])
            v_op(lambda e, o=amax2[:, 0:1], i=w2ld[0]:
                 e.tensor_reduce(o, i, AXL.X, ALU.max, apply_absolute_value=True),
                 waits=[tk_l0, tk_b2, tk_s2], drain=True)
            v_op(lambda e, o=amax2[:, 1:2], i=w2ld[1]:
                 e.tensor_reduce(o, i, AXL.X, ALU.max, apply_absolute_value=True),
                 waits=[tk_l1], drain=True)
            amax = s_rows[:, 0:1]
            v_op(lambda e, o=amax, a=amax2[:, 0:1], b=amax2[:, 1:2]:
                 e.tensor_tensor(o, a, b, ALU.max), drain=True)
            v_op(lambda e, o=s_sw2[:, cc:cc + 1], i=amax:
                 e.tensor_scalar(o, i, EPS_Q, (1.0 / 127.0), ALU.max, ALU.mult),
                 drain=True)
            v_op(lambda e, o=s_rw2[:, cc:cc + 1], i=amax:
                 e.tensor_scalar(o, i, EPS_Q, None, ALU.max, ALU.bypass), drain=True)
            v_op(lambda e, o=s_rw2[:, cc:cc + 1], i=s_rw2[:, cc:cc + 1]:
                 e.reciprocal(o, i), drain=True)
            tk_q = None
            for hq2 in range(2):
                wbuf = w2ld[hq2]
                v_op(lambda e, o=wbuf, i=wbuf, r=s_rw2[:, cc:cc + 1]:
                     e.tensor_scalar(o, i, r, 127.0, ALU.mult, ALU.mult))
                v_op(lambda e, o=wbuf, i=wbuf:
                     e.tensor_scalar(o, i, MAGIC, MAGIC, ALU.add, ALU.subtract))
                v_op(lambda e, o=wbuf, i=wbuf:
                     e.tensor_scalar(o, i, 127.0, -128.0, ALU.min, ALU.max))
                tk_q = v_op(lambda e, o=w2qi[:, hq2 * 2048:(hq2 + 1) * 2048], i=wbuf:
                            e.tensor_copy(o, i), waits=[(PE, PE.n)] if hq2 == 0 else [],
                            inc=(hq2 == 1))
            w2_free = tk_q
            v_op(lambda e, o=s_sc2[:, cc:cc + 1], a=s_sw2[:, cc:cc + 1], b=s_s2r[:, 0:1]:
                 e.tensor_tensor(o, a, b, ALU.mult), drain=True)
            v_op(lambda e, o=s_rows[:, 1:2], i=s_sc2[:, cc:cc + 1]: e.reciprocal(o, i), drain=True)
            v_op(lambda e, o=s_rows[:, 2:3], a=s_b2[:, cc:cc + 1], b=s_rows[:, 1:2]:
                 e.tensor_tensor(o, a, b, ALU.mult), drain=True)
            v_op(lambda e, o=s_rows[:, 2:3], i=s_rows[:, 2:3]:
                 e.tensor_scalar(o, i, MAGIC, MAGIC, ALU.add, ALU.subtract), drain=True)
            v_op(lambda e, o=s_sb2[:, cc:cc + 1], a=s_rows[:, 2:3], b=s_sc2[:, cc:cc + 1]:
                 e.tensor_tensor(o, a, b, ALU.mult), drain=True)
            for g in range(8):
                wt("tensor", [tk_q, psbf.rd])
                for q4 in range(4):
                    fc = g * 4 + q4
                    tktr = em("tensor", lambda e, i=w2qi[:, fc * 128:(fc + 1) * 128],
                              o=psbf.t[:, q4 * 128:(q4 + 1) * 128]:
                              e.transpose(o, i, s_idb), inc=PE if q4 == 3 else None)
                psbf.wr = tktr
                tk_e = None
                for q4 in range(4):
                    fc = g * 4 + q4
                    tk_e = v_op(lambda e, o=w2qT[:, fc * C + cc * 128: fc * C + cc * 128 + 128],
                                i=psbf.t[:, q4 * 128:(q4 + 1) * 128]:
                                e.tensor_copy(o, i),
                                waits=[psbf.wr] if q4 == 0 else [], inc=(q4 == 3))
                psbf.rd = tk_e
        w2q_tok = DV.tok()

        if DEBUG:
            st_dma(d_dbgw[:, 0:16384], w2qT[:, 0:16384], waits=[w2q_tok])
            st_dma(d_dbgb[:, 0:8192], w2qT[:, 16384:24576], waits=[w2q_tok])
        h_tok = [None, None]
        h_free = [None, None]
        oe_tok = [None, None]
        oe_free = [None, None]

        def load_h(tt, fc, p2):
            return ld_dma(hqe[p2], d_h[fc * 128:(fc + 1) * 128, tt * 512:(tt + 1) * 512],
                          waits=[h_free[p2]] if h_free[p2] else [])

        def load_o1e(i, p2):
            si, tti = divmod(i, NTT)
            return ld_dma(o1e[p2][:, :].rearrange("p (ic j) -> p ic j", ic=8),
                          d_o1[si * C:(si + 1) * C, tti * 512:(tti + 1) * 512]
                          .rearrange("(ic p) j -> p ic j", p=128),
                          waits=[oe_free[p2]] if oe_free[p2] else [])

        wt("sync", [GQ.tok()])
        h_tok[0] = load_h(0, 0, 0)
        oe_tok[0] = load_o1e(0, 0)
        out_toks = []
        ffn2_pe = None
        for tt in range(NSEQ * NTT if LIMIT != "D" else 0):
            if ffn2_pe is not None:
                wt("vector", [ffn2_pe])
            for fc in range(NFC):
                par = fc % 2
                nxt = tt * NFC + fc + 1
                if nxt < NSEQ * NTT * NFC:
                    h_tok[nxt % 2] = load_h(nxt // NFC, nxt % NFC, nxt % 2)
                v_op(lambda e, o=zte, i=hqe[par], r=s_s2r[:, 1:2]:
                     e.tensor_scalar(o, i, r, MAGIC, ALU.mult, ALU.add),
                     waits=[h_tok[par], tk_s2])
                v_op(lambda e, o=zte, i=zte:
                     e.tensor_scalar(o, i, MAGIC, 255.0, ALU.subtract, ALU.min))
                tk_hq = v_op(lambda e, o=hq[:, fc * 512:(fc + 1) * 512], i=zte:
                             e.tensor_scalar(o, i, 0.0, None, ALU.max, ALU.bypass),
                             inc=True)
                h_free[par] = tk_hq
            hq_tok = tk_hq
            if tt + 1 < NSEQ * NTT:
                oe_tok[(tt + 1) % 2] = load_o1e(tt + 1, (tt + 1) % 2)
            par_o = tt % 2
            o2_toks = []
            for co in range(NIC):
                bank = pbanks[co % 4]
                wt("tensor", [bank.rd, hq_tok, w2q_tok])
                for fc in range(NFC):
                    tkmm = em("tensor", lambda e, p=bank.t,
                              A=w2qT[:, fc * C + co * 128: fc * C + co * 128 + 128],
                              B=hq[:, fc * 512:(fc + 1) * 512],
                              st=(fc == 0), sp=(fc == NFC - 1):
                              e.matmul(p[:, :], A, B, start=st, stop=sp),
                              inc=PE if fc == NFC - 1 else None)
                bank.wr = tkmm
                tk_o2 = v_op(lambda e, o=o2b[:, co * 512:(co + 1) * 512], p=bank.t,
                             sc=s_sc2[:, co:co + 1], b=s_sb2[:, co:co + 1]:
                             e.tensor_scalar(o, p[:, :], sc, b, ALU.mult, ALU.add),
                             waits=[bank.wr], inc=True)
                bank.rd = tk_o2
                o2_toks.append(tk_o2)
            if DEBUG and tt == 0:
                _dt1 = st_dma(d_dbg[:, 0:4096], o2b, waits=[o2_toks[-1]])
                _dt2 = st_dma(d_dbgb2[:, 0:16384], hq, waits=[hq_tok])
                wt("vector", [_dt1, _dt2])
            chunks = [o2b[:, ic * 512:(ic + 1) * 512] for ic in range(8)]
            tk_r = emit_ln(chunks, xsqe, m_e, r_e, negm2e,
                           pbanks[5], pbanks[6], o2_toks, r_free=(DV, DV.n))
            for ic in range(8):
                oc = o2b[:, ic * 512:(ic + 1) * 512]
                v_op(lambda e, o=zte, a=chunks[ic], b=m_e:
                     e.tensor_tensor(o, a, b, ALU.subtract),
                     waits=[tk_r, oe_tok[par_o]] if ic == 0 else [])
                v_op(lambda e, o=zte, a=zte, b=r_e:
                     e.tensor_tensor(o, a, b, ALU.mult))
                tk_f = v_op(lambda e, o=oc, a=zte, b=o1e[par_o][:, ic * 512:(ic + 1) * 512]:
                            e.tensor_tensor(o, a, b, ALU.add), inc=(ic == 7))
            oe_free[par_o] = tk_f
            si, tti = divmod(tt, NTT)
            tk_out = st_dma(d_out[si * C:(si + 1) * C, tti * 512:(tti + 1) * 512]
                            .rearrange("(ic p) j -> p ic j", p=128),
                            o2b[:, :].rearrange("p (ic j) -> p ic j", ic=8),
                            waits=[tk_f])
            out_toks.append(tk_out)
            wt("vector", [tk_out])
            ffn2_pe = (PE, PE.n)

        if HWDBG:
            wt("gpsimd", [GQ.tok(), (DV, DV.n), (PE, PE.n), (AC, AC.n)])
            st_dma(dbg_o1, d_o1[:, :])
            st_dma(dbg_y, d_yt[:, :])
            st_dma(dbg_h, d_h[:, :])
            st_dma(dbg_q, d_qs[:, :])
            st_dma(dbg_s[0:1, 0:2], s_s1r[0:1, 0:2])
            st_dma(dbg_s[0:1, 2:4], s_s2r[0:1, 0:2])
            st_dma(dbg_s[0:1, 4:6].bitcast(F32), d_cco[0:1, 0:2])
            st_dma(dbg_s[0:1, 6:8].bitcast(F32), d_cci[0:1, 0:2])
            st_dma(dbg_g[:, 0:2], s_gmax[:, 0:2])
            st_dma(dbg_g[0:1, 2:10], s_park[0:1, 0:8])
        wt("gpsimd", [GQ.tok()])
        wt("sync", [LD.tok()])

        with nc.Block() as block:
            @block.sync
            def _(e):
                for fn in prog.streams["sync"]:
                    fn(e)

            @block.vector
            def _(e):
                for fn in prog.streams["vector"]:
                    fn(e)

            @block.scalar
            def _(e):
                for fn in prog.streams["scalar"]:
                    fn(e)

            @block.tensor
            def _(e):
                for fn in prog.streams["tensor"]:
                    fn(e)

            @block.gpsimd
            def _(e):
                for fn in prog.streams["gpsimd"]:
                    fn(e)

    return nc



def make_inputs(core, inp):
    x = np.asarray(inp["L0_attn_x"], dtype=np.float32)
    wq = np.asarray(inp["L0_attn_Wq"], dtype=np.float32)
    wk = np.asarray(inp["L0_attn_Wk"], dtype=np.float32)
    wv = np.asarray(inp["L0_attn_Wv"], dtype=np.float32)
    wp = np.asarray(inp["L0_attn_Wp"], dtype=np.float32)

    def wlay(w):
        return np.ascontiguousarray(
            w.reshape(NIC, 128, C).transpose(1, 0, 2).reshape(128, NIC * C))

    def blay(b, nch):
        return np.ascontiguousarray(np.asarray(b, dtype=np.float32)
                                    .reshape(nch, 128).T)

    cosTM, sswTM = _rotary_tables()
    wt = _wt_tables(inp["L0_attn_rel_bias_table"])
    brows = np.concatenate([np.asarray(inp["L0_attn_bq"], dtype=np.float32),
                            np.asarray(inp["L0_attn_bk"], dtype=np.float32),
                            np.asarray(inp["L0_attn_bv"], dtype=np.float32)])[None, :]
    return {
        "x_fm": np.ascontiguousarray(
            x[core * NSEQ:(core + 1) * NSEQ].transpose(0, 2, 1).reshape(NSEQ * C, T)),
        "wq": wlay(wq), "wk": wlay(wk), "wv": wlay(wv), "wp": wlay(wp),
        "brows": np.ascontiguousarray(brows),
        "bp_c": blay(inp["L0_attn_bp"], NIC),
        "w1": np.ascontiguousarray(np.asarray(inp["L0_ffn_w1"], dtype=np.float32)),
        "b1_c": blay(inp["L0_ffn_b1"], NFC),
        "w2": np.ascontiguousarray(np.asarray(inp["L0_ffn_w2"], dtype=np.float32)),
        "b2_c": blay(inp["L0_ffn_b2"], NIC),
        "wt": wt.reshape(H * 128, 2688),
        "cosTM": cosTM, "sswTM": sswTM,
        "identb": np.eye(128, dtype=ml_dtypes.bfloat16),
        "identf": np.eye(128, dtype=np.float32),
        "epscol": np.full((128, 1), EPS_LN, dtype=np.float32),
    }


def kernel(**inputs):
    nc = build_nc()
    in_maps = [make_inputs(c, inputs) for c in range(NCORES)]
    res = run_bass_kernel_spmd(nc, in_maps, core_ids=list(range(NCORES)))
    outs = [np.asarray(r["out_fm"]).reshape(NSEQ, C, T).transpose(0, 2, 1)
            for r in res.results]
    return np.ascontiguousarray(np.concatenate(outs, axis=0), dtype=np.float32)


# version 04
# speedup vs baseline: 1.2179x; 1.2179x over previous
import math
from contextlib import ExitStack

import numpy as np
import ml_dtypes

import concourse.bass as bass
import concourse.mybir as mybir
from concourse.bass_utils import run_bass_kernel_spmd

F32 = mybir.dt.float32
BF16 = mybir.dt.bfloat16
AF = mybir.ActivationFunctionType
ALU = mybir.AluOpType
AXL = mybir.AxisListType
MAGIC = 12582912.0
NEG = -1.0e38

DEBUG = False
LIMIT = ""
HWDBG = False
LAYOUT = {}
NCORES = 8
NSEQ = 2
T = 2048
C = 1024
H = 16
D = 64
FF = 4096
NTT = 4
TTOK = 512
NIC = 8
NFC = 32
EPS_LN = 1e-5
EPS_Q = 1e-8



def _rotary_tables():
    inv = 1.0 / (10000.0 ** (np.arange(0, D, 2, dtype=np.float64) / D))
    t = np.arange(T, dtype=np.float64)
    fr = np.einsum("i,j->ij", t, inv)
    fr = np.concatenate([fr, fr], axis=1)
    cos = np.cos(fr).astype(np.float32)
    sin = np.sin(fr).astype(np.float32)
    ssw = sin.copy()
    ssw[:, 32:] = -ssw[:, 32:]
    cosTM = cos.reshape(16, 128, 64).transpose(1, 0, 2).reshape(128, 16 * 64).copy()
    sswTM = ssw.reshape(16, 128, 64).transpose(1, 0, 2).reshape(128, 16 * 64).copy()
    return cosTM, sswTM


def _bucket(n):
    n = np.asarray(n, dtype=np.int64)
    max_exact = 16
    out = np.where(n < max_exact, n, 0).astype(np.int64)
    big = n >= max_exact
    nsafe = np.maximum(n, 1).astype(np.float64)
    val = max_exact + (np.log(nsafe / max_exact) / math.log(128.0 / max_exact) * 16.0)
    val = val.astype(np.int32)
    val = np.minimum(val, 31)
    out = np.where(big, val, out)
    return out


def _wt_tables(rel_bias_table):
    table = np.asarray(rel_bias_table, dtype=np.float32)
    n = np.arange(-2047, 2048, dtype=np.int64)
    g = np.empty((H, n.size), dtype=np.float32)
    pos = n >= 0
    buck = _bucket(np.maximum(n, 0))
    for h in range(H):
        g[h] = np.where(pos, table[buck, h], np.float32(NEG))
    M = 2688
    p = np.arange(128)[:, None]
    m = np.arange(M)[None, :]
    idx = m - p - 511 + 2047
    idx = np.clip(idx, 0, n.size - 1)
    wt = g[:, idx]
    return np.ascontiguousarray(wt)



class Sem:
    def __init__(self, handle):
        self.h = handle
        self.n = 0

    def inc(self, amount=1):
        self.n += amount
        return self.n

    def tok(self):
        return (self, self.n)


class Prog:

    ENGS = ("sync", "vector", "scalar", "tensor", "gpsimd")

    def __init__(self):
        self.streams = {e: [] for e in self.ENGS}

    def wait(self, eng, tok):
        if tok is None:
            return
        sem, val = tok
        if val <= 0:
            return
        self.streams[eng].append(lambda e, s=sem.h, v=val: e.wait_ge(s, v))

    def waits(self, eng, toks):
        best = {}
        for t in toks:
            if t is None:
                continue
            s, v = t
            if v > 0 and (id(s) not in best or best[id(s)][1] < v):
                best[id(s)] = (s, v)
        for s, v in best.values():
            self.wait(eng, (s, v))

    def emit(self, eng, fn, inc=None, inc_amt=1):
        if inc is None:
            self.streams[eng].append(fn)
            return None
        sem = inc
        val = sem.inc(inc_amt)

        def run(e, fn=fn, s=sem.h, a=inc_amt):
            inst = fn(e)
            inst.then_inc(s, a)
        self.streams[eng].append(run)
        return (sem, val)


class PsumBank:
    def __init__(self, tile):
        self.t = tile
        self.wr = None
        self.rd = None



def build_nc():
    nc = bass.Bass("TRN2", target_bir_lowering=False, detect_race_conditions=False)

    d_x = nc.dram_tensor("x_fm", [NSEQ * C, T], F32, kind="ExternalInput").ap()
    d_wq = nc.dram_tensor("wq", [128, NIC * C], F32, kind="ExternalInput").ap()
    d_wk = nc.dram_tensor("wk", [128, NIC * C], F32, kind="ExternalInput").ap()
    d_wv = nc.dram_tensor("wv", [128, NIC * C], F32, kind="ExternalInput").ap()
    d_wp = nc.dram_tensor("wp", [128, NIC * C], F32, kind="ExternalInput").ap()
    d_brows = nc.dram_tensor("brows", [1, 3 * C], F32, kind="ExternalInput").ap()
    d_bp = nc.dram_tensor("bp_c", [128, NIC], F32, kind="ExternalInput").ap()
    d_w1 = nc.dram_tensor("w1", [FF, C], F32, kind="ExternalInput").ap()
    d_b1 = nc.dram_tensor("b1_c", [128, NFC], F32, kind="ExternalInput").ap()
    d_w2 = nc.dram_tensor("w2", [C, FF], F32, kind="ExternalInput").ap()
    d_b2 = nc.dram_tensor("b2_c", [128, NIC], F32, kind="ExternalInput").ap()
    d_wt = nc.dram_tensor("wt", [H * 128, 2688], F32, kind="ExternalInput").ap()
    d_cos = nc.dram_tensor("cosTM", [128, 16 * 64], F32, kind="ExternalInput").ap()
    d_ssw = nc.dram_tensor("sswTM", [128, 16 * 64], F32, kind="ExternalInput").ap()
    d_idb = nc.dram_tensor("identb", [128, 128], BF16, kind="ExternalInput").ap()
    d_idf = nc.dram_tensor("identf", [128, 128], F32, kind="ExternalInput").ap()
    d_eps = nc.dram_tensor("epscol", [128, 1], F32, kind="ExternalInput").ap()

    d_out = nc.dram_tensor("out_fm", [NSEQ * C, T], F32, kind="ExternalOutput").ap()
    if HWDBG:
        dbg_o1 = nc.dram_tensor("dbg_o1", [NSEQ * C, T], F32, kind="ExternalOutput").ap()
        dbg_y = nc.dram_tensor("dbg_y", [NSEQ * 2 * C, T], BF16, kind="ExternalOutput").ap()
        dbg_h = nc.dram_tensor("dbg_h", [FF, NSEQ * T], F32, kind="ExternalOutput").ap()
        dbg_s = nc.dram_tensor("dbg_s", [1, 8], F32, kind="ExternalOutput").ap()
        dbg_g = nc.dram_tensor("dbg_g", [128, 16], F32, kind="ExternalOutput").ap()
        dbg_q = nc.dram_tensor("dbg_q", [NSEQ * 2 * T, C], BF16, kind="ExternalOutput").ap()

    d_wsp = nc.dram_tensor("wsplit", [4 * 2 * 128, NIC * C], BF16)
    d_qs = nc.dram_tensor("q_tm", [NSEQ * 2 * T, C], BF16)
    d_ks = nc.dram_tensor("k_tm", [NSEQ * 2 * T, C], BF16)
    d_vs = nc.dram_tensor("v_tm", [NSEQ * T, H * 65], F32)
    d_yt = nc.dram_tensor("y_fm", [NSEQ * 2 * C, T], BF16)
    d_o1 = nc.dram_tensor("out1_fm", [NSEQ * C, T], F32)
    d_mr = nc.dram_tensor("m2r2", [NSEQ * NTT, 2 * TTOK], F32)
    d_h = nc.dram_tensor("h_fm", [FF, NSEQ * T], F32)
    d_cci = nc.dram_tensor("cc_in", [1, 8], F32)
    d_cco = nc.dram_tensor("cc_out", [1, 8], F32)
    d_cci2 = nc.dram_tensor("cc_in2", [1, 8], F32)
    d_cco2 = nc.dram_tensor("cc_out2", [1, 8], F32)
    if DEBUG:
        d_dbg = nc.dram_tensor("dbg", [128, 8192], F32)
        d_dbgb = nc.dram_tensor("dbgb", [128, 8192], BF16)
        d_dbgw = nc.dram_tensor("dbgw", [128, 16384], BF16)
        d_dbgb2 = nc.dram_tensor("dbgb2", [128, 16384], BF16)

    prog = Prog()

    with ExitStack() as ctx:
        AW = 52992
        arena = ctx.enter_context(nc.sbuf_tensor("arena", [128, AW], F32))

        alloc_off = [0]

        def alloc32(n, align=16):
            o = (alloc_off[0] + align - 1) // align * align
            alloc_off[0] = o + n
            assert alloc_off[0] <= AW, f"arena overflow: {alloc_off[0]} > {AW}"
            return o

        def f32(n, align=16):
            o = alloc32(n, align)
            return arena[:, o:o + n]

        def b16(n, align=16):
            o = alloc32((n + 1) // 2, align)
            return arena[:, o:o + (n + 1) // 2].bitcast(BF16)

        s_ones = f32(128)
        s_i1024 = f32(128)
        s_idf = f32(128)
        s_idb = b16(128)
        s_eps = f32(1)
        s_bp = f32(NIC)
        s_b1 = f32(NFC)
        s_b2 = f32(NIC)
        s_sw1 = f32(NFC)
        s_rw1 = f32(NFC)
        s_sw2 = f32(NIC)
        s_rw2 = f32(NIC)
        s_sc1 = f32(NFC)
        s_sb1 = f32(NFC)
        s_sc2 = f32(NIC)
        s_sb2 = f32(NIC)
        s_gmax = f32(2)
        s_s1r = f32(2)
        s_s2r = f32(2)
        s_rows = f32(2048)
        s_park = f32(16)
        s_cc = f32(8)

        PERSIST_END = alloc_off[0]

        pbanks = [PsumBank(ctx.enter_context(nc.psum_tensor(f"ps{i}", [128, 512], F32)))
                  for i in range(7)]
        psbf_t = ctx.enter_context(nc.psum_tensor("psbf", [128, 1024], BF16))
        psbf = PsumBank(psbf_t)

        LD = Sem(ctx.enter_context(nc.semaphore("ld")))
        GQ = Sem(ctx.enter_context(nc.semaphore("gq")))
        PE = Sem(ctx.enter_context(nc.semaphore("pe")))
        DV = Sem(ctx.enter_context(nc.semaphore("dv")))
        AC = Sem(ctx.enter_context(nc.semaphore("ac")))
        CCS1 = Sem(ctx.enter_context(nc.semaphore("ccs1")))
        CCS2 = Sem(ctx.enter_context(nc.semaphore("ccs2")))

        em = prog.emit
        wt = prog.waits

        def ld_dma(dst, src, waits=()):
            wt("sync", waits)
            return em("sync", lambda e, d=dst, s=src: e.dma_start(out=d, in_=s),
                      inc=LD, inc_amt=16)

        def ld_dma_tr(dst, src, waits=()):
            wt("sync", waits)
            return em("sync", lambda e, d=dst, s=src: e.dma_start(out=d, in_=s, transpose=True),
                      inc=LD, inc_amt=16)

        def st_dma(dst, src, waits=()):
            wt("gpsimd", waits)
            return em("gpsimd", lambda e, d=dst, s=src: e.dma_start(out=d, in_=s),
                      inc=GQ, inc_amt=16)

        def v_op(fn, waits=(), inc=False, drain=False):
            wt("vector", waits)
            if drain:
                tok = em("vector", fn, inc=DV)
                prog.wait("vector", tok)
                return tok
            return em("vector", fn, inc=DV if inc else None)

        def a_op(fn, waits=(), inc=False):
            wt("scalar", waits)
            return em("scalar", fn, inc=AC if inc else None)

        def split_hilo(src32, dst_hi, dst_lo, waits=(), inc=True):
            v_op(lambda e, a=dst_hi, b=src32: e.tensor_copy(a, b), waits=waits)
            return v_op(lambda e, o=dst_lo, a=src32, b=dst_hi:
                        e.tensor_tensor(o, a, b, ALU.subtract), inc=inc)

        tk_idf = ld_dma(s_idf, d_idf)
        tk_idb = ld_dma(s_idb, d_idb)
        tk_eps = ld_dma(s_eps, d_eps)
        tk_bp = ld_dma(s_bp, d_bp)
        tk_b1 = ld_dma(s_b1, d_b1)
        tk_b2 = ld_dma(s_b2, d_b2)
        v_op(lambda e: e.memset(s_ones, 1.0))
        v_op(lambda e: e.memset(s_i1024, 1.0 / 1024.0))

        def pool():
            toks = [(PE, PE.n), (DV, DV.n), (AC, AC.n), (GQ, GQ.n), (LD, LD.n)]
            for eng in ("sync", "vector", "scalar", "tensor", "gpsimd"):
                wt(eng, toks)
            alloc_off[0] = PERSIST_END
            return alloc_off[0]

        pool()
        wtmp = [f32(NIC * C) for _ in range(2)]
        whi_st = b16(NIC * C)
        wlo_st = b16(NIC * C)
        wload = [None, None]
        wins = [d_wq, d_wk, d_wv, d_wp]
        prev_done = None
        prev_stores = []
        for wi in range(4):
            par = wi % 2
            if wi == 0:
                wload[0] = ld_dma(wtmp[0], wins[0])
            if wi + 1 < 4:
                wload[(wi + 1) % 2] = ld_dma(wtmp[(wi + 1) % 2], wins[wi + 1],
                                             waits=[] if wi == 0 else [prev_done])
            tk = split_hilo(wtmp[par], whi_st, wlo_st,
                            waits=[wload[par]] + prev_stores)
            t1 = st_dma(d_wsp[(wi * 2 + 0) * 128:(wi * 2 + 1) * 128, :], whi_st, waits=[tk])
            t2 = st_dma(d_wsp[(wi * 2 + 1) * 128:(wi * 2 + 2) * 128, :], wlo_st, waits=[tk])
            prev_done = tk
            prev_stores = [t1, t2]
        wsp_tok = prev_stores[1]

        def emit_ln(chunks, xsq, m_sb, r_sb, negm2, st_b1, st_b2, src_toks,
                    r_free=None):
            pe_sq = [(PE, PE.n), (PE, PE.n)]
            for ic in range(8):
                par = ic % 2
                wl = [src_toks[ic], pe_sq[par]]
                tk = a_op(lambda e, o=xsq[par], i=chunks[ic]:
                          e.activation(o, i, AF.Square), waits=wl, inc=True)
                wt("tensor", [st_b1.rd if ic == 0 else None, src_toks[ic]])
                em("tensor", lambda e, p=st_b1.t, r=chunks[ic], s=(ic == 0), t=(ic == 7):
                   e.matmul(p[0:1, :], s_ones[:, 0:1], r, start=s, stop=t))
                wt("tensor", [st_b2.rd if ic == 0 else None, tk])
                tk2 = em("tensor", lambda e, p=st_b2.t, r=xsq[par], s=(ic == 0), t=(ic == 7):
                         e.matmul(p[0:1, :], s_ones[:, 0:1], r, start=s, stop=t),
                         inc=PE)
                pe_sq[par] = tk2
            st_b1.wr = tk2
            st_b2.wr = tk2
            r1 = s_rows[0:1, 0:512]
            r2 = s_rows[0:1, 512:1024]
            v_op(lambda e: e.tensor_copy(r1, st_b1.t[0:1, :]), waits=[st_b1.wr])
            tk_rows = v_op(lambda e: e.tensor_copy(r2, st_b2.t[0:1, :]), inc=True)
            st_b1.rd = tk_rows
            st_b2.rd = tk_rows
            wt("tensor", [tk_rows, st_b1.rd])
            em("tensor", lambda e: e.matmul(st_b1.t[:, :], s_i1024[0:1, 0:128], r1,
                                            start=True, stop=True))
            tkb = em("tensor", lambda e: e.matmul(st_b2.t[:, :], s_i1024[0:1, 0:128], r2,
                                                  start=True, stop=True), inc=PE)
            st_b1.wr = tkb
            st_b2.wr = tkb
            v_op(lambda e: e.tensor_copy(m_sb, st_b1.t[:, :]), waits=[tkb])
            v_op(lambda e: e.scalar_tensor_tensor(negm2, st_b1.t[:, :], -1.0, m_sb,
                                                  ALU.mult, ALU.mult))
            tk_var = v_op(lambda e: e.tensor_tensor(negm2, st_b2.t[:, :], negm2, ALU.add),
                          inc=True)
            st_b1.rd = tk_var
            st_b2.rd = tk_var
            a_op(lambda e: e.activation(r_sb, negm2, AF.Ln, bias=s_eps),
                 waits=[tk_var, r_free])
            tk_r = a_op(lambda e: e.activation(r_sb, r_sb, AF.Exp, scale=-0.5), inc=True)
            return tk_r

        yt_store_toks = []
        o1_store_toks = []
        mr_store_toks = []
        cmax_tok = None

        for s in range(NSEQ if LIMIT in ("", "D") else (0 if LIMIT == "W" else 1)):
            pool()
            s_cos = f32(1024)
            s_ssw = f32(1024)
            s_brow = f32(3 * C)
            xb = [f32(NIC * 512) for _ in range(2)]
            xsq = [f32(512) for _ in range(2)]
            m_sb = f32(512); r_sb = f32(512); negm2 = f32(512)
            xlnh = b16(NIC * 512); xlnl = b16(NIC * 512)
            t1b = f32(1024); ub = f32(1024); qp32 = f32(1024)
            sth = b16(1024); stl = b16(1024)
            vstage = f32(H * 65)
            wqh = b16(NIC * C); wql = b16(NIC * C)
            wkh = b16(NIC * C); wkl = b16(NIC * C)
            wvh = b16(NIC * C); wvl = b16(NIC * C)
            a_end = alloc_off[0]

            wspw = [wsp_tok] if s == 0 else []
            def wsp_rows(wi, hl):
                r0 = (wi * 2 + hl) * 128
                return d_wsp[r0:r0 + 128, :]
            tk_wq = [ld_dma(wqh, wsp_rows(0, 0), waits=wspw),
                     ld_dma(wql, wsp_rows(0, 1))]
            tk_wk = [ld_dma(wkh, wsp_rows(1, 0)),
                     ld_dma(wkl, wsp_rows(1, 1))]
            tk_wv = [ld_dma(wvh, wsp_rows(2, 0)),
                     ld_dma(wvl, wsp_rows(2, 1))]
            tk_cos = ld_dma(s_cos, d_cos)
            tk_ssw = ld_dma(s_ssw, d_ssw)
            tk_brow = ld_dma(s_brow[0:1, :], d_brows)

            v_op(lambda e: e.memset(vstage, 1.0))

            x_tok = [None, None]
            x_free = [None, None]

            def load_x(tt, dst_par):
                sap = d_x[s * C:(s + 1) * C, tt * 512:(tt + 1) * 512] \
                    .rearrange("(ic p) j -> p ic j", p=128)
                return ld_dma(xb[dst_par][:, :].rearrange("p (ic j) -> p ic j", ic=8),
                              sap, waits=[x_free[dst_par]] if x_free[dst_par] else [])

            x_tok[0] = load_x(0, 0)

            for tt in range(NTT):
                par = tt % 2
                if tt + 1 < NTT:
                    x_tok[(tt + 1) % 2] = load_x(tt + 1, (tt + 1) % 2)
                chunks = [xb[par][:, ic * 512:(ic + 1) * 512] for ic in range(8)]
                src_toks = [x_tok[par]] * 8
                tk_r = emit_ln(chunks, xsq, m_sb, r_sb, negm2,
                               pbanks[5], pbanks[6], src_toks,
                               r_free=(DV, DV.n))
                if DEBUG and s == 0 and tt == 0:
                    dtk = DV.tok()
                    atk = AC.tok()
                    st_dma(d_dbg[:, 0:4096], xb[par], waits=[x_tok[par]])
                    st_dma(d_dbg[:, 4096:4608], m_sb, waits=[dtk, atk])
                    st_dma(d_dbg[:, 4608:5120], r_sb, waits=[dtk, atk])
                    st_dma(d_dbg[:, 5120:5632], negm2, waits=[dtk, atk])
                if tt > 0:
                    wt("vector", [(PE, PE.n)])
                last_split = None
                for ic in range(8):
                    xc = chunks[ic]
                    xh = xlnh[:, ic * 512:(ic + 1) * 512]
                    xl = xlnl[:, ic * 512:(ic + 1) * 512]
                    v_op(lambda e, o=qp32[:, 0:512], a=xc, b=m_sb:
                         e.tensor_tensor(o, a, b, ALU.subtract),
                         waits=[tk_r] if ic == 0 else [])
                    v_op(lambda e, o=qp32[:, 0:512], a=qp32[:, 0:512], b=r_sb:
                         e.tensor_tensor(o, a, b, ALU.mult))
                    last_split = split_hilo(qp32[:, 0:512], xh, xl,
                                            inc=(ic == 7))
                x_free[par] = last_split
                xln_tok = last_split

                if DEBUG and s == 0 and tt == 0:
                    st_dma(d_dbgb[:, 0:4096], xlnh, waits=[xln_tok])
                    st_dma(d_dbgb[:, 4096:8192], xlnl, waits=[xln_tok])
                    st_dma(d_dbgw[:, 0:8192], wvh, waits=tk_wv)
                    st_dma(d_dbgw[:, 8192:16384], wvl, waits=tk_wv)
                for tc in range(4):
                    tcg = tt * 4 + tc
                    for pi, (wh, wl, wtk, brow_off, is_v) in enumerate([
                            (wqh, wql, tk_wq, 0, False),
                            (wkh, wkl, tk_wk, C, False),
                            (wvh, wvl, tk_wv, 2 * C, True)]):
                        for fh in range(2):
                            bank = pbanks[((tc * 3 + pi) * 2 + fh) % 4]
                            wt("tensor", [bank.rd, xln_tok] + wtk + [tk_brow])
                            nmm = 0
                            for ic in range(8):
                                lh = xlnh[:, ic * 512 + tc * 128: ic * 512 + tc * 128 + 128]
                                ll = xlnl[:, ic * 512 + tc * 128: ic * 512 + tc * 128 + 128]
                                rh = wh[:, ic * 1024 + fh * 512: ic * 1024 + fh * 512 + 512]
                                rl = wl[:, ic * 1024 + fh * 512: ic * 1024 + fh * 512 + 512]
                                for (a_, b_) in ((lh, rh), (lh, rl), (ll, rh)):
                                    em("tensor", lambda e, p=bank.t, A=a_, B=b_, st=(nmm == 0):
                                       e.matmul(p[:, :], A, B, start=st, stop=False))
                                    nmm += 1
                            brow = s_brow[0:1, brow_off + fh * 512: brow_off + fh * 512 + 512]
                            bank.wr = em("tensor", lambda e, p=bank.t, r=brow:
                                         e.matmul(p[:, :], s_ones[0:1, 0:128], r,
                                                  start=False, stop=True), inc=PE)
                            if not is_v:
                                cosv = s_cos[:, tcg * 64:(tcg + 1) * 64] \
                                    .unsqueeze(1).broadcast_to((128, 8, 64))
                                sswv = s_ssw[:, tcg * 64:(tcg + 1) * 64] \
                                    .unsqueeze(1).broadcast_to((128, 8, 64))
                                t1v = t1b[:, fh * 512:(fh + 1) * 512] \
                                    .rearrange("p (h m) -> p h m", h=8)
                                uv = ub[:, fh * 512:(fh + 1) * 512] \
                                    .rearrange("p (h m) -> p h m", h=8)
                                pv = bank.t[:, :].rearrange("p (h m) -> p h m", h=8)
                                v_op(lambda e, o=t1v, a=pv, b=cosv:
                                     e.tensor_tensor(o, a, b, ALU.mult),
                                     waits=[bank.wr, tk_cos, tk_ssw])
                                bank.rd = v_op(lambda e, o=uv, a=pv, b=sswv:
                                               e.tensor_tensor(o, a, b, ALU.mult), inc=True)
                                t4 = t1b[:, fh * 512:(fh + 1) * 512] \
                                    .rearrange("p (h z m) -> p h z m", h=8, z=2)
                                u4 = ub[:, fh * 512:(fh + 1) * 512] \
                                    .rearrange("p (h z m) -> p h z m", h=8, z=2)
                                q4 = qp32[:, fh * 512:(fh + 1) * 512] \
                                    .rearrange("p (h z m) -> p h z m", h=8, z=2)
                                v_op(lambda e, o=q4[:, :, 0, :], a=t4[:, :, 0, :],
                                     b=u4[:, :, 1, :]: e.tensor_tensor(o, a, b, ALU.add))
                                v_op(lambda e, o=q4[:, :, 1, :], a=t4[:, :, 1, :],
                                     b=u4[:, :, 0, :]: e.tensor_tensor(o, a, b, ALU.add))
                            else:
                                dstv = vstage[:, fh * 520: fh * 520 + 520] \
                                    .rearrange("p (h m) -> p h m", h=8)[:, :, 0:64]
                                bank.rd = v_op(lambda e, o=dstv,
                                               a=bank.t[:, :].rearrange("p (h m) -> p h m", h=8):
                                               e.tensor_copy(o, a), waits=[bank.wr], inc=True)
                        if not is_v:
                            tkq = split_hilo(qp32, sth, stl, inc=True)
                            dram = d_qs if pi == 0 else d_ks
                            row0 = tt * 512 + tc * 128
                            tk1 = st_dma(dram[(s * 2 + 0) * T + row0:
                                               (s * 2 + 0) * T + row0 + 128, :],
                                         sth, waits=[tkq])
                            tk2 = st_dma(dram[(s * 2 + 1) * T + row0:
                                               (s * 2 + 1) * T + row0 + 128, :],
                                         stl, waits=[tkq])
                            wt("vector", [tk1, tk2])
                        else:
                            row0 = s * T + tt * 512 + tc * 128
                            tkv = st_dma(d_vs[row0:row0 + 128, :], vstage,
                                         waits=[bank.rd])
                            wt("vector", [tkv])

            if LIMIT == "A":
                break
            pool()
            if DEBUG:
                LAYOUT[f"phaseB_start_s{s}"] = alloc_off[0]
            qf = [[b16(T), b16(T)] for _ in range(2)]
            if DEBUG:
                LAYOUT[f"qf_end_s{s}"] = alloc_off[0]
            kf = [[b16(T), b16(T)] for _ in range(2)]
            vsb = [f32(16 * 130) for _ in range(2)]
            wtp = [f32(2 * 2688) for _ in range(2)]
            tbuf = [f32(512) for _ in range(4)]
            ebuf = [f32(512) for _ in range(4)]
            bcb = f32(512)
            zb = f32(512)
            ysh = [b16(T), b16(T)]
            ysl = [b16(T), b16(T)]
            b_end = alloc_off[0]
            assert b_end <= AW

            def load_hp(hp, par, guards):
                toks = []
                for dram, buf in ((d_qs, qf[par]), (d_ks, kf[par])):
                    for hl in range(2):
                        srcap = dram[(s * 2 + hl) * T:(s * 2 + hl + 1) * T,
                                     hp * 128:(hp + 1) * 128]
                        toks.append(ld_dma_tr(buf[hl], srcap, waits=guards))
                sap = d_vs[s * T:(s + 1) * T, hp * 130:(hp + 1) * 130] \
                    .rearrange("(jt p) c -> p jt c", p=128)
                toks.append(ld_dma(vsb[par][:, :].rearrange("p (jt c) -> p jt c", jt=16),
                                   sap, waits=guards))
                toks.append(ld_dma(wtp[par][:, 0:2688],
                                   d_wt[2 * hp * 128:(2 * hp + 1) * 128, :], waits=guards))
                toks.append(ld_dma(wtp[par][:, 2688:5376],
                                   d_wt[(2 * hp + 1) * 128:(2 * hp + 2) * 128, :],
                                   waits=guards))
                return toks

            wt("sync", [GQ.tok()])
            hp_tok = [None, None]
            hp_done = [(), ()]
            av_pe = [None, None, None, None]
            tb_ac = [None, None, None, None]
            hp_tok[0] = load_hp(0, 0, [])

            for hp in range(8):
                par = hp % 2
                if hp + 1 < 8:
                    hp_tok[(hp + 1) % 2] = load_hp(hp + 1, (hp + 1) % 2,
                                                   list(hp_done[(hp + 1) % 2]))
                wt("tensor", hp_tok[par])
                qh, ql = qf[par]
                kh, kl = kf[par]
                av0, av1 = pbanks[4], pbanks[5]
                for qb in range(4):
                    q0 = qb * 512
                    njt = 4 * qb + 4
                    pend = None

                    def emit_av(entry, njt=None):
                        jt2, items = entry
                        for (hh, ebx, tk_exp, slot) in items:
                            avb = av0 if hh == 0 else av1
                            vsl = vsb[par][:, jt2 * 130 + hh * 65:
                                           jt2 * 130 + hh * 65 + 65]
                            wt("tensor", [tk_exp] + ([avb.rd] if jt2 == 0 else []))
                            tk_av = em("tensor", lambda e, p=avb.t, A=vsl, B=ebx,
                                       st=(jt2 == 0), sp=(jt2 == njt - 1):
                                       e.matmul(p[0:65, :], A, B, start=st, stop=sp),
                                       inc=PE)
                            avb.wr = tk_av
                            av_pe[slot] = tk_av

                    for jt in range(njt):
                        sb0 = pbanks[jt % 2]
                        sb1 = pbanks[2 + jt % 2]
                        wt("tensor", [sb0.rd, sb1.rd])
                        for ip, (a_, b_) in enumerate((("h", "h"), ("h", "l"), ("l", "h"))):
                            ka = kh if a_ == "h" else kl
                            qa = qh if b_ == "h" else ql
                            em("tensor", lambda e, p=sb0.t,
                               A=ka[0:64, jt * 128:(jt + 1) * 128],
                               B=qa[0:64, q0:q0 + 512], st=(ip == 0), sp=(ip == 2):
                               e.matmul(p[:, :], A, B, start=st, stop=sp,
                                        tile_position=(0, 0)))
                            tk_qk = em("tensor", lambda e, p=sb1.t,
                                       A=ka[64:128, jt * 128:(jt + 1) * 128],
                                       B=qa[64:128, q0:q0 + 512], st=(ip == 0), sp=(ip == 2):
                                       e.matmul(p[:, :], A, B, start=st, stop=sp,
                                                tile_position=(64, 0)),
                                       inc=PE if ip == 2 else None)
                        sb0.wr = tk_qk
                        sb1.wr = tk_qk
                        off = (q0 - 128 * jt) + 511
                        items = []
                        for hh, sb in ((0, sb0), (1, sb1)):
                            slot = 2 * (jt % 2) + hh
                            tb = tbuf[slot]
                            wslice = wtp[par][:, hh * 2688 + off: hh * 2688 + off + 512]
                            tk_stt = v_op(lambda e, o=tb, p=sb.t, w=wslice:
                                          e.scalar_tensor_tensor(o, p[:, :], 1.0, w,
                                                                 ALU.mult, ALU.add),
                                          waits=[sb.wr, tb_ac[slot]]
                                          + (hp_tok[par] if jt == 0 else []),
                                          inc=True)
                            sb.rd = tk_stt
                            ebx = ebuf[slot]
                            tk_exp = a_op(lambda e, o=ebx, i=tb:
                                          e.activation(o, i, AF.Exp, scale=0.125),
                                          waits=[tk_stt, av_pe[slot]],
                                          inc=True)
                            tb_ac[slot] = tk_exp
                            items.append((hh, ebx, tk_exp, slot))
                        if pend is not None:
                            emit_av(pend, njt)
                        pend = (jt, items)
                    emit_av(pend, njt)
                    for hh, avb in ((0, av0), (1, av1)):
                        srow = s_rows[64:65, hh * 512: hh * 512 + 512]
                        tk_s = v_op(lambda e, o=srow, p=avb.t:
                                    e.tensor_copy(o, p[64:65, :]), waits=[avb.wr], inc=True)
                        a_op(lambda e, o=srow, i=srow: e.activation(o, i, AF.Ln),
                             waits=[tk_s])
                        tk_rr = a_op(lambda e, o=srow, i=srow:
                                     e.activation(o, i, AF.Exp, scale=-1.0), inc=True)
                        ub6 = pbanks[6]
                        wt("tensor", [tk_rr, ub6.rd])
                        ub6.wr = em("tensor", lambda e, p=ub6.t, r=srow:
                                    e.matmul(p[:, :], s_ones[64:65, 0:128], r,
                                             start=True, stop=True), inc=PE)
                        v_op(lambda e, o=bcb, p=ub6.t: e.tensor_copy(o, p[:, :]),
                             waits=[ub6.wr])
                        ub6.rd = v_op(lambda e, o=zb[0:64, :], a=avb.t[0:64, :], b=bcb[0:64, :]:
                                      e.tensor_tensor(o, a, b, ALU.mult), inc=True)
                        avb.rd = ub6.rd
                        ysh_t = ysh[hh][0:64, q0:q0 + 512]
                        ysl_t = ysl[hh][0:64, q0:q0 + 512]
                        v_op(lambda e, o=ysh_t, i=zb[0:64, :]: e.tensor_copy(o, i))
                        v_op(lambda e, o=ysl_t, a=zb[0:64, :], b=ysh_t:
                             e.tensor_tensor(o, a, b, ALU.subtract), inc=True)
                hp_done[par] = ((PE, PE.n), (DV, DV.n), (AC, AC.n))
                tk_y = DV.tok()
                for hh in range(2):
                    rows0 = hp * 128 + hh * 64
                    yh_d = d_yt[(s * 2 + 0) * C + rows0:(s * 2 + 0) * C + rows0 + 64, :]
                    yl_d = d_yt[(s * 2 + 1) * C + rows0:(s * 2 + 1) * C + rows0 + 64, :]
                    yt_store_toks.append(st_dma(yh_d, ysh[hh][0:64, :], waits=[tk_y]))
                    yt_store_toks.append(st_dma(yl_d, ysl[hh][0:64, :], waits=[tk_y]))
                wt("vector", [yt_store_toks[-1]])

            if LIMIT == "B":
                break
            pool()
            wptmp = f32(NIC * C)
            wph = b16(NIC * C); wpl = b16(NIC * C)
            ytb = [[b16(NIC * 512), b16(NIC * 512)] for _ in range(2)]
            aob = f32(NIC * 512)
            xcb = [f32(NIC * 512) for _ in range(2)]
            o1b = f32(NIC * 512)
            xsqc = [f32(512) for _ in range(2)]
            m_sbc = f32(512); r_sbc = f32(512); negm2c = f32(512)
            zt = f32(512)
            gmaxcol = s_gmax[:, 0:1]
            c_end = alloc_off[0]

            tk_wp = ld_dma(wptmp, d_wp)
            tk_wps = split_hilo(wptmp, wph, wpl, waits=[tk_wp], inc=True)
            if s == 0:
                v_op(lambda e: e.memset(gmaxcol, -1.0e38))
            wt("sync", [GQ.tok()])

            yt_tok = [None, None]
            pe_after_tt = [None, None]

            def load_yt(tt, p2, guards):
                toks = []
                for hl in range(2):
                    srcap = d_yt[(s * 2 + hl) * C:(s * 2 + hl + 1) * C,
                                 tt * 512:(tt + 1) * 512] \
                        .rearrange("(ic p) j -> p ic j", p=128)
                    toks.append(ld_dma(ytb[p2][hl][:, :].rearrange("p (ic j) -> p ic j", ic=8),
                                       srcap, waits=guards))
                return toks

            def load_xc(tt, p2, guards):
                return ld_dma(xcb[p2][:, :].rearrange("p (ic j) -> p ic j", ic=8),
                              d_x[s * C:(s + 1) * C, tt * 512:(tt + 1) * 512]
                              .rearrange("(ic p) j -> p ic j", p=128), waits=guards)

            yt_tok[0] = load_yt(0, 0, [])
            xc_tok = [None, None]
            xc_free = [None, None]
            xc_tok[0] = load_xc(0, 0, [])

            for tt in range(NTT):
                par = tt % 2
                if tt + 1 < NTT:
                    gpar = (tt + 1) % 2
                    yt_tok[gpar] = load_yt(tt + 1, gpar,
                                           [pe_after_tt[gpar]] if pe_after_tt[gpar] else [])
                    xc_tok[gpar] = load_xc(tt + 1, gpar,
                                           [xc_free[gpar]] if xc_free[gpar] else [])
                ao_toks = []
                for co in range(8):
                    bank = pbanks[co % 4]
                    wt("tensor", [bank.rd, tk_wps] + yt_tok[par])
                    nmm = 0
                    for ic in range(8):
                        for (A, B) in ((wph, ytb[par][0]), (wph, ytb[par][1]),
                                       (wpl, ytb[par][0])):
                            last = nmm == 23
                            tkmm = em("tensor", lambda e, p=bank.t,
                                      a=A[:, ic * 1024 + co * 128: ic * 1024 + co * 128 + 128],
                                      b=B[:, ic * 512:(ic + 1) * 512],
                                      st=(nmm == 0), sp=last:
                                      e.matmul(p[:, :], a, b, start=st, stop=sp),
                                      inc=PE if last else None)
                            nmm += 1
                    bank.wr = tkmm
                    tk_ao = v_op(lambda e, o=aob[:, co * 512:(co + 1) * 512], p=bank.t,
                                 b=s_bp[:, co:co + 1]:
                                 e.tensor_scalar(o, p[:, :], b, None, ALU.add, ALU.bypass),
                                 waits=[bank.wr], inc=True)
                    bank.rd = tk_ao
                    ao_toks.append(tk_ao)
                chunks = [aob[:, ic * 512:(ic + 1) * 512] for ic in range(8)]
                tk_r = emit_ln(chunks, xsqc, m_sbc, r_sbc, negm2c,
                               pbanks[5], pbanks[6], ao_toks, r_free=(DV, DV.n))
                o1_toks = []
                for ic in range(8):
                    oc = o1b[:, ic * 512:(ic + 1) * 512]
                    v_op(lambda e, o=zt, a=chunks[ic], b=m_sbc:
                         e.tensor_tensor(o, a, b, ALU.subtract),
                         waits=[tk_r] if ic == 0 else [])
                    v_op(lambda e, o=zt, a=zt, b=r_sbc:
                         e.tensor_tensor(o, a, b, ALU.mult))
                    tko = v_op(lambda e, o=oc, a=zt, b=xcb[par][:, ic * 512:(ic + 1) * 512]:
                               e.tensor_tensor(o, a, b, ALU.add),
                               waits=[xc_tok[par]], inc=True)
                    o1_toks.append(tko)
                xc_free[par] = o1_toks[-1]
                tk_o1 = st_dma(d_o1[s * C:(s + 1) * C, tt * 512:(tt + 1) * 512]
                               .rearrange("(ic p) j -> p ic j", p=128),
                               o1b[:, :].rearrange("p (ic j) -> p ic j", ic=8),
                               waits=[o1_toks[-1]])
                o1_store_toks.append(tk_o1)
                chunks2 = [o1b[:, ic * 512:(ic + 1) * 512] for ic in range(8)]
                tk_r2 = emit_ln(chunks2, xsqc, m_sbc, r_sbc, negm2c,
                                pbanks[5], pbanks[6], o1_toks, r_free=(DV, DV.n))
                row = s * NTT + tt
                tk_mr = st_dma(d_mr[row:row + 1, 0:512], m_sbc[0:1, :], waits=[tk_r2])
                tk_mr2 = st_dma(d_mr[row:row + 1, 512:1024], r_sbc[0:1, :], waits=[tk_r2])
                mr_store_toks += [tk_mr, tk_mr2]
                for ic in range(8):
                    v_op(lambda e, o=zt, a=chunks2[ic], b=m_sbc:
                         e.tensor_tensor(o, a, b, ALU.subtract),
                         waits=[tk_r2] if ic == 0 else [])
                    v_op(lambda e, o=zt, a=zt, b=r_sbc:
                         e.tensor_tensor(o, a, b, ALU.mult))
                    v_op(lambda e, o=zt[:, 0:1], i=zt:
                         e.tensor_reduce(o, i, AXL.X, ALU.max), drain=True)
                    tk_m = v_op(lambda e, o=gmaxcol, a=gmaxcol, b=zt[:, 0:1]:
                                e.tensor_tensor(o, a, b, ALU.max),
                                inc=(ic == 7), drain=True)
                cmax_tok = tk_m
                pe_after_tt[par] = (PE, PE.n)
                wt("vector", [tk_o1])

        if LIMIT and LIMIT != "D":
            pool()
            if LIMIT == "C":
                st_dma(d_out[:, :], d_o1[:, :], waits=[(GQ, GQ.n), (DV, DV.n)])
            elif LIMIT == "B":
                st_dma(d_out[0:1024, :].bitcast(BF16)[:, 0:2048], d_yt[0:1024, :],
                       waits=[(GQ, GQ.n), (DV, DV.n)])
                st_dma(d_out[1024:2048, :].bitcast(BF16)[:, 0:2048], d_yt[2048:3072, :])
            elif LIMIT == "A":
                st_dma(d_out[0:1024, 0:512].bitcast(BF16), d_qs[0:1024, :],
                       waits=[(GQ, GQ.n), (DV, DV.n)])
                st_dma(d_out[0:1024, 512:1024].bitcast(BF16), d_vs[0:1024, 0:1024],
                       waits=[])
                st_dma(d_out[1024:2048, 0:512].bitcast(BF16), d_qs[2048:3072, :])
                st_dma(d_out[1024:2048, 512:1024].bitcast(BF16), d_vs[2048:3072, 0:1024])
            wt("gpsimd", [GQ.tok()])
            wt("sync", [LD.tok()])
            with nc.Block() as block:
                @block.sync
                def _(e):
                    for fn in prog.streams["sync"]:
                        fn(e)

                @block.vector
                def _(e):
                    for fn in prog.streams["vector"]:
                        fn(e)

                @block.scalar
                def _(e):
                    for fn in prog.streams["scalar"]:
                        fn(e)

                @block.tensor
                def _(e):
                    for fn in prog.streams["tensor"]:
                        fn(e)

                @block.gpsimd
                def _(e):
                    for fn in prog.streams["gpsimd"]:
                        fn(e)
            return nc
        def emit_barrier(gcol, cci, cco, dst_pair, prev_tok, ccs):
            if not hasattr(emit_barrier, "park"):
                emit_barrier.park = 0
                emit_barrier.park2 = 0
            """gcol [128,1] local max -> AllReduce max -> dst_pair [128, 2] (s, 1/s)."""
            ub = pbanks[6]
            wt("tensor", [prev_tok, ub.rd, tk_idf])
            ub.wr = em("tensor", lambda e, g=gcol: e.transpose(ub.t[0:1, 0:128], g, s_idf),
                       inc=PE)
            row = s_rows[0:1, 0:128]
            tk1 = v_op(lambda e, o=row[0:1, 0:1], i=ub.t[0:1, 0:128]:
                       e.tensor_reduce(o, i, AXL.X, ALU.max), waits=[ub.wr], inc=True, drain=True)
            ub.rd = tk1
            if HWDBG:
                pk = emit_barrier.park
                v_op(lambda e, o=s_park[0:1, pk:pk + 1], i=row[0:1, 0:1]:
                     e.tensor_copy(o, i))
                v_op(lambda e, o=s_park[0:1, 4 + pk * 2: 6 + pk * 2],
                     i=ub.t[0:1, 0:2]: e.tensor_copy(o, i))
                emit_barrier.park += 1
            tkc = v_op(lambda e, o=s_cc[0:1, :], i=row[0:1, 0:1].broadcast_to((1, 8)):
                       e.tensor_copy(o, i), inc=True, drain=True)
            if HWDBG:
                pk2 = emit_barrier.park2
                v_op(lambda e, o=s_park[0:1, 2 + pk2:3 + pk2], i=s_cc[0:1, 0:1]:
                     e.tensor_copy(o, i))
                emit_barrier.park2 += 1
            tkw = st_dma(cci.ap(), s_cc[0:1, :], waits=[tkc])
            wt("gpsimd", [tkw])
            tk_cc = em("gpsimd", lambda e: e.collective_compute(
                "AllReduce", ALU.max, replica_groups=[list(range(NCORES))],
                ins=[cci.ap()], outs=[cco.ap()]), inc=ccs)
            wt("gpsimd", [tk_cc])
            tk_rd = em("gpsimd", lambda e: e.dma_start(out=s_cc[0:1, :], in_=cco.ap()),
                       inc=GQ, inc_amt=16)
            r0 = s_rows[0:1, 0:1]
            r1 = s_rows[0:1, 1:2]
            r2 = s_rows[0:1, 2:4]
            v_op(lambda e: e.tensor_scalar(r0, s_cc[0:1, 0:1], EPS_Q, 1.0 / 255.0,
                                           ALU.max, ALU.mult), waits=[tk_rd], drain=True)
            v_op(lambda e: e.tensor_scalar(r1, s_cc[0:1, 0:1], EPS_Q, None,
                                           ALU.max, ALU.bypass), drain=True)
            v_op(lambda e: e.reciprocal(r1, r1), drain=True)
            v_op(lambda e: e.tensor_scalar(r1, r1, 255.0, None, ALU.mult, ALU.bypass),
                 drain=True)
            v_op(lambda e: e.tensor_copy(r2[0:1, 0:1], r0), drain=True)
            tk2 = v_op(lambda e: e.tensor_copy(r2[0:1, 1:2], r1), inc=True, drain=True)
            wt("tensor", [tk2, ub.rd])
            ub.wr = em("tensor", lambda e: e.matmul(ub.t[:, 0:2], s_ones[0:1, 0:128], r2,
                                                    start=True, stop=True), inc=PE)
            tk3 = v_op(lambda e, o=dst_pair: e.tensor_copy(o, ub.t[:, 0:2]),
                       waits=[ub.wr], inc=True)
            ub.rd = tk3
            return tk3

        tk_s1 = emit_barrier(s_gmax[:, 0:1], d_cci, d_cco, s_s1r, cmax_tok, CCS1)

        pool()
        w1qT = b16(NIC * FF)
        xq2 = b16(NIC * NSEQ * T)
        wld = [f32(1024), f32(1024)]
        wqi = b16(1024)
        o1l = [f32(NIC * 512) for _ in range(2)]
        m2b = f32(512); r2b = f32(512)
        mrl = [f32(1024), f32(1024)]
        zq = f32(512)
        hst = [f32(512) for _ in range(2)]
        hst_free = [(), ()]
        d_end = alloc_off[0]
        assert d_end <= AW, d_end

        w1fc_tok = [None] * NFC
        wl_tok = [None, None]
        wl_free = [None, None]
        wl_tok[0] = ld_dma(wld[0], d_w1[0:128, :])
        for fc in range(NFC):
            par = fc % 2
            if fc + 1 < NFC:
                wl_tok[(fc + 1) % 2] = ld_dma(wld[(fc + 1) % 2],
                                              d_w1[(fc + 1) * 128:(fc + 2) * 128, :],
                                              waits=[wl_free[(fc + 1) % 2]]
                                              if wl_free[(fc + 1) % 2] else [])
            amax = s_rows[:, 0:1]
            v_op(lambda e, o=amax, i=wld[par]:
                 e.tensor_reduce(o, i, AXL.X, ALU.max, apply_absolute_value=True),
                 waits=[wl_tok[par], tk_b1], drain=True)
            v_op(lambda e, o=s_sw1[:, fc:fc + 1], i=amax:
                 e.tensor_scalar(o, i, EPS_Q, (1.0 / 127.0), ALU.max, ALU.mult),
                 drain=True)
            v_op(lambda e, o=s_rw1[:, fc:fc + 1], i=amax:
                 e.tensor_scalar(o, i, EPS_Q, None, ALU.max, ALU.bypass), drain=True)
            v_op(lambda e, o=s_rw1[:, fc:fc + 1], i=s_rw1[:, fc:fc + 1]:
                 e.reciprocal(o, i), drain=True)
            v_op(lambda e, o=wld[par], i=wld[par], r=s_rw1[:, fc:fc + 1]:
                 e.tensor_scalar(o, i, r, 127.0, ALU.mult, ALU.mult))
            v_op(lambda e, o=wld[par], i=wld[par]:
                 e.tensor_scalar(o, i, MAGIC, MAGIC, ALU.add, ALU.subtract))
            v_op(lambda e, o=wld[par], i=wld[par]:
                 e.tensor_scalar(o, i, 127.0, -128.0, ALU.min, ALU.max))
            tk_q = v_op(lambda e, o=wqi, i=wld[par]: e.tensor_copy(o, i),
                        waits=[(PE, PE.n)], inc=True)
            wl_free[par] = tk_q
            v_op(lambda e, o=s_sc1[:, fc:fc + 1], a=s_sw1[:, fc:fc + 1], b=s_s1r[:, 0:1]:
                 e.tensor_tensor(o, a, b, ALU.mult), waits=[tk_s1], drain=True)
            v_op(lambda e, o=s_rows[:, 1:2], i=s_sc1[:, fc:fc + 1]: e.reciprocal(o, i), drain=True)
            v_op(lambda e, o=s_rows[:, 2:3], a=s_b1[:, fc:fc + 1], b=s_rows[:, 1:2]:
                 e.tensor_tensor(o, a, b, ALU.mult), drain=True)
            v_op(lambda e, o=s_rows[:, 2:3], i=s_rows[:, 2:3]:
                 e.tensor_scalar(o, i, MAGIC, MAGIC, ALU.add, ALU.subtract), drain=True)
            v_op(lambda e, o=s_sb1[:, fc:fc + 1], a=s_rows[:, 2:3], b=s_sc1[:, fc:fc + 1]:
                 e.tensor_tensor(o, a, b, ALU.mult), drain=True)
            for g in range(2):
                wt("tensor", [tk_q, psbf.rd, tk_idb])
                for q4 in range(4):
                    ic = g * 4 + q4
                    tktr = em("tensor", lambda e, i=wqi[:, ic * 128:(ic + 1) * 128],
                              o=psbf.t[:, q4 * 128:(q4 + 1) * 128]:
                              e.transpose(o, i, s_idb), inc=PE if q4 == 3 else None)
                psbf.wr = tktr
                tk_e = None
                for q4 in range(4):
                    ic = g * 4 + q4
                    tk_e = v_op(lambda e, o=w1qT[:, ic * FF + fc * 128: ic * FF + fc * 128 + 128],
                                i=psbf.t[:, q4 * 128:(q4 + 1) * 128]:
                                e.tensor_copy(o, i),
                                waits=[psbf.wr] if q4 == 0 else [], inc=(q4 == 3))
                psbf.rd = tk_e
            w1fc_tok[fc] = tk_e
        w1q_tok = DV.tok()

        o1_tok = [None, None]
        o1_free = [None, None]

        def load_o1(si, tti, p2):
            return ld_dma(o1l[p2][:, :].rearrange("p (ic j) -> p ic j", ic=8),
                          d_o1[si * C:(si + 1) * C, tti * 512:(tti + 1) * 512]
                          .rearrange("(ic p) j -> p ic j", p=128),
                          waits=[o1_free[p2]] if o1_free[p2] else [])

        wt("sync", [GQ.tok()])
        o1_tok[0] = load_o1(0, 0, 0)
        tiles = [(si, tti) for si in range(NSEQ) for tti in range(NTT)]
        xq2_tile_tok = [None] * len(tiles)

        def emit_xq2(i):
            si, tti = tiles[i]
            par = i % 2
            if i + 1 < len(tiles):
                o1_tok[(i + 1) % 2] = load_o1(*tiles[i + 1], (i + 1) % 2)
            rrow = si * NTT + tti
            mr_tok = ld_dma(mrl[par][0:1, :], d_mr[rrow:rrow + 1, :],
                            waits=[pbanks[6].wr] if i >= 2 else [])
            ub = pbanks[6]
            wt("tensor", [mr_tok, ub.rd])
            em("tensor", lambda e, m=mrl[par]: e.matmul(ub.t[:, :], s_ones[0:1, 0:128],
                                            m[0:1, 0:512], start=True, stop=True))
            ub2 = pbanks[5]
            wt("tensor", [mr_tok, ub2.rd])
            tk_bc = em("tensor", lambda e, m=mrl[par]: e.matmul(ub2.t[:, :], s_ones[0:1, 0:128],
                                                    m[0:1, 512:1024], start=True, stop=True),
                       inc=PE)
            ub.wr = tk_bc
            ub2.wr = tk_bc
            v_op(lambda e: e.tensor_copy(m2b, ub.t[:, :]), waits=[tk_bc])
            tk_m2 = v_op(lambda e: e.tensor_copy(r2b, ub2.t[:, :]), inc=True)
            ub.rd = tk_m2
            ub2.rd = tk_m2
            tk_xq = None
            for ic in range(8):
                col = ic * (NSEQ * T) + si * T + tti * 512
                v_op(lambda e, o=zq, a=o1l[par][:, ic * 512:(ic + 1) * 512], b=m2b:
                     e.tensor_tensor(o, a, b, ALU.subtract),
                     waits=[o1_tok[par], tk_m2] if ic == 0 else [])
                v_op(lambda e, o=zq, a=zq, b=r2b:
                     e.tensor_tensor(o, a, b, ALU.mult))
                v_op(lambda e, o=zq, i=zq, r=s_s1r[:, 1:2]:
                     e.tensor_scalar(o, i, r, MAGIC, ALU.mult, ALU.add))
                v_op(lambda e, o=zq, i=zq:
                     e.tensor_scalar(o, i, MAGIC, 255.0, ALU.subtract, ALU.min))
                tk_xq = v_op(lambda e, o=xq2[:, col:col + 512], i=zq:
                             e.tensor_scalar(o, i, 0.0, None, ALU.max, ALU.bypass),
                             inc=(ic == 7))
            o1_free[par] = tk_xq
            xq2_tile_tok[i] = tk_xq

        emit_xq2(0)

        v_op(lambda e: e.memset(s_gmax[:, 1:2], 0.0))
        dmax_tok = None
        for tt in range(NSEQ * NTT):
            if tt + 1 < NSEQ * NTT:
                emit_xq2(tt + 1)
            for fc in range(NFC):
                bank = pbanks[(tt * NFC + fc) % 4]
                wt("tensor", [bank.rd, xq2_tile_tok[tt], w1fc_tok[fc]])
                for ic in range(8):
                    tkmm = em("tensor", lambda e, p=bank.t,
                              A=w1qT[:, ic * FF + fc * 128: ic * FF + fc * 128 + 128],
                              B=xq2[:, ic * (NSEQ * T) + tt * 512:
                                    ic * (NSEQ * T) + tt * 512 + 512],
                              st=(ic == 0), sp=(ic == 7):
                              e.matmul(p[:, :], A, B, start=st, stop=sp),
                              inc=PE if ic == 7 else None)
                bank.wr = tkmm
                par = (tt * NFC + fc) % 2
                tk_h = a_op(lambda e, o=hst[par], p=bank.t, sc=s_sc1[:, fc:fc + 1],
                            b=s_sb1[:, fc:fc + 1]:
                            e.activation(o, p[:, :], AF.Relu, bias=b, scale=sc),
                            waits=[bank.wr, w1q_tok, tk_s1] + list(hst_free[par]),
                            inc=True)
                bank.rd = tk_h
                v_op(lambda e, o=s_rows[:, 4:5], i=hst[par]:
                     e.tensor_reduce(o, i, AXL.X, ALU.max), waits=[tk_h], drain=True)
                dmax_tok = v_op(lambda e, o=s_gmax[:, 1:2], a=s_gmax[:, 1:2],
                                b=s_rows[:, 4:5]:
                                e.tensor_tensor(o, a, b, ALU.max), inc=True, drain=True)
                tk_st = st_dma(d_h[fc * 128:(fc + 1) * 128, tt * 512:(tt + 1) * 512],
                               hst[par], waits=[tk_h])
                hst_free[par] = (tk_st, dmax_tok)

        tk_s2 = emit_barrier(s_gmax[:, 1:2], d_cci2, d_cco2, s_s2r, dmax_tok, CCS2)

        if LIMIT == "D":
            pool()
            st_dma(d_out[:, :], d_h[0:2048, 0:2048], waits=[(GQ, GQ.n), (DV, DV.n)])
            _t = st_dma(d_out[0:1, 0:2].bitcast(F32), None, waits=[]) if False else None
            st_dma(d_out[2047:2048, 0:2], s_s1r[0:1, 0:2], waits=[tk_s2])
        pool()
        w2qT = b16(NFC * C)
        hq = b16(NFC * 512)
        w2ld = [f32(FF // 2), f32(FF // 2)]
        w2qi = b16(FF)
        o2b = f32(NIC * 512)
        o1e = [f32(NIC * 512) for _ in range(2)]
        hqe = [f32(512) for _ in range(2)]
        xsqe = [f32(512) for _ in range(2)]
        m_e = f32(512); r_e = f32(512); negm2e = f32(512)
        zte = f32(512)
        amax2 = f32(2)
        e_end = alloc_off[0]
        assert e_end <= AW, e_end

        w2_free = None
        for cc in range(NIC if LIMIT != "D" else 0):
            tk_l0 = ld_dma(w2ld[0], d_w2[cc * 128:(cc + 1) * 128, 0:2048],
                           waits=[w2_free] if w2_free else [])
            tk_l1 = ld_dma(w2ld[1], d_w2[cc * 128:(cc + 1) * 128, 2048:4096],
                           waits=[w2_free] if w2_free else [])
            v_op(lambda e, o=amax2[:, 0:1], i=w2ld[0]:
                 e.tensor_reduce(o, i, AXL.X, ALU.max, apply_absolute_value=True),
                 waits=[tk_l0, tk_b2, tk_s2], drain=True)
            v_op(lambda e, o=amax2[:, 1:2], i=w2ld[1]:
                 e.tensor_reduce(o, i, AXL.X, ALU.max, apply_absolute_value=True),
                 waits=[tk_l1], drain=True)
            amax = s_rows[:, 0:1]
            v_op(lambda e, o=amax, a=amax2[:, 0:1], b=amax2[:, 1:2]:
                 e.tensor_tensor(o, a, b, ALU.max), drain=True)
            v_op(lambda e, o=s_sw2[:, cc:cc + 1], i=amax:
                 e.tensor_scalar(o, i, EPS_Q, (1.0 / 127.0), ALU.max, ALU.mult),
                 drain=True)
            v_op(lambda e, o=s_rw2[:, cc:cc + 1], i=amax:
                 e.tensor_scalar(o, i, EPS_Q, None, ALU.max, ALU.bypass), drain=True)
            v_op(lambda e, o=s_rw2[:, cc:cc + 1], i=s_rw2[:, cc:cc + 1]:
                 e.reciprocal(o, i), drain=True)
            tk_q = None
            for hq2 in range(2):
                wbuf = w2ld[hq2]
                v_op(lambda e, o=wbuf, i=wbuf, r=s_rw2[:, cc:cc + 1]:
                     e.tensor_scalar(o, i, r, 127.0, ALU.mult, ALU.mult))
                v_op(lambda e, o=wbuf, i=wbuf:
                     e.tensor_scalar(o, i, MAGIC, MAGIC, ALU.add, ALU.subtract))
                v_op(lambda e, o=wbuf, i=wbuf:
                     e.tensor_scalar(o, i, 127.0, -128.0, ALU.min, ALU.max))
                tk_q = v_op(lambda e, o=w2qi[:, hq2 * 2048:(hq2 + 1) * 2048], i=wbuf:
                            e.tensor_copy(o, i), waits=[(PE, PE.n)] if hq2 == 0 else [],
                            inc=(hq2 == 1))
            w2_free = tk_q
            v_op(lambda e, o=s_sc2[:, cc:cc + 1], a=s_sw2[:, cc:cc + 1], b=s_s2r[:, 0:1]:
                 e.tensor_tensor(o, a, b, ALU.mult), drain=True)
            v_op(lambda e, o=s_rows[:, 1:2], i=s_sc2[:, cc:cc + 1]: e.reciprocal(o, i), drain=True)
            v_op(lambda e, o=s_rows[:, 2:3], a=s_b2[:, cc:cc + 1], b=s_rows[:, 1:2]:
                 e.tensor_tensor(o, a, b, ALU.mult), drain=True)
            v_op(lambda e, o=s_rows[:, 2:3], i=s_rows[:, 2:3]:
                 e.tensor_scalar(o, i, MAGIC, MAGIC, ALU.add, ALU.subtract), drain=True)
            v_op(lambda e, o=s_sb2[:, cc:cc + 1], a=s_rows[:, 2:3], b=s_sc2[:, cc:cc + 1]:
                 e.tensor_tensor(o, a, b, ALU.mult), drain=True)
            for g in range(8):
                wt("tensor", [tk_q, psbf.rd])
                for q4 in range(4):
                    fc = g * 4 + q4
                    tktr = em("tensor", lambda e, i=w2qi[:, fc * 128:(fc + 1) * 128],
                              o=psbf.t[:, q4 * 128:(q4 + 1) * 128]:
                              e.transpose(o, i, s_idb), inc=PE if q4 == 3 else None)
                psbf.wr = tktr
                tk_e = None
                for q4 in range(4):
                    fc = g * 4 + q4
                    tk_e = v_op(lambda e, o=w2qT[:, fc * C + cc * 128: fc * C + cc * 128 + 128],
                                i=psbf.t[:, q4 * 128:(q4 + 1) * 128]:
                                e.tensor_copy(o, i),
                                waits=[psbf.wr] if q4 == 0 else [], inc=(q4 == 3))
                psbf.rd = tk_e
        w2q_tok = DV.tok()

        if DEBUG:
            st_dma(d_dbgw[:, 0:16384], w2qT[:, 0:16384], waits=[w2q_tok])
            st_dma(d_dbgb[:, 0:8192], w2qT[:, 16384:24576], waits=[w2q_tok])
        h_tok = [None, None]
        h_free = [None, None]
        oe_tok = [None, None]
        oe_free = [None, None]

        def load_h(tt, fc, p2):
            return ld_dma(hqe[p2], d_h[fc * 128:(fc + 1) * 128, tt * 512:(tt + 1) * 512],
                          waits=[h_free[p2]] if h_free[p2] else [])

        def load_o1e(i, p2):
            si, tti = divmod(i, NTT)
            return ld_dma(o1e[p2][:, :].rearrange("p (ic j) -> p ic j", ic=8),
                          d_o1[si * C:(si + 1) * C, tti * 512:(tti + 1) * 512]
                          .rearrange("(ic p) j -> p ic j", p=128),
                          waits=[oe_free[p2]] if oe_free[p2] else [])

        wt("sync", [GQ.tok()])
        h_tok[0] = load_h(0, 0, 0)
        oe_tok[0] = load_o1e(0, 0)
        out_toks = []
        ffn2_pe = None
        for tt in range(NSEQ * NTT if LIMIT != "D" else 0):
            if ffn2_pe is not None:
                wt("vector", [ffn2_pe])
            for fc in range(NFC):
                par = fc % 2
                nxt = tt * NFC + fc + 1
                if nxt < NSEQ * NTT * NFC:
                    h_tok[nxt % 2] = load_h(nxt // NFC, nxt % NFC, nxt % 2)
                v_op(lambda e, o=zte, i=hqe[par], r=s_s2r[:, 1:2]:
                     e.tensor_scalar(o, i, r, MAGIC, ALU.mult, ALU.add),
                     waits=[h_tok[par], tk_s2])
                v_op(lambda e, o=zte, i=zte:
                     e.tensor_scalar(o, i, MAGIC, 255.0, ALU.subtract, ALU.min))
                tk_hq = v_op(lambda e, o=hq[:, fc * 512:(fc + 1) * 512], i=zte:
                             e.tensor_scalar(o, i, 0.0, None, ALU.max, ALU.bypass),
                             inc=True)
                h_free[par] = tk_hq
            hq_tok = tk_hq
            if tt + 1 < NSEQ * NTT:
                oe_tok[(tt + 1) % 2] = load_o1e(tt + 1, (tt + 1) % 2)
            par_o = tt % 2
            o2_toks = []
            for co in range(NIC):
                bank = pbanks[co % 4]
                wt("tensor", [bank.rd, hq_tok, w2q_tok])
                for fc in range(NFC):
                    tkmm = em("tensor", lambda e, p=bank.t,
                              A=w2qT[:, fc * C + co * 128: fc * C + co * 128 + 128],
                              B=hq[:, fc * 512:(fc + 1) * 512],
                              st=(fc == 0), sp=(fc == NFC - 1):
                              e.matmul(p[:, :], A, B, start=st, stop=sp),
                              inc=PE if fc == NFC - 1 else None)
                bank.wr = tkmm
                tk_o2 = v_op(lambda e, o=o2b[:, co * 512:(co + 1) * 512], p=bank.t,
                             sc=s_sc2[:, co:co + 1], b=s_sb2[:, co:co + 1]:
                             e.tensor_scalar(o, p[:, :], sc, b, ALU.mult, ALU.add),
                             waits=[bank.wr], inc=True)
                bank.rd = tk_o2
                o2_toks.append(tk_o2)
            if DEBUG and tt == 0:
                _dt1 = st_dma(d_dbg[:, 0:4096], o2b, waits=[o2_toks[-1]])
                _dt2 = st_dma(d_dbgb2[:, 0:16384], hq, waits=[hq_tok])
                wt("vector", [_dt1, _dt2])
            chunks = [o2b[:, ic * 512:(ic + 1) * 512] for ic in range(8)]
            tk_r = emit_ln(chunks, xsqe, m_e, r_e, negm2e,
                           pbanks[5], pbanks[6], o2_toks, r_free=(DV, DV.n))
            for ic in range(8):
                oc = o2b[:, ic * 512:(ic + 1) * 512]
                v_op(lambda e, o=zte, a=chunks[ic], b=m_e:
                     e.tensor_tensor(o, a, b, ALU.subtract),
                     waits=[tk_r, oe_tok[par_o]] if ic == 0 else [])
                v_op(lambda e, o=zte, a=zte, b=r_e:
                     e.tensor_tensor(o, a, b, ALU.mult))
                tk_f = v_op(lambda e, o=oc, a=zte, b=o1e[par_o][:, ic * 512:(ic + 1) * 512]:
                            e.tensor_tensor(o, a, b, ALU.add), inc=(ic == 7))
            oe_free[par_o] = tk_f
            si, tti = divmod(tt, NTT)
            tk_out = st_dma(d_out[si * C:(si + 1) * C, tti * 512:(tti + 1) * 512]
                            .rearrange("(ic p) j -> p ic j", p=128),
                            o2b[:, :].rearrange("p (ic j) -> p ic j", ic=8),
                            waits=[tk_f])
            out_toks.append(tk_out)
            wt("vector", [tk_out])
            ffn2_pe = (PE, PE.n)

        if HWDBG:
            wt("gpsimd", [GQ.tok(), (DV, DV.n), (PE, PE.n), (AC, AC.n)])
            st_dma(dbg_o1, d_o1[:, :])
            st_dma(dbg_y, d_yt[:, :])
            st_dma(dbg_h, d_h[:, :])
            st_dma(dbg_q, d_qs[:, :])
            st_dma(dbg_s[0:1, 0:2], s_s1r[0:1, 0:2])
            st_dma(dbg_s[0:1, 2:4], s_s2r[0:1, 0:2])
            st_dma(dbg_s[0:1, 4:6].bitcast(F32), d_cco[0:1, 0:2])
            st_dma(dbg_s[0:1, 6:8].bitcast(F32), d_cci[0:1, 0:2])
            st_dma(dbg_g[:, 0:2], s_gmax[:, 0:2])
            st_dma(dbg_g[0:1, 2:10], s_park[0:1, 0:8])
        wt("gpsimd", [GQ.tok()])
        wt("sync", [LD.tok()])

        with nc.Block() as block:
            @block.sync
            def _(e):
                for fn in prog.streams["sync"]:
                    fn(e)

            @block.vector
            def _(e):
                for fn in prog.streams["vector"]:
                    fn(e)

            @block.scalar
            def _(e):
                for fn in prog.streams["scalar"]:
                    fn(e)

            @block.tensor
            def _(e):
                for fn in prog.streams["tensor"]:
                    fn(e)

            @block.gpsimd
            def _(e):
                for fn in prog.streams["gpsimd"]:
                    fn(e)

    return nc



def make_inputs(core, inp):
    x = np.asarray(inp["L0_attn_x"], dtype=np.float32)
    wq = np.asarray(inp["L0_attn_Wq"], dtype=np.float32)
    wk = np.asarray(inp["L0_attn_Wk"], dtype=np.float32)
    wv = np.asarray(inp["L0_attn_Wv"], dtype=np.float32)
    wp = np.asarray(inp["L0_attn_Wp"], dtype=np.float32)

    def wlay(w):
        return np.ascontiguousarray(
            w.reshape(NIC, 128, C).transpose(1, 0, 2).reshape(128, NIC * C))

    def blay(b, nch):
        return np.ascontiguousarray(np.asarray(b, dtype=np.float32)
                                    .reshape(nch, 128).T)

    cosTM, sswTM = _rotary_tables()
    wt = _wt_tables(inp["L0_attn_rel_bias_table"])
    brows = np.concatenate([np.asarray(inp["L0_attn_bq"], dtype=np.float32),
                            np.asarray(inp["L0_attn_bk"], dtype=np.float32),
                            np.asarray(inp["L0_attn_bv"], dtype=np.float32)])[None, :]
    return {
        "x_fm": np.ascontiguousarray(
            x[core * NSEQ:(core + 1) * NSEQ].transpose(0, 2, 1).reshape(NSEQ * C, T)),
        "wq": wlay(wq), "wk": wlay(wk), "wv": wlay(wv), "wp": wlay(wp),
        "brows": np.ascontiguousarray(brows),
        "bp_c": blay(inp["L0_attn_bp"], NIC),
        "w1": np.ascontiguousarray(np.asarray(inp["L0_ffn_w1"], dtype=np.float32)),
        "b1_c": blay(inp["L0_ffn_b1"], NFC),
        "w2": np.ascontiguousarray(np.asarray(inp["L0_ffn_w2"], dtype=np.float32)),
        "b2_c": blay(inp["L0_ffn_b2"], NIC),
        "wt": wt.reshape(H * 128, 2688),
        "cosTM": cosTM, "sswTM": sswTM,
        "identb": np.eye(128, dtype=ml_dtypes.bfloat16),
        "identf": np.eye(128, dtype=np.float32),
        "epscol": np.full((128, 1), EPS_LN, dtype=np.float32),
    }


def kernel(**inputs):
    nc = build_nc()
    in_maps = [make_inputs(c, inputs) for c in range(NCORES)]
    res = run_bass_kernel_spmd(nc, in_maps, core_ids=list(range(NCORES)))
    outs = [np.asarray(r["out_fm"]).reshape(NSEQ, C, T).transpose(0, 2, 1)
            for r in res.results]
    return np.ascontiguousarray(np.concatenate(outs, axis=0), dtype=np.float32)


# version 05
# speedup vs baseline: 1.2238x; 1.0048x over previous
import math
from contextlib import ExitStack

import numpy as np
import ml_dtypes

import concourse.bass as bass
import concourse.mybir as mybir
from concourse.bass_utils import run_bass_kernel_spmd

F32 = mybir.dt.float32
BF16 = mybir.dt.bfloat16
AF = mybir.ActivationFunctionType
ALU = mybir.AluOpType
AXL = mybir.AxisListType
MAGIC = 12582912.0
NEG = -1.0e38

DEBUG = False
LIMIT = ""
HWDBG = False
LAYOUT = {}
NCORES = 8
NSEQ = 2
T = 2048
C = 1024
H = 16
D = 64
FF = 4096
NTT = 4
TTOK = 512
NIC = 8
NFC = 32
EPS_LN = 1e-5
EPS_Q = 1e-8



def _rotary_tables():
    inv = 1.0 / (10000.0 ** (np.arange(0, D, 2, dtype=np.float64) / D))
    t = np.arange(T, dtype=np.float64)
    fr = np.einsum("i,j->ij", t, inv)
    fr = np.concatenate([fr, fr], axis=1)
    cos = np.cos(fr).astype(np.float32)
    sin = np.sin(fr).astype(np.float32)
    ssw = sin.copy()
    ssw[:, 32:] = -ssw[:, 32:]
    cosTM = cos.reshape(16, 128, 64).transpose(1, 0, 2).reshape(128, 16 * 64).copy()
    sswTM = ssw.reshape(16, 128, 64).transpose(1, 0, 2).reshape(128, 16 * 64).copy()
    return cosTM, sswTM


def _bucket(n):
    n = np.asarray(n, dtype=np.int64)
    max_exact = 16
    out = np.where(n < max_exact, n, 0).astype(np.int64)
    big = n >= max_exact
    nsafe = np.maximum(n, 1).astype(np.float64)
    val = max_exact + (np.log(nsafe / max_exact) / math.log(128.0 / max_exact) * 16.0)
    val = val.astype(np.int32)
    val = np.minimum(val, 31)
    out = np.where(big, val, out)
    return out


def _wt_tables(rel_bias_table):
    table = np.asarray(rel_bias_table, dtype=np.float32)
    n = np.arange(-2047, 2048, dtype=np.int64)
    g = np.empty((H, n.size), dtype=np.float32)
    pos = n >= 0
    buck = _bucket(np.maximum(n, 0))
    for h in range(H):
        g[h] = np.where(pos, table[buck, h], np.float32(NEG))
    M = 2688
    p = np.arange(128)[:, None]
    m = np.arange(M)[None, :]
    idx = m - p - 511 + 2047
    idx = np.clip(idx, 0, n.size - 1)
    wt = g[:, idx]
    return np.ascontiguousarray(wt)



class Sem:
    def __init__(self, handle):
        self.h = handle
        self.n = 0

    def inc(self, amount=1):
        self.n += amount
        return self.n

    def tok(self):
        return (self, self.n)


class Prog:

    ENGS = ("sync", "vector", "scalar", "tensor", "gpsimd")

    def __init__(self):
        self.streams = {e: [] for e in self.ENGS}

    def wait(self, eng, tok):
        if tok is None:
            return
        sem, val = tok
        if val <= 0:
            return
        self.streams[eng].append(lambda e, s=sem.h, v=val: e.wait_ge(s, v))

    def waits(self, eng, toks):
        best = {}
        for t in toks:
            if t is None:
                continue
            s, v = t
            if v > 0 and (id(s) not in best or best[id(s)][1] < v):
                best[id(s)] = (s, v)
        for s, v in best.values():
            self.wait(eng, (s, v))

    def emit(self, eng, fn, inc=None, inc_amt=1):
        if inc is None:
            self.streams[eng].append(fn)
            return None
        sem = inc
        val = sem.inc(inc_amt)

        def run(e, fn=fn, s=sem.h, a=inc_amt):
            inst = fn(e)
            inst.then_inc(s, a)
        self.streams[eng].append(run)
        return (sem, val)


class PsumBank:
    def __init__(self, tile):
        self.t = tile
        self.wr = None
        self.rd = None



def build_nc():
    nc = bass.Bass("TRN2", target_bir_lowering=False, detect_race_conditions=False)

    d_x = nc.dram_tensor("x_fm", [NSEQ * C, T], F32, kind="ExternalInput").ap()
    d_wq = nc.dram_tensor("wq", [128, NIC * C], F32, kind="ExternalInput").ap()
    d_wk = nc.dram_tensor("wk", [128, NIC * C], F32, kind="ExternalInput").ap()
    d_wv = nc.dram_tensor("wv", [128, NIC * C], F32, kind="ExternalInput").ap()
    d_wp = nc.dram_tensor("wp", [128, NIC * C], F32, kind="ExternalInput").ap()
    d_brows = nc.dram_tensor("brows", [1, 3 * C], F32, kind="ExternalInput").ap()
    d_bp = nc.dram_tensor("bp_c", [128, NIC], F32, kind="ExternalInput").ap()
    d_w1 = nc.dram_tensor("w1", [FF, C], F32, kind="ExternalInput").ap()
    d_b1 = nc.dram_tensor("b1_c", [128, NFC], F32, kind="ExternalInput").ap()
    d_w2 = nc.dram_tensor("w2", [C, FF], F32, kind="ExternalInput").ap()
    d_b2 = nc.dram_tensor("b2_c", [128, NIC], F32, kind="ExternalInput").ap()
    d_wt = nc.dram_tensor("wt", [H * 128, 2688], F32, kind="ExternalInput").ap()
    d_cos = nc.dram_tensor("cosTM", [128, 16 * 64], F32, kind="ExternalInput").ap()
    d_ssw = nc.dram_tensor("sswTM", [128, 16 * 64], F32, kind="ExternalInput").ap()
    d_idb = nc.dram_tensor("identb", [128, 128], BF16, kind="ExternalInput").ap()
    d_idf = nc.dram_tensor("identf", [128, 128], F32, kind="ExternalInput").ap()
    d_eps = nc.dram_tensor("epscol", [128, 1], F32, kind="ExternalInput").ap()

    d_out = nc.dram_tensor("out_fm", [NSEQ * C, T], F32, kind="ExternalOutput").ap()
    if HWDBG:
        dbg_o1 = nc.dram_tensor("dbg_o1", [NSEQ * C, T], F32, kind="ExternalOutput").ap()
        dbg_y = nc.dram_tensor("dbg_y", [NSEQ * 2 * C, T], BF16, kind="ExternalOutput").ap()
        dbg_h = nc.dram_tensor("dbg_h", [FF, NSEQ * T], F32, kind="ExternalOutput").ap()
        dbg_s = nc.dram_tensor("dbg_s", [1, 8], F32, kind="ExternalOutput").ap()
        dbg_g = nc.dram_tensor("dbg_g", [128, 16], F32, kind="ExternalOutput").ap()
        dbg_q = nc.dram_tensor("dbg_q", [NSEQ * 2 * T, C], BF16, kind="ExternalOutput").ap()

    d_wsp = nc.dram_tensor("wsplit", [4 * 2 * 128, NIC * C], BF16)
    d_qs = nc.dram_tensor("q_tm", [NSEQ * 2 * T, C], BF16)
    d_ks = nc.dram_tensor("k_tm", [NSEQ * 2 * T, C], BF16)
    d_vs = nc.dram_tensor("v_tm", [NSEQ * T, H * 65], F32)
    d_yt = nc.dram_tensor("y_fm", [NSEQ * 2 * C, T], BF16)
    d_o1 = nc.dram_tensor("out1_fm", [NSEQ * C, T], F32)
    d_mr = nc.dram_tensor("m2r2", [NSEQ * NTT, 2 * TTOK], F32)
    d_h = nc.dram_tensor("h_fm", [FF, NSEQ * T], F32)
    d_cci = nc.dram_tensor("cc_in", [1, 8], F32)
    d_cco = nc.dram_tensor("cc_out", [1, 8], F32)
    d_cci2 = nc.dram_tensor("cc_in2", [1, 8], F32)
    d_cco2 = nc.dram_tensor("cc_out2", [1, 8], F32)
    if DEBUG:
        d_dbg = nc.dram_tensor("dbg", [128, 8192], F32)
        d_dbgb = nc.dram_tensor("dbgb", [128, 8192], BF16)
        d_dbgw = nc.dram_tensor("dbgw", [128, 16384], BF16)
        d_dbgb2 = nc.dram_tensor("dbgb2", [128, 16384], BF16)

    prog = Prog()

    with ExitStack() as ctx:
        AW = 52992
        arena = ctx.enter_context(nc.sbuf_tensor("arena", [128, AW], F32))

        alloc_off = [0]

        def alloc32(n, align=16):
            o = (alloc_off[0] + align - 1) // align * align
            alloc_off[0] = o + n
            assert alloc_off[0] <= AW, f"arena overflow: {alloc_off[0]} > {AW}"
            return o

        def f32(n, align=16):
            o = alloc32(n, align)
            return arena[:, o:o + n]

        def b16(n, align=16):
            o = alloc32((n + 1) // 2, align)
            return arena[:, o:o + (n + 1) // 2].bitcast(BF16)

        s_ones = f32(128)
        s_i1024 = f32(128)
        s_idf = f32(128)
        s_idb = b16(128)
        s_eps = f32(1)
        s_bp = f32(NIC)
        s_b1 = f32(NFC)
        s_b2 = f32(NIC)
        s_sw1 = f32(NFC)
        s_rw1 = f32(NFC)
        s_sw2 = f32(NIC)
        s_rw2 = f32(NIC)
        s_sc1 = f32(NFC)
        s_sb1 = f32(NFC)
        s_sc2 = f32(NIC)
        s_sb2 = f32(NIC)
        s_gmax = f32(2)
        s_s1r = f32(2)
        s_s2r = f32(2)
        s_rows = f32(2048)
        s_park = f32(16)
        s_cc = f32(8)

        PERSIST_END = alloc_off[0]

        pbanks = [PsumBank(ctx.enter_context(nc.psum_tensor(f"ps{i}", [128, 512], F32)))
                  for i in range(7)]
        psbf_t = ctx.enter_context(nc.psum_tensor("psbf", [128, 1024], BF16))
        psbf = PsumBank(psbf_t)

        LD = Sem(ctx.enter_context(nc.semaphore("ld")))
        GQ = Sem(ctx.enter_context(nc.semaphore("gq")))
        PE = Sem(ctx.enter_context(nc.semaphore("pe")))
        DV = Sem(ctx.enter_context(nc.semaphore("dv")))
        AC = Sem(ctx.enter_context(nc.semaphore("ac")))
        CCS1 = Sem(ctx.enter_context(nc.semaphore("ccs1")))
        CCS2 = Sem(ctx.enter_context(nc.semaphore("ccs2")))

        em = prog.emit
        wt = prog.waits

        def ld_dma(dst, src, waits=()):
            wt("sync", waits)
            return em("sync", lambda e, d=dst, s=src: e.dma_start(out=d, in_=s),
                      inc=LD, inc_amt=16)

        def ld_dma_tr(dst, src, waits=()):
            wt("sync", waits)
            return em("sync", lambda e, d=dst, s=src: e.dma_start(out=d, in_=s, transpose=True),
                      inc=LD, inc_amt=16)

        def st_dma(dst, src, waits=()):
            wt("gpsimd", waits)
            return em("gpsimd", lambda e, d=dst, s=src: e.dma_start(out=d, in_=s),
                      inc=GQ, inc_amt=16)

        def v_op(fn, waits=(), inc=False, drain=False):
            wt("vector", waits)
            if drain:
                tok = em("vector", fn, inc=DV)
                prog.wait("vector", tok)
                return tok
            return em("vector", fn, inc=DV if inc else None)

        def a_op(fn, waits=(), inc=False):
            wt("scalar", waits)
            return em("scalar", fn, inc=AC if inc else None)

        def split_hilo(src32, dst_hi, dst_lo, waits=(), inc=True):
            v_op(lambda e, a=dst_hi, b=src32: e.tensor_copy(a, b), waits=waits)
            return v_op(lambda e, o=dst_lo, a=src32, b=dst_hi:
                        e.tensor_tensor(o, a, b, ALU.subtract), inc=inc)

        tk_idf = ld_dma(s_idf, d_idf)
        tk_idb = ld_dma(s_idb, d_idb)
        tk_eps = ld_dma(s_eps, d_eps)
        tk_bp = ld_dma(s_bp, d_bp)
        tk_b1 = ld_dma(s_b1, d_b1)
        tk_b2 = ld_dma(s_b2, d_b2)
        v_op(lambda e: e.memset(s_ones, 1.0))
        v_op(lambda e: e.memset(s_i1024, 1.0 / 1024.0))

        def pool():
            toks = [(PE, PE.n), (DV, DV.n), (AC, AC.n), (GQ, GQ.n), (LD, LD.n)]
            for eng in ("sync", "vector", "scalar", "tensor", "gpsimd"):
                wt(eng, toks)
            alloc_off[0] = PERSIST_END
            return alloc_off[0]

        pool()
        wtmp = [f32(NIC * C) for _ in range(2)]
        whi_st = b16(NIC * C)
        wlo_st = b16(NIC * C)
        wload = [None, None]
        wins = [d_wq, d_wk, d_wv, d_wp]
        prev_done = None
        prev_stores = []
        for wi in range(4):
            par = wi % 2
            if wi == 0:
                wload[0] = ld_dma(wtmp[0], wins[0])
            if wi + 1 < 4:
                wload[(wi + 1) % 2] = ld_dma(wtmp[(wi + 1) % 2], wins[wi + 1],
                                             waits=[] if wi == 0 else [prev_done])
            tk = split_hilo(wtmp[par], whi_st, wlo_st,
                            waits=[wload[par]] + prev_stores)
            t1 = st_dma(d_wsp[(wi * 2 + 0) * 128:(wi * 2 + 1) * 128, :], whi_st, waits=[tk])
            t2 = st_dma(d_wsp[(wi * 2 + 1) * 128:(wi * 2 + 2) * 128, :], wlo_st, waits=[tk])
            prev_done = tk
            prev_stores = [t1, t2]
        wsp_tok = prev_stores[1]

        def emit_ln(chunks, xsq, m_sb, r_sb, negm2, st_b1, st_b2, src_toks,
                    r_free=None):
            pe_sq = [(PE, PE.n), (PE, PE.n)]
            for ic in range(8):
                par = ic % 2
                wl = [src_toks[ic], pe_sq[par]]
                tk = a_op(lambda e, o=xsq[par], i=chunks[ic]:
                          e.activation(o, i, AF.Square), waits=wl, inc=True)
                wt("tensor", [st_b1.rd if ic == 0 else None, src_toks[ic]])
                em("tensor", lambda e, p=st_b1.t, r=chunks[ic], s=(ic == 0), t=(ic == 7):
                   e.matmul(p[0:1, :], s_ones[:, 0:1], r, start=s, stop=t))
                wt("tensor", [st_b2.rd if ic == 0 else None, tk])
                tk2 = em("tensor", lambda e, p=st_b2.t, r=xsq[par], s=(ic == 0), t=(ic == 7):
                         e.matmul(p[0:1, :], s_ones[:, 0:1], r, start=s, stop=t),
                         inc=PE)
                pe_sq[par] = tk2
            st_b1.wr = tk2
            st_b2.wr = tk2
            r1 = s_rows[0:1, 0:512]
            r2 = s_rows[0:1, 512:1024]
            v_op(lambda e: e.tensor_copy(r1, st_b1.t[0:1, :]), waits=[st_b1.wr])
            tk_rows = v_op(lambda e: e.tensor_copy(r2, st_b2.t[0:1, :]), inc=True)
            st_b1.rd = tk_rows
            st_b2.rd = tk_rows
            wt("tensor", [tk_rows, st_b1.rd])
            em("tensor", lambda e: e.matmul(st_b1.t[:, :], s_i1024[0:1, 0:128], r1,
                                            start=True, stop=True))
            tkb = em("tensor", lambda e: e.matmul(st_b2.t[:, :], s_i1024[0:1, 0:128], r2,
                                                  start=True, stop=True), inc=PE)
            st_b1.wr = tkb
            st_b2.wr = tkb
            v_op(lambda e: e.tensor_copy(m_sb, st_b1.t[:, :]), waits=[tkb])
            v_op(lambda e: e.scalar_tensor_tensor(negm2, st_b1.t[:, :], -1.0, m_sb,
                                                  ALU.mult, ALU.mult))
            tk_var = v_op(lambda e: e.tensor_tensor(negm2, st_b2.t[:, :], negm2, ALU.add),
                          inc=True)
            st_b1.rd = tk_var
            st_b2.rd = tk_var
            a_op(lambda e: e.activation(r_sb, negm2, AF.Ln, bias=s_eps),
                 waits=[tk_var, r_free])
            tk_r = a_op(lambda e: e.activation(r_sb, r_sb, AF.Exp, scale=-0.5), inc=True)
            return tk_r

        yt_store_toks = []
        o1_store_toks = []
        mr_store_toks = []
        cmax_tok = None

        for s in range(NSEQ if LIMIT in ("", "D") else (0 if LIMIT == "W" else 1)):
            pool()
            s_cos = f32(1024)
            s_ssw = f32(1024)
            s_brow = f32(3 * C)
            xb = [f32(NIC * 512) for _ in range(2)]
            xsq = [f32(512) for _ in range(2)]
            m_sb = f32(512); r_sb = f32(512); negm2 = f32(512)
            xlnh = b16(NIC * 512); xlnl = b16(NIC * 512)
            t1b = f32(1024); ub = f32(1024); qp32 = f32(1024)
            sth = b16(1024); stl = b16(1024)
            vstage = f32(H * 65)
            wqh = b16(NIC * C); wql = b16(NIC * C)
            wkh = b16(NIC * C); wkl = b16(NIC * C)
            wvh = b16(NIC * C); wvl = b16(NIC * C)
            a_end = alloc_off[0]

            wspw = [wsp_tok] if s == 0 else []
            def wsp_rows(wi, hl):
                r0 = (wi * 2 + hl) * 128
                return d_wsp[r0:r0 + 128, :]
            tk_wq = [ld_dma(wqh, wsp_rows(0, 0), waits=wspw),
                     ld_dma(wql, wsp_rows(0, 1))]
            tk_wk = [ld_dma(wkh, wsp_rows(1, 0)),
                     ld_dma(wkl, wsp_rows(1, 1))]
            tk_wv = [ld_dma(wvh, wsp_rows(2, 0)),
                     ld_dma(wvl, wsp_rows(2, 1))]
            tk_cos = ld_dma(s_cos, d_cos)
            tk_ssw = ld_dma(s_ssw, d_ssw)
            tk_brow = ld_dma(s_brow[0:1, :], d_brows)

            v_op(lambda e: e.memset(vstage, 1.0))

            x_tok = [None, None]
            x_free = [None, None]

            def load_x(tt, dst_par):
                sap = d_x[s * C:(s + 1) * C, tt * 512:(tt + 1) * 512] \
                    .rearrange("(ic p) j -> p ic j", p=128)
                return ld_dma(xb[dst_par][:, :].rearrange("p (ic j) -> p ic j", ic=8),
                              sap, waits=[x_free[dst_par]] if x_free[dst_par] else [])

            x_tok[0] = load_x(0, 0)

            for tt in range(NTT):
                par = tt % 2
                if tt + 1 < NTT:
                    x_tok[(tt + 1) % 2] = load_x(tt + 1, (tt + 1) % 2)
                chunks = [xb[par][:, ic * 512:(ic + 1) * 512] for ic in range(8)]
                src_toks = [x_tok[par]] * 8
                tk_r = emit_ln(chunks, xsq, m_sb, r_sb, negm2,
                               pbanks[5], pbanks[6], src_toks,
                               r_free=(DV, DV.n))
                if DEBUG and s == 0 and tt == 0:
                    dtk = DV.tok()
                    atk = AC.tok()
                    st_dma(d_dbg[:, 0:4096], xb[par], waits=[x_tok[par]])
                    st_dma(d_dbg[:, 4096:4608], m_sb, waits=[dtk, atk])
                    st_dma(d_dbg[:, 4608:5120], r_sb, waits=[dtk, atk])
                    st_dma(d_dbg[:, 5120:5632], negm2, waits=[dtk, atk])
                if tt > 0:
                    wt("vector", [(PE, PE.n)])
                last_split = None
                for ic in range(8):
                    xc = chunks[ic]
                    xh = xlnh[:, ic * 512:(ic + 1) * 512]
                    xl = xlnl[:, ic * 512:(ic + 1) * 512]
                    v_op(lambda e, o=qp32[:, 0:512], a=xc, b=m_sb:
                         e.tensor_tensor(o, a, b, ALU.subtract),
                         waits=[tk_r] if ic == 0 else [])
                    v_op(lambda e, o=qp32[:, 0:512], a=qp32[:, 0:512], b=r_sb:
                         e.tensor_tensor(o, a, b, ALU.mult))
                    last_split = split_hilo(qp32[:, 0:512], xh, xl,
                                            inc=(ic == 7))
                x_free[par] = last_split
                xln_tok = last_split

                if DEBUG and s == 0 and tt == 0:
                    st_dma(d_dbgb[:, 0:4096], xlnh, waits=[xln_tok])
                    st_dma(d_dbgb[:, 4096:8192], xlnl, waits=[xln_tok])
                    st_dma(d_dbgw[:, 0:8192], wvh, waits=tk_wv)
                    st_dma(d_dbgw[:, 8192:16384], wvl, waits=tk_wv)
                for tc in range(4):
                    tcg = tt * 4 + tc
                    for pi, (wh, wl, wtk, brow_off, is_v) in enumerate([
                            (wqh, wql, tk_wq, 0, False),
                            (wkh, wkl, tk_wk, C, False),
                            (wvh, wvl, tk_wv, 2 * C, True)]):
                        for fh in range(2):
                            bank = pbanks[((tc * 3 + pi) * 2 + fh) % 4]
                            wt("tensor", [bank.rd, xln_tok] + wtk + [tk_brow])
                            nmm = 0
                            for ic in range(8):
                                lh = xlnh[:, ic * 512 + tc * 128: ic * 512 + tc * 128 + 128]
                                ll = xlnl[:, ic * 512 + tc * 128: ic * 512 + tc * 128 + 128]
                                rh = wh[:, ic * 1024 + fh * 512: ic * 1024 + fh * 512 + 512]
                                rl = wl[:, ic * 1024 + fh * 512: ic * 1024 + fh * 512 + 512]
                                for (a_, b_) in ((lh, rh), (lh, rl), (ll, rh)):
                                    em("tensor", lambda e, p=bank.t, A=a_, B=b_, st=(nmm == 0):
                                       e.matmul(p[:, :], A, B, start=st, stop=False))
                                    nmm += 1
                            brow = s_brow[0:1, brow_off + fh * 512: brow_off + fh * 512 + 512]
                            bank.wr = em("tensor", lambda e, p=bank.t, r=brow:
                                         e.matmul(p[:, :], s_ones[0:1, 0:128], r,
                                                  start=False, stop=True), inc=PE)
                            if not is_v:
                                cosv = s_cos[:, tcg * 64:(tcg + 1) * 64] \
                                    .unsqueeze(1).broadcast_to((128, 8, 64))
                                sswv = s_ssw[:, tcg * 64:(tcg + 1) * 64] \
                                    .unsqueeze(1).broadcast_to((128, 8, 64))
                                t1v = t1b[:, fh * 512:(fh + 1) * 512] \
                                    .rearrange("p (h m) -> p h m", h=8)
                                uv = ub[:, fh * 512:(fh + 1) * 512] \
                                    .rearrange("p (h m) -> p h m", h=8)
                                pv = bank.t[:, :].rearrange("p (h m) -> p h m", h=8)
                                v_op(lambda e, o=t1v, a=pv, b=cosv:
                                     e.tensor_tensor(o, a, b, ALU.mult),
                                     waits=[bank.wr, tk_cos, tk_ssw])
                                bank.rd = v_op(lambda e, o=uv, a=pv, b=sswv:
                                               e.tensor_tensor(o, a, b, ALU.mult), inc=True)
                                t4 = t1b[:, fh * 512:(fh + 1) * 512] \
                                    .rearrange("p (h z m) -> p h z m", h=8, z=2)
                                u4 = ub[:, fh * 512:(fh + 1) * 512] \
                                    .rearrange("p (h z m) -> p h z m", h=8, z=2)
                                q4 = qp32[:, fh * 512:(fh + 1) * 512] \
                                    .rearrange("p (h z m) -> p h z m", h=8, z=2)
                                v_op(lambda e, o=q4[:, :, 0, :], a=t4[:, :, 0, :],
                                     b=u4[:, :, 1, :]: e.tensor_tensor(o, a, b, ALU.add))
                                v_op(lambda e, o=q4[:, :, 1, :], a=t4[:, :, 1, :],
                                     b=u4[:, :, 0, :]: e.tensor_tensor(o, a, b, ALU.add))
                            else:
                                dstv = vstage[:, fh * 520: fh * 520 + 520] \
                                    .rearrange("p (h m) -> p h m", h=8)[:, :, 0:64]
                                bank.rd = v_op(lambda e, o=dstv,
                                               a=bank.t[:, :].rearrange("p (h m) -> p h m", h=8):
                                               e.tensor_copy(o, a), waits=[bank.wr], inc=True)
                        if not is_v:
                            tkq = split_hilo(qp32, sth, stl, inc=True)
                            dram = d_qs if pi == 0 else d_ks
                            row0 = tt * 512 + tc * 128
                            tk1 = st_dma(dram[(s * 2 + 0) * T + row0:
                                               (s * 2 + 0) * T + row0 + 128, :],
                                         sth, waits=[tkq])
                            tk2 = st_dma(dram[(s * 2 + 1) * T + row0:
                                               (s * 2 + 1) * T + row0 + 128, :],
                                         stl, waits=[tkq])
                            wt("vector", [tk1, tk2])
                        else:
                            row0 = s * T + tt * 512 + tc * 128
                            tkv = st_dma(d_vs[row0:row0 + 128, :], vstage,
                                         waits=[bank.rd])
                            wt("vector", [tkv])

            if LIMIT == "A":
                break
            pool()
            if DEBUG:
                LAYOUT[f"phaseB_start_s{s}"] = alloc_off[0]
            qf = [[b16(T), b16(T)] for _ in range(2)]
            if DEBUG:
                LAYOUT[f"qf_end_s{s}"] = alloc_off[0]
            kf = [[b16(T), b16(T)] for _ in range(2)]
            vsb = [f32(16 * 130) for _ in range(2)]
            wtp = [f32(2 * 2688) for _ in range(2)]
            tbuf = [f32(512) for _ in range(4)]
            ebuf = [f32(512) for _ in range(4)]
            bcb = f32(512)
            zb = f32(512)
            ysh = [b16(T), b16(T)]
            ysl = [b16(T), b16(T)]
            b_end = alloc_off[0]
            assert b_end <= AW

            def load_hp(hp, par, guards):
                toks = []
                for dram, buf in ((d_qs, qf[par]), (d_ks, kf[par])):
                    for hl in range(2):
                        srcap = dram[(s * 2 + hl) * T:(s * 2 + hl + 1) * T,
                                     hp * 128:(hp + 1) * 128]
                        toks.append(ld_dma_tr(buf[hl], srcap, waits=guards))
                sap = d_vs[s * T:(s + 1) * T, hp * 130:(hp + 1) * 130] \
                    .rearrange("(jt p) c -> p jt c", p=128)
                toks.append(ld_dma(vsb[par][:, :].rearrange("p (jt c) -> p jt c", jt=16),
                                   sap, waits=guards))
                toks.append(ld_dma(wtp[par][:, 0:2688],
                                   d_wt[2 * hp * 128:(2 * hp + 1) * 128, :], waits=guards))
                toks.append(ld_dma(wtp[par][:, 2688:5376],
                                   d_wt[(2 * hp + 1) * 128:(2 * hp + 2) * 128, :],
                                   waits=guards))
                return toks

            wt("sync", [GQ.tok()])
            hp_tok = [None, None]
            hp_done = [(), ()]
            av_pe = [None, None, None, None]
            tb_ac = [None, None, None, None]
            hp_tok[0] = load_hp(0, 0, [])

            for hp in range(8):
                par = hp % 2
                if hp + 1 < 8:
                    hp_tok[(hp + 1) % 2] = load_hp(hp + 1, (hp + 1) % 2,
                                                   list(hp_done[(hp + 1) % 2]))
                wt("tensor", hp_tok[par])
                qh, ql = qf[par]
                kh, kl = kf[par]
                av0, av1 = pbanks[4], pbanks[5]
                for qb in range(4):
                    q0 = qb * 512
                    njt = 4 * qb + 4
                    pend = None

                    def emit_av(entry, njt=None):
                        jt2, items = entry
                        for (hh, ebx, tk_exp, slot) in items:
                            avb = av0 if hh == 0 else av1
                            vsl = vsb[par][:, jt2 * 130 + hh * 65:
                                           jt2 * 130 + hh * 65 + 65]
                            wt("tensor", [tk_exp] + ([avb.rd] if jt2 == 0 else []))
                            tk_av = em("tensor", lambda e, p=avb.t, A=vsl, B=ebx,
                                       st=(jt2 == 0), sp=(jt2 == njt - 1):
                                       e.matmul(p[0:65, :], A, B, start=st, stop=sp),
                                       inc=PE)
                            avb.wr = tk_av
                            av_pe[slot] = tk_av

                    for jt in range(njt):
                        sb0 = pbanks[jt % 2]
                        sb1 = pbanks[2 + jt % 2]
                        wt("tensor", [sb0.rd, sb1.rd])
                        for ip, (a_, b_) in enumerate((("h", "h"), ("h", "l"), ("l", "h"))):
                            ka = kh if a_ == "h" else kl
                            qa = qh if b_ == "h" else ql
                            em("tensor", lambda e, p=sb0.t,
                               A=ka[0:64, jt * 128:(jt + 1) * 128],
                               B=qa[0:64, q0:q0 + 512], st=(ip == 0), sp=(ip == 2):
                               e.matmul(p[:, :], A, B, start=st, stop=sp,
                                        tile_position=(0, 0)))
                            tk_qk = em("tensor", lambda e, p=sb1.t,
                                       A=ka[64:128, jt * 128:(jt + 1) * 128],
                                       B=qa[64:128, q0:q0 + 512], st=(ip == 0), sp=(ip == 2):
                                       e.matmul(p[:, :], A, B, start=st, stop=sp,
                                                tile_position=(64, 0)),
                                       inc=PE if ip == 2 else None)
                        sb0.wr = tk_qk
                        sb1.wr = tk_qk
                        off = (q0 - 128 * jt) + 511
                        items = []
                        for hh, sb in ((0, sb0), (1, sb1)):
                            slot = 2 * (jt % 2) + hh
                            tb = tbuf[slot]
                            wslice = wtp[par][:, hh * 2688 + off: hh * 2688 + off + 512]
                            tk_stt = v_op(lambda e, o=tb, p=sb.t, w=wslice:
                                          e.scalar_tensor_tensor(o, p[:, :], 1.0, w,
                                                                 ALU.mult, ALU.add),
                                          waits=[sb.wr, tb_ac[slot]]
                                          + (hp_tok[par] if jt == 0 else []),
                                          inc=True)
                            sb.rd = tk_stt
                            ebx = ebuf[slot]
                            tk_exp = a_op(lambda e, o=ebx, i=tb:
                                          e.activation(o, i, AF.Exp, scale=0.125),
                                          waits=[tk_stt, av_pe[slot]],
                                          inc=True)
                            tb_ac[slot] = tk_exp
                            items.append((hh, ebx, tk_exp, slot))
                        if pend is not None:
                            emit_av(pend, njt)
                        pend = (jt, items)
                    emit_av(pend, njt)
                    for hh, avb in ((0, av0), (1, av1)):
                        srow = s_rows[64:65, hh * 512: hh * 512 + 512]
                        tk_s = v_op(lambda e, o=srow, p=avb.t:
                                    e.tensor_copy(o, p[64:65, :]), waits=[avb.wr], inc=True)
                        a_op(lambda e, o=srow, i=srow: e.activation(o, i, AF.Ln),
                             waits=[tk_s])
                        tk_rr = a_op(lambda e, o=srow, i=srow:
                                     e.activation(o, i, AF.Exp, scale=-1.0), inc=True)
                        ub6 = pbanks[6]
                        wt("tensor", [tk_rr, ub6.rd])
                        ub6.wr = em("tensor", lambda e, p=ub6.t, r=srow:
                                    e.matmul(p[:, :], s_ones[64:65, 0:128], r,
                                             start=True, stop=True), inc=PE)
                        v_op(lambda e, o=bcb, p=ub6.t: e.tensor_copy(o, p[:, :]),
                             waits=[ub6.wr])
                        ub6.rd = v_op(lambda e, o=zb[0:64, :], a=avb.t[0:64, :], b=bcb[0:64, :]:
                                      e.tensor_tensor(o, a, b, ALU.mult), inc=True)
                        avb.rd = ub6.rd
                        ysh_t = ysh[hh][0:64, q0:q0 + 512]
                        ysl_t = ysl[hh][0:64, q0:q0 + 512]
                        v_op(lambda e, o=ysh_t, i=zb[0:64, :]: e.tensor_copy(o, i))
                        v_op(lambda e, o=ysl_t, a=zb[0:64, :], b=ysh_t:
                             e.tensor_tensor(o, a, b, ALU.subtract), inc=True)
                hp_done[par] = ((PE, PE.n), (DV, DV.n), (AC, AC.n))
                tk_y = DV.tok()
                for hh in range(2):
                    rows0 = hp * 128 + hh * 64
                    yh_d = d_yt[(s * 2 + 0) * C + rows0:(s * 2 + 0) * C + rows0 + 64, :]
                    yl_d = d_yt[(s * 2 + 1) * C + rows0:(s * 2 + 1) * C + rows0 + 64, :]
                    yt_store_toks.append(st_dma(yh_d, ysh[hh][0:64, :], waits=[tk_y]))
                    yt_store_toks.append(st_dma(yl_d, ysl[hh][0:64, :], waits=[tk_y]))
                wt("vector", [yt_store_toks[-1]])

            if LIMIT == "B":
                break
            pool()
            wph = b16(NIC * C); wpl = b16(NIC * C)
            ytb = [[b16(NIC * 512), b16(NIC * 512)] for _ in range(2)]
            aob = f32(NIC * 512)
            xcb = [f32(NIC * 512) for _ in range(2)]
            o1b = f32(NIC * 512)
            xsqc = [f32(512) for _ in range(2)]
            m_sbc = f32(512); r_sbc = f32(512); negm2c = f32(512)
            zt = f32(512)
            gmaxcol = s_gmax[:, 0:1]
            c_end = alloc_off[0]

            tk_wph = ld_dma(wph, d_wsp[(3 * 2 + 0) * 128:(3 * 2 + 1) * 128, :],
                            waits=[wsp_tok] if s == 0 else [])
            tk_wpl = ld_dma(wpl, d_wsp[(3 * 2 + 1) * 128:(3 * 2 + 2) * 128, :])
            if s == 0:
                v_op(lambda e: e.memset(gmaxcol, -1.0e38))
            wt("sync", [GQ.tok()])

            yt_tok = [None, None]
            pe_after_tt = [None, None]

            def load_yt(tt, p2, guards):
                toks = []
                for hl in range(2):
                    srcap = d_yt[(s * 2 + hl) * C:(s * 2 + hl + 1) * C,
                                 tt * 512:(tt + 1) * 512] \
                        .rearrange("(ic p) j -> p ic j", p=128)
                    toks.append(ld_dma(ytb[p2][hl][:, :].rearrange("p (ic j) -> p ic j", ic=8),
                                       srcap, waits=guards))
                return toks

            def load_xc(tt, p2, guards):
                return ld_dma(xcb[p2][:, :].rearrange("p (ic j) -> p ic j", ic=8),
                              d_x[s * C:(s + 1) * C, tt * 512:(tt + 1) * 512]
                              .rearrange("(ic p) j -> p ic j", p=128), waits=guards)

            yt_tok[0] = load_yt(0, 0, [])
            xc_tok = [None, None]
            xc_free = [None, None]
            xc_tok[0] = load_xc(0, 0, [])

            for tt in range(NTT):
                par = tt % 2
                if tt + 1 < NTT:
                    gpar = (tt + 1) % 2
                    yt_tok[gpar] = load_yt(tt + 1, gpar,
                                           [pe_after_tt[gpar]] if pe_after_tt[gpar] else [])
                    xc_tok[gpar] = load_xc(tt + 1, gpar,
                                           [xc_free[gpar]] if xc_free[gpar] else [])
                ao_toks = []
                for co in range(8):
                    bank = pbanks[co % 4]
                    wt("tensor", [bank.rd, tk_wph, tk_wpl] + yt_tok[par])
                    nmm = 0
                    for ic in range(8):
                        for (A, B) in ((wph, ytb[par][0]), (wph, ytb[par][1]),
                                       (wpl, ytb[par][0])):
                            last = nmm == 23
                            tkmm = em("tensor", lambda e, p=bank.t,
                                      a=A[:, ic * 1024 + co * 128: ic * 1024 + co * 128 + 128],
                                      b=B[:, ic * 512:(ic + 1) * 512],
                                      st=(nmm == 0), sp=last:
                                      e.matmul(p[:, :], a, b, start=st, stop=sp),
                                      inc=PE if last else None)
                            nmm += 1
                    bank.wr = tkmm
                    tk_ao = v_op(lambda e, o=aob[:, co * 512:(co + 1) * 512], p=bank.t,
                                 b=s_bp[:, co:co + 1]:
                                 e.tensor_scalar(o, p[:, :], b, None, ALU.add, ALU.bypass),
                                 waits=[bank.wr], inc=True)
                    bank.rd = tk_ao
                    ao_toks.append(tk_ao)
                chunks = [aob[:, ic * 512:(ic + 1) * 512] for ic in range(8)]
                tk_r = emit_ln(chunks, xsqc, m_sbc, r_sbc, negm2c,
                               pbanks[5], pbanks[6], ao_toks, r_free=(DV, DV.n))
                o1_toks = []
                for ic in range(8):
                    oc = o1b[:, ic * 512:(ic + 1) * 512]
                    v_op(lambda e, o=zt, a=chunks[ic], b=m_sbc:
                         e.tensor_tensor(o, a, b, ALU.subtract),
                         waits=[tk_r] if ic == 0 else [])
                    v_op(lambda e, o=zt, a=zt, b=r_sbc:
                         e.tensor_tensor(o, a, b, ALU.mult))
                    tko = v_op(lambda e, o=oc, a=zt, b=xcb[par][:, ic * 512:(ic + 1) * 512]:
                               e.tensor_tensor(o, a, b, ALU.add),
                               waits=[xc_tok[par]], inc=True)
                    o1_toks.append(tko)
                xc_free[par] = o1_toks[-1]
                tk_o1 = st_dma(d_o1[s * C:(s + 1) * C, tt * 512:(tt + 1) * 512]
                               .rearrange("(ic p) j -> p ic j", p=128),
                               o1b[:, :].rearrange("p (ic j) -> p ic j", ic=8),
                               waits=[o1_toks[-1]])
                o1_store_toks.append(tk_o1)
                chunks2 = [o1b[:, ic * 512:(ic + 1) * 512] for ic in range(8)]
                tk_r2 = emit_ln(chunks2, xsqc, m_sbc, r_sbc, negm2c,
                                pbanks[5], pbanks[6], o1_toks, r_free=(DV, DV.n))
                row = s * NTT + tt
                tk_mr = st_dma(d_mr[row:row + 1, 0:512], m_sbc[0:1, :], waits=[tk_r2])
                tk_mr2 = st_dma(d_mr[row:row + 1, 512:1024], r_sbc[0:1, :], waits=[tk_r2])
                mr_store_toks += [tk_mr, tk_mr2]
                for ic in range(8):
                    v_op(lambda e, o=zt, a=chunks2[ic], b=m_sbc:
                         e.tensor_tensor(o, a, b, ALU.subtract),
                         waits=[tk_r2] if ic == 0 else [])
                    v_op(lambda e, o=zt, a=zt, b=r_sbc:
                         e.tensor_tensor(o, a, b, ALU.mult))
                    v_op(lambda e, o=zt[:, 0:1], i=zt:
                         e.tensor_reduce(o, i, AXL.X, ALU.max), drain=True)
                    tk_m = v_op(lambda e, o=gmaxcol, a=gmaxcol, b=zt[:, 0:1]:
                                e.tensor_tensor(o, a, b, ALU.max),
                                inc=(ic == 7), drain=True)
                cmax_tok = tk_m
                pe_after_tt[par] = (PE, PE.n)
                wt("vector", [tk_o1])

        if LIMIT and LIMIT != "D":
            pool()
            if LIMIT == "C":
                st_dma(d_out[:, :], d_o1[:, :], waits=[(GQ, GQ.n), (DV, DV.n)])
            elif LIMIT == "B":
                st_dma(d_out[0:1024, :].bitcast(BF16)[:, 0:2048], d_yt[0:1024, :],
                       waits=[(GQ, GQ.n), (DV, DV.n)])
                st_dma(d_out[1024:2048, :].bitcast(BF16)[:, 0:2048], d_yt[2048:3072, :])
            elif LIMIT == "A":
                st_dma(d_out[0:1024, 0:512].bitcast(BF16), d_qs[0:1024, :],
                       waits=[(GQ, GQ.n), (DV, DV.n)])
                st_dma(d_out[0:1024, 512:1024].bitcast(BF16), d_vs[0:1024, 0:1024],
                       waits=[])
                st_dma(d_out[1024:2048, 0:512].bitcast(BF16), d_qs[2048:3072, :])
                st_dma(d_out[1024:2048, 512:1024].bitcast(BF16), d_vs[2048:3072, 0:1024])
            wt("gpsimd", [GQ.tok()])
            wt("sync", [LD.tok()])
            with nc.Block() as block:
                @block.sync
                def _(e):
                    for fn in prog.streams["sync"]:
                        fn(e)

                @block.vector
                def _(e):
                    for fn in prog.streams["vector"]:
                        fn(e)

                @block.scalar
                def _(e):
                    for fn in prog.streams["scalar"]:
                        fn(e)

                @block.tensor
                def _(e):
                    for fn in prog.streams["tensor"]:
                        fn(e)

                @block.gpsimd
                def _(e):
                    for fn in prog.streams["gpsimd"]:
                        fn(e)
            return nc
        def emit_barrier(gcol, cci, cco, dst_pair, prev_tok, ccs):
            if not hasattr(emit_barrier, "park"):
                emit_barrier.park = 0
                emit_barrier.park2 = 0
            """gcol [128,1] local max -> AllReduce max -> dst_pair [128, 2] (s, 1/s)."""
            ub = pbanks[6]
            wt("tensor", [prev_tok, ub.rd, tk_idf])
            ub.wr = em("tensor", lambda e, g=gcol: e.transpose(ub.t[0:1, 0:128], g, s_idf),
                       inc=PE)
            row = s_rows[0:1, 0:128]
            tk1 = v_op(lambda e, o=row[0:1, 0:1], i=ub.t[0:1, 0:128]:
                       e.tensor_reduce(o, i, AXL.X, ALU.max), waits=[ub.wr], inc=True, drain=True)
            ub.rd = tk1
            if HWDBG:
                pk = emit_barrier.park
                v_op(lambda e, o=s_park[0:1, pk:pk + 1], i=row[0:1, 0:1]:
                     e.tensor_copy(o, i))
                v_op(lambda e, o=s_park[0:1, 4 + pk * 2: 6 + pk * 2],
                     i=ub.t[0:1, 0:2]: e.tensor_copy(o, i))
                emit_barrier.park += 1
            tkc = v_op(lambda e, o=s_cc[0:1, :], i=row[0:1, 0:1].broadcast_to((1, 8)):
                       e.tensor_copy(o, i), inc=True, drain=True)
            if HWDBG:
                pk2 = emit_barrier.park2
                v_op(lambda e, o=s_park[0:1, 2 + pk2:3 + pk2], i=s_cc[0:1, 0:1]:
                     e.tensor_copy(o, i))
                emit_barrier.park2 += 1
            tkw = st_dma(cci.ap(), s_cc[0:1, :], waits=[tkc])
            wt("gpsimd", [tkw])
            tk_cc = em("gpsimd", lambda e: e.collective_compute(
                "AllReduce", ALU.max, replica_groups=[list(range(NCORES))],
                ins=[cci.ap()], outs=[cco.ap()]), inc=ccs)
            wt("gpsimd", [tk_cc])
            tk_rd = em("gpsimd", lambda e: e.dma_start(out=s_cc[0:1, :], in_=cco.ap()),
                       inc=GQ, inc_amt=16)
            r0 = s_rows[0:1, 0:1]
            r1 = s_rows[0:1, 1:2]
            r2 = s_rows[0:1, 2:4]
            v_op(lambda e: e.tensor_scalar(r0, s_cc[0:1, 0:1], EPS_Q, 1.0 / 255.0,
                                           ALU.max, ALU.mult), waits=[tk_rd], drain=True)
            v_op(lambda e: e.tensor_scalar(r1, s_cc[0:1, 0:1], EPS_Q, None,
                                           ALU.max, ALU.bypass), drain=True)
            v_op(lambda e: e.reciprocal(r1, r1), drain=True)
            v_op(lambda e: e.tensor_scalar(r1, r1, 255.0, None, ALU.mult, ALU.bypass),
                 drain=True)
            v_op(lambda e: e.tensor_copy(r2[0:1, 0:1], r0), drain=True)
            tk2 = v_op(lambda e: e.tensor_copy(r2[0:1, 1:2], r1), inc=True, drain=True)
            wt("tensor", [tk2, ub.rd])
            ub.wr = em("tensor", lambda e: e.matmul(ub.t[:, 0:2], s_ones[0:1, 0:128], r2,
                                                    start=True, stop=True), inc=PE)
            tk3 = v_op(lambda e, o=dst_pair: e.tensor_copy(o, ub.t[:, 0:2]),
                       waits=[ub.wr], inc=True)
            ub.rd = tk3
            return tk3

        tk_s1 = emit_barrier(s_gmax[:, 0:1], d_cci, d_cco, s_s1r, cmax_tok, CCS1)

        pool()
        w1qT = b16(NIC * FF)
        xq2 = b16(NIC * NSEQ * T)
        wld = [f32(1024), f32(1024)]
        wqi = b16(1024)
        o1l = [f32(NIC * 512) for _ in range(2)]
        m2b = f32(512); r2b = f32(512)
        mrl = [f32(1024), f32(1024)]
        zq = f32(512)
        hst = [f32(512) for _ in range(2)]
        hst_free = [(), ()]
        d_end = alloc_off[0]
        assert d_end <= AW, d_end

        w1fc_tok = [None] * NFC
        wl_tok = [None, None]
        wl_free = [None, None]
        wl_tok[0] = ld_dma(wld[0], d_w1[0:128, :])
        for fc in range(NFC):
            par = fc % 2
            if fc + 1 < NFC:
                wl_tok[(fc + 1) % 2] = ld_dma(wld[(fc + 1) % 2],
                                              d_w1[(fc + 1) * 128:(fc + 2) * 128, :],
                                              waits=[wl_free[(fc + 1) % 2]]
                                              if wl_free[(fc + 1) % 2] else [])
            amax = s_rows[:, 0:1]
            v_op(lambda e, o=amax, i=wld[par]:
                 e.tensor_reduce(o, i, AXL.X, ALU.max, apply_absolute_value=True),
                 waits=[wl_tok[par], tk_b1], drain=True)
            v_op(lambda e, o=s_sw1[:, fc:fc + 1], i=amax:
                 e.tensor_scalar(o, i, EPS_Q, (1.0 / 127.0), ALU.max, ALU.mult),
                 drain=True)
            v_op(lambda e, o=s_rw1[:, fc:fc + 1], i=amax:
                 e.tensor_scalar(o, i, EPS_Q, None, ALU.max, ALU.bypass), drain=True)
            v_op(lambda e, o=s_rw1[:, fc:fc + 1], i=s_rw1[:, fc:fc + 1]:
                 e.reciprocal(o, i), drain=True)
            v_op(lambda e, o=wld[par], i=wld[par], r=s_rw1[:, fc:fc + 1]:
                 e.tensor_scalar(o, i, r, 127.0, ALU.mult, ALU.mult))
            v_op(lambda e, o=wld[par], i=wld[par]:
                 e.tensor_scalar(o, i, MAGIC, MAGIC, ALU.add, ALU.subtract))
            v_op(lambda e, o=wld[par], i=wld[par]:
                 e.tensor_scalar(o, i, 127.0, -128.0, ALU.min, ALU.max))
            tk_q = v_op(lambda e, o=wqi, i=wld[par]: e.tensor_copy(o, i),
                        waits=[(PE, PE.n)], inc=True)
            wl_free[par] = tk_q
            v_op(lambda e, o=s_sc1[:, fc:fc + 1], a=s_sw1[:, fc:fc + 1], b=s_s1r[:, 0:1]:
                 e.tensor_tensor(o, a, b, ALU.mult), waits=[tk_s1], drain=True)
            v_op(lambda e, o=s_rows[:, 1:2], i=s_sc1[:, fc:fc + 1]: e.reciprocal(o, i), drain=True)
            v_op(lambda e, o=s_rows[:, 2:3], a=s_b1[:, fc:fc + 1], b=s_rows[:, 1:2]:
                 e.tensor_tensor(o, a, b, ALU.mult), drain=True)
            v_op(lambda e, o=s_rows[:, 2:3], i=s_rows[:, 2:3]:
                 e.tensor_scalar(o, i, MAGIC, MAGIC, ALU.add, ALU.subtract), drain=True)
            v_op(lambda e, o=s_sb1[:, fc:fc + 1], a=s_rows[:, 2:3], b=s_sc1[:, fc:fc + 1]:
                 e.tensor_tensor(o, a, b, ALU.mult), drain=True)
            for g in range(2):
                wt("tensor", [tk_q, psbf.rd, tk_idb])
                for q4 in range(4):
                    ic = g * 4 + q4
                    tktr = em("tensor", lambda e, i=wqi[:, ic * 128:(ic + 1) * 128],
                              o=psbf.t[:, q4 * 128:(q4 + 1) * 128]:
                              e.transpose(o, i, s_idb), inc=PE if q4 == 3 else None)
                psbf.wr = tktr
                tk_e = None
                for q4 in range(4):
                    ic = g * 4 + q4
                    tk_e = v_op(lambda e, o=w1qT[:, ic * FF + fc * 128: ic * FF + fc * 128 + 128],
                                i=psbf.t[:, q4 * 128:(q4 + 1) * 128]:
                                e.tensor_copy(o, i),
                                waits=[psbf.wr] if q4 == 0 else [], inc=(q4 == 3))
                psbf.rd = tk_e
            w1fc_tok[fc] = tk_e
        w1q_tok = DV.tok()

        o1_tok = [None, None]
        o1_free = [None, None]

        def load_o1(si, tti, p2):
            return ld_dma(o1l[p2][:, :].rearrange("p (ic j) -> p ic j", ic=8),
                          d_o1[si * C:(si + 1) * C, tti * 512:(tti + 1) * 512]
                          .rearrange("(ic p) j -> p ic j", p=128),
                          waits=[o1_free[p2]] if o1_free[p2] else [])

        wt("sync", [GQ.tok()])
        o1_tok[0] = load_o1(0, 0, 0)
        tiles = [(si, tti) for si in range(NSEQ) for tti in range(NTT)]
        xq2_tile_tok = [None] * len(tiles)

        def emit_xq2(i):
            si, tti = tiles[i]
            par = i % 2
            if i + 1 < len(tiles):
                o1_tok[(i + 1) % 2] = load_o1(*tiles[i + 1], (i + 1) % 2)
            rrow = si * NTT + tti
            mr_tok = ld_dma(mrl[par][0:1, :], d_mr[rrow:rrow + 1, :],
                            waits=[pbanks[6].wr] if i >= 2 else [])
            ub = pbanks[6]
            wt("tensor", [mr_tok, ub.rd])
            em("tensor", lambda e, m=mrl[par]: e.matmul(ub.t[:, :], s_ones[0:1, 0:128],
                                            m[0:1, 0:512], start=True, stop=True))
            ub2 = pbanks[5]
            wt("tensor", [mr_tok, ub2.rd])
            tk_bc = em("tensor", lambda e, m=mrl[par]: e.matmul(ub2.t[:, :], s_ones[0:1, 0:128],
                                                    m[0:1, 512:1024], start=True, stop=True),
                       inc=PE)
            ub.wr = tk_bc
            ub2.wr = tk_bc
            v_op(lambda e: e.tensor_copy(m2b, ub.t[:, :]), waits=[tk_bc])
            tk_m2 = v_op(lambda e: e.tensor_copy(r2b, ub2.t[:, :]), inc=True)
            ub.rd = tk_m2
            ub2.rd = tk_m2
            tk_xq = None
            for ic in range(8):
                col = ic * (NSEQ * T) + si * T + tti * 512
                v_op(lambda e, o=zq, a=o1l[par][:, ic * 512:(ic + 1) * 512], b=m2b:
                     e.tensor_tensor(o, a, b, ALU.subtract),
                     waits=[o1_tok[par], tk_m2] if ic == 0 else [])
                v_op(lambda e, o=zq, a=zq, b=r2b:
                     e.tensor_tensor(o, a, b, ALU.mult))
                v_op(lambda e, o=zq, i=zq, r=s_s1r[:, 1:2]:
                     e.tensor_scalar(o, i, r, MAGIC, ALU.mult, ALU.add))
                v_op(lambda e, o=zq, i=zq:
                     e.tensor_scalar(o, i, MAGIC, 255.0, ALU.subtract, ALU.min))
                tk_xq = v_op(lambda e, o=xq2[:, col:col + 512], i=zq:
                             e.tensor_scalar(o, i, 0.0, None, ALU.max, ALU.bypass),
                             inc=(ic == 7))
            o1_free[par] = tk_xq
            xq2_tile_tok[i] = tk_xq

        emit_xq2(0)

        v_op(lambda e: e.memset(s_gmax[:, 1:2], 0.0))
        dmax_tok = None
        for tt in range(NSEQ * NTT):
            if tt + 1 < NSEQ * NTT:
                emit_xq2(tt + 1)
            for fc in range(NFC):
                bank = pbanks[(tt * NFC + fc) % 4]
                wt("tensor", [bank.rd, xq2_tile_tok[tt], w1fc_tok[fc]])
                for ic in range(8):
                    tkmm = em("tensor", lambda e, p=bank.t,
                              A=w1qT[:, ic * FF + fc * 128: ic * FF + fc * 128 + 128],
                              B=xq2[:, ic * (NSEQ * T) + tt * 512:
                                    ic * (NSEQ * T) + tt * 512 + 512],
                              st=(ic == 0), sp=(ic == 7):
                              e.matmul(p[:, :], A, B, start=st, stop=sp),
                              inc=PE if ic == 7 else None)
                bank.wr = tkmm
                par = (tt * NFC + fc) % 2
                tk_h = a_op(lambda e, o=hst[par], p=bank.t, sc=s_sc1[:, fc:fc + 1],
                            b=s_sb1[:, fc:fc + 1]:
                            e.activation(o, p[:, :], AF.Relu, bias=b, scale=sc),
                            waits=[bank.wr, w1q_tok, tk_s1] + list(hst_free[par]),
                            inc=True)
                bank.rd = tk_h
                v_op(lambda e, o=s_rows[:, 4:5], i=hst[par]:
                     e.tensor_reduce(o, i, AXL.X, ALU.max), waits=[tk_h], drain=True)
                dmax_tok = v_op(lambda e, o=s_gmax[:, 1:2], a=s_gmax[:, 1:2],
                                b=s_rows[:, 4:5]:
                                e.tensor_tensor(o, a, b, ALU.max), inc=True, drain=True)
                tk_st = st_dma(d_h[fc * 128:(fc + 1) * 128, tt * 512:(tt + 1) * 512],
                               hst[par], waits=[tk_h])
                hst_free[par] = (tk_st, dmax_tok)

        tk_s2 = emit_barrier(s_gmax[:, 1:2], d_cci2, d_cco2, s_s2r, dmax_tok, CCS2)

        if LIMIT == "D":
            pool()
            st_dma(d_out[:, :], d_h[0:2048, 0:2048], waits=[(GQ, GQ.n), (DV, DV.n)])
            _t = st_dma(d_out[0:1, 0:2].bitcast(F32), None, waits=[]) if False else None
            st_dma(d_out[2047:2048, 0:2], s_s1r[0:1, 0:2], waits=[tk_s2])
        pool()
        w2qT = b16(NFC * C)
        hq = b16(NFC * 512)
        w2ld = [f32(FF // 2), f32(FF // 2)]
        w2qi = b16(FF)
        o2b = f32(NIC * 512)
        o1e = [f32(NIC * 512) for _ in range(2)]
        hqe = [f32(512) for _ in range(2)]
        xsqe = [f32(512) for _ in range(2)]
        m_e = f32(512); r_e = f32(512); negm2e = f32(512)
        zte = f32(512)
        amax2 = f32(2)
        e_end = alloc_off[0]
        assert e_end <= AW, e_end

        w2_free = None
        w2cc_tok = [None] * NIC
        for cc in range(NIC if LIMIT != "D" else 0):
            tk_l0 = ld_dma(w2ld[0], d_w2[cc * 128:(cc + 1) * 128, 0:2048],
                           waits=[w2_free] if w2_free else [])
            tk_l1 = ld_dma(w2ld[1], d_w2[cc * 128:(cc + 1) * 128, 2048:4096],
                           waits=[w2_free] if w2_free else [])
            v_op(lambda e, o=amax2[:, 0:1], i=w2ld[0]:
                 e.tensor_reduce(o, i, AXL.X, ALU.max, apply_absolute_value=True),
                 waits=[tk_l0, tk_b2], drain=True)
            v_op(lambda e, o=amax2[:, 1:2], i=w2ld[1]:
                 e.tensor_reduce(o, i, AXL.X, ALU.max, apply_absolute_value=True),
                 waits=[tk_l1], drain=True)
            amax = s_rows[:, 0:1]
            v_op(lambda e, o=amax, a=amax2[:, 0:1], b=amax2[:, 1:2]:
                 e.tensor_tensor(o, a, b, ALU.max), drain=True)
            v_op(lambda e, o=s_sw2[:, cc:cc + 1], i=amax:
                 e.tensor_scalar(o, i, EPS_Q, (1.0 / 127.0), ALU.max, ALU.mult),
                 drain=True)
            v_op(lambda e, o=s_rw2[:, cc:cc + 1], i=amax:
                 e.tensor_scalar(o, i, EPS_Q, None, ALU.max, ALU.bypass), drain=True)
            v_op(lambda e, o=s_rw2[:, cc:cc + 1], i=s_rw2[:, cc:cc + 1]:
                 e.reciprocal(o, i), drain=True)
            tk_q = None
            for hq2 in range(2):
                wbuf = w2ld[hq2]
                v_op(lambda e, o=wbuf, i=wbuf, r=s_rw2[:, cc:cc + 1]:
                     e.tensor_scalar(o, i, r, 127.0, ALU.mult, ALU.mult))
                v_op(lambda e, o=wbuf, i=wbuf:
                     e.tensor_scalar(o, i, MAGIC, MAGIC, ALU.add, ALU.subtract))
                v_op(lambda e, o=wbuf, i=wbuf:
                     e.tensor_scalar(o, i, 127.0, -128.0, ALU.min, ALU.max))
                tk_q = v_op(lambda e, o=w2qi[:, hq2 * 2048:(hq2 + 1) * 2048], i=wbuf:
                            e.tensor_copy(o, i), waits=[(PE, PE.n)] if hq2 == 0 else [],
                            inc=(hq2 == 1))
            w2_free = tk_q
            v_op(lambda e, o=s_sc2[:, cc:cc + 1], a=s_sw2[:, cc:cc + 1], b=s_s2r[:, 0:1]:
                 e.tensor_tensor(o, a, b, ALU.mult), waits=[tk_s2], drain=True)
            v_op(lambda e, o=s_rows[:, 1:2], i=s_sc2[:, cc:cc + 1]: e.reciprocal(o, i), drain=True)
            v_op(lambda e, o=s_rows[:, 2:3], a=s_b2[:, cc:cc + 1], b=s_rows[:, 1:2]:
                 e.tensor_tensor(o, a, b, ALU.mult), drain=True)
            v_op(lambda e, o=s_rows[:, 2:3], i=s_rows[:, 2:3]:
                 e.tensor_scalar(o, i, MAGIC, MAGIC, ALU.add, ALU.subtract), drain=True)
            v_op(lambda e, o=s_sb2[:, cc:cc + 1], a=s_rows[:, 2:3], b=s_sc2[:, cc:cc + 1]:
                 e.tensor_tensor(o, a, b, ALU.mult), drain=True)
            for g in range(8):
                wt("tensor", [tk_q, psbf.rd])
                for q4 in range(4):
                    fc = g * 4 + q4
                    tktr = em("tensor", lambda e, i=w2qi[:, fc * 128:(fc + 1) * 128],
                              o=psbf.t[:, q4 * 128:(q4 + 1) * 128]:
                              e.transpose(o, i, s_idb), inc=PE if q4 == 3 else None)
                psbf.wr = tktr
                tk_e = None
                for q4 in range(4):
                    fc = g * 4 + q4
                    tk_e = v_op(lambda e, o=w2qT[:, fc * C + cc * 128: fc * C + cc * 128 + 128],
                                i=psbf.t[:, q4 * 128:(q4 + 1) * 128]:
                                e.tensor_copy(o, i),
                                waits=[psbf.wr] if q4 == 0 else [], inc=(q4 == 3))
                psbf.rd = tk_e
            w2cc_tok[cc] = tk_e
        w2q_tok = DV.tok()

        if DEBUG:
            st_dma(d_dbgw[:, 0:16384], w2qT[:, 0:16384], waits=[w2q_tok])
            st_dma(d_dbgb[:, 0:8192], w2qT[:, 16384:24576], waits=[w2q_tok])
        h_tok = [None, None]
        h_free = [None, None]
        oe_tok = [None, None]
        oe_free = [None, None]

        def load_h(tt, fc, p2):
            return ld_dma(hqe[p2], d_h[fc * 128:(fc + 1) * 128, tt * 512:(tt + 1) * 512],
                          waits=[h_free[p2]] if h_free[p2] else [])

        def load_o1e(i, p2):
            si, tti = divmod(i, NTT)
            return ld_dma(o1e[p2][:, :].rearrange("p (ic j) -> p ic j", ic=8),
                          d_o1[si * C:(si + 1) * C, tti * 512:(tti + 1) * 512]
                          .rearrange("(ic p) j -> p ic j", p=128),
                          waits=[oe_free[p2]] if oe_free[p2] else [])

        wt("sync", [GQ.tok()])
        h_tok[0] = load_h(0, 0, 0)
        oe_tok[0] = load_o1e(0, 0)
        out_toks = []
        ffn2_pe = None
        for tt in range(NSEQ * NTT if LIMIT != "D" else 0):
            if ffn2_pe is not None:
                wt("vector", [ffn2_pe])
            for fc in range(NFC):
                par = fc % 2
                nxt = tt * NFC + fc + 1
                if nxt < NSEQ * NTT * NFC:
                    h_tok[nxt % 2] = load_h(nxt // NFC, nxt % NFC, nxt % 2)
                v_op(lambda e, o=zte, i=hqe[par], r=s_s2r[:, 1:2]:
                     e.tensor_scalar(o, i, r, MAGIC, ALU.mult, ALU.add),
                     waits=[h_tok[par], tk_s2])
                v_op(lambda e, o=zte, i=zte:
                     e.tensor_scalar(o, i, MAGIC, 255.0, ALU.subtract, ALU.min))
                tk_hq = v_op(lambda e, o=hq[:, fc * 512:(fc + 1) * 512], i=zte:
                             e.tensor_scalar(o, i, 0.0, None, ALU.max, ALU.bypass),
                             inc=True)
                h_free[par] = tk_hq
            hq_tok = tk_hq
            if tt + 1 < NSEQ * NTT:
                oe_tok[(tt + 1) % 2] = load_o1e(tt + 1, (tt + 1) % 2)
            par_o = tt % 2
            o2_toks = []
            for co in range(NIC):
                bank = pbanks[co % 4]
                wt("tensor", [bank.rd, hq_tok, w2cc_tok[co]])
                for fc in range(NFC):
                    tkmm = em("tensor", lambda e, p=bank.t,
                              A=w2qT[:, fc * C + co * 128: fc * C + co * 128 + 128],
                              B=hq[:, fc * 512:(fc + 1) * 512],
                              st=(fc == 0), sp=(fc == NFC - 1):
                              e.matmul(p[:, :], A, B, start=st, stop=sp),
                              inc=PE if fc == NFC - 1 else None)
                bank.wr = tkmm
                tk_o2 = v_op(lambda e, o=o2b[:, co * 512:(co + 1) * 512], p=bank.t,
                             sc=s_sc2[:, co:co + 1], b=s_sb2[:, co:co + 1]:
                             e.tensor_scalar(o, p[:, :], sc, b, ALU.mult, ALU.add),
                             waits=[bank.wr], inc=True)
                bank.rd = tk_o2
                o2_toks.append(tk_o2)
            if DEBUG and tt == 0:
                _dt1 = st_dma(d_dbg[:, 0:4096], o2b, waits=[o2_toks[-1]])
                _dt2 = st_dma(d_dbgb2[:, 0:16384], hq, waits=[hq_tok])
                wt("vector", [_dt1, _dt2])
            chunks = [o2b[:, ic * 512:(ic + 1) * 512] for ic in range(8)]
            tk_r = emit_ln(chunks, xsqe, m_e, r_e, negm2e,
                           pbanks[5], pbanks[6], o2_toks, r_free=(DV, DV.n))
            for ic in range(8):
                oc = o2b[:, ic * 512:(ic + 1) * 512]
                v_op(lambda e, o=zte, a=chunks[ic], b=m_e:
                     e.tensor_tensor(o, a, b, ALU.subtract),
                     waits=[tk_r, oe_tok[par_o]] if ic == 0 else [])
                v_op(lambda e, o=zte, a=zte, b=r_e:
                     e.tensor_tensor(o, a, b, ALU.mult))
                tk_f = v_op(lambda e, o=oc, a=zte, b=o1e[par_o][:, ic * 512:(ic + 1) * 512]:
                            e.tensor_tensor(o, a, b, ALU.add), inc=(ic == 7))
            oe_free[par_o] = tk_f
            si, tti = divmod(tt, NTT)
            tk_out = st_dma(d_out[si * C:(si + 1) * C, tti * 512:(tti + 1) * 512]
                            .rearrange("(ic p) j -> p ic j", p=128),
                            o2b[:, :].rearrange("p (ic j) -> p ic j", ic=8),
                            waits=[tk_f])
            out_toks.append(tk_out)
            wt("vector", [tk_out])
            ffn2_pe = (PE, PE.n)

        if HWDBG:
            wt("gpsimd", [GQ.tok(), (DV, DV.n), (PE, PE.n), (AC, AC.n)])
            st_dma(dbg_o1, d_o1[:, :])
            st_dma(dbg_y, d_yt[:, :])
            st_dma(dbg_h, d_h[:, :])
            st_dma(dbg_q, d_qs[:, :])
            st_dma(dbg_s[0:1, 0:2], s_s1r[0:1, 0:2])
            st_dma(dbg_s[0:1, 2:4], s_s2r[0:1, 0:2])
            st_dma(dbg_s[0:1, 4:6].bitcast(F32), d_cco[0:1, 0:2])
            st_dma(dbg_s[0:1, 6:8].bitcast(F32), d_cci[0:1, 0:2])
            st_dma(dbg_g[:, 0:2], s_gmax[:, 0:2])
            st_dma(dbg_g[0:1, 2:10], s_park[0:1, 0:8])
        wt("gpsimd", [GQ.tok()])
        wt("sync", [LD.tok()])

        with nc.Block() as block:
            @block.sync
            def _(e):
                for fn in prog.streams["sync"]:
                    fn(e)

            @block.vector
            def _(e):
                for fn in prog.streams["vector"]:
                    fn(e)

            @block.scalar
            def _(e):
                for fn in prog.streams["scalar"]:
                    fn(e)

            @block.tensor
            def _(e):
                for fn in prog.streams["tensor"]:
                    fn(e)

            @block.gpsimd
            def _(e):
                for fn in prog.streams["gpsimd"]:
                    fn(e)

    return nc



def make_inputs(core, inp):
    x = np.asarray(inp["L0_attn_x"], dtype=np.float32)
    wq = np.asarray(inp["L0_attn_Wq"], dtype=np.float32)
    wk = np.asarray(inp["L0_attn_Wk"], dtype=np.float32)
    wv = np.asarray(inp["L0_attn_Wv"], dtype=np.float32)
    wp = np.asarray(inp["L0_attn_Wp"], dtype=np.float32)

    def wlay(w):
        return np.ascontiguousarray(
            w.reshape(NIC, 128, C).transpose(1, 0, 2).reshape(128, NIC * C))

    def blay(b, nch):
        return np.ascontiguousarray(np.asarray(b, dtype=np.float32)
                                    .reshape(nch, 128).T)

    cosTM, sswTM = _rotary_tables()
    wt = _wt_tables(inp["L0_attn_rel_bias_table"])
    brows = np.concatenate([np.asarray(inp["L0_attn_bq"], dtype=np.float32),
                            np.asarray(inp["L0_attn_bk"], dtype=np.float32),
                            np.asarray(inp["L0_attn_bv"], dtype=np.float32)])[None, :]
    return {
        "x_fm": np.ascontiguousarray(
            x[core * NSEQ:(core + 1) * NSEQ].transpose(0, 2, 1).reshape(NSEQ * C, T)),
        "wq": wlay(wq), "wk": wlay(wk), "wv": wlay(wv), "wp": wlay(wp),
        "brows": np.ascontiguousarray(brows),
        "bp_c": blay(inp["L0_attn_bp"], NIC),
        "w1": np.ascontiguousarray(np.asarray(inp["L0_ffn_w1"], dtype=np.float32)),
        "b1_c": blay(inp["L0_ffn_b1"], NFC),
        "w2": np.ascontiguousarray(np.asarray(inp["L0_ffn_w2"], dtype=np.float32)),
        "b2_c": blay(inp["L0_ffn_b2"], NIC),
        "wt": wt.reshape(H * 128, 2688),
        "cosTM": cosTM, "sswTM": sswTM,
        "identb": np.eye(128, dtype=ml_dtypes.bfloat16),
        "identf": np.eye(128, dtype=np.float32),
        "epscol": np.full((128, 1), EPS_LN, dtype=np.float32),
    }


def kernel(**inputs):
    nc = build_nc()
    in_maps = [make_inputs(c, inputs) for c in range(NCORES)]
    res = run_bass_kernel_spmd(nc, in_maps, core_ids=list(range(NCORES)))
    outs = [np.asarray(r["out_fm"]).reshape(NSEQ, C, T).transpose(0, 2, 1)
            for r in res.results]
    return np.ascontiguousarray(np.concatenate(outs, axis=0), dtype=np.float32)


# version 06
# speedup vs baseline: 1.2259x; 1.0017x over previous
import math
from contextlib import ExitStack

import numpy as np
import ml_dtypes

import concourse.bass as bass
import concourse.mybir as mybir
from concourse.bass_utils import run_bass_kernel_spmd

F32 = mybir.dt.float32
BF16 = mybir.dt.bfloat16
AF = mybir.ActivationFunctionType
ALU = mybir.AluOpType
AXL = mybir.AxisListType
MAGIC = 12582912.0
NEG = -1.0e38

DEBUG = False
SERIAL_DE = False
LIMIT = ""
HWDBG = False
LAYOUT = {}
NCORES = 8
NSEQ = 2
T = 2048
C = 1024
H = 16
D = 64
FF = 4096
NTT = 4
TTOK = 512
NIC = 8
NFC = 32
EPS_LN = 1e-5
EPS_Q = 1e-8



def _rotary_tables():
    inv = 1.0 / (10000.0 ** (np.arange(0, D, 2, dtype=np.float64) / D))
    t = np.arange(T, dtype=np.float64)
    fr = np.einsum("i,j->ij", t, inv)
    fr = np.concatenate([fr, fr], axis=1)
    cos = np.cos(fr).astype(np.float32)
    sin = np.sin(fr).astype(np.float32)
    ssw = sin.copy()
    ssw[:, 32:] = -ssw[:, 32:]
    cosTM = cos.reshape(16, 128, 64).transpose(1, 0, 2).reshape(128, 16 * 64).copy()
    sswTM = ssw.reshape(16, 128, 64).transpose(1, 0, 2).reshape(128, 16 * 64).copy()
    return cosTM, sswTM


def _bucket(n):
    n = np.asarray(n, dtype=np.int64)
    max_exact = 16
    out = np.where(n < max_exact, n, 0).astype(np.int64)
    big = n >= max_exact
    nsafe = np.maximum(n, 1).astype(np.float64)
    val = max_exact + (np.log(nsafe / max_exact) / math.log(128.0 / max_exact) * 16.0)
    val = val.astype(np.int32)
    val = np.minimum(val, 31)
    out = np.where(big, val, out)
    return out


def _wt_tables(rel_bias_table):
    table = np.asarray(rel_bias_table, dtype=np.float32)
    n = np.arange(-2047, 2048, dtype=np.int64)
    g = np.empty((H, n.size), dtype=np.float32)
    pos = n >= 0
    buck = _bucket(np.maximum(n, 0))
    for h in range(H):
        g[h] = np.where(pos, table[buck, h], np.float32(NEG))
    M = 2688
    p = np.arange(128)[:, None]
    m = np.arange(M)[None, :]
    idx = m - p - 511 + 2047
    idx = np.clip(idx, 0, n.size - 1)
    wt = g[:, idx]
    return np.ascontiguousarray(wt)



class Sem:
    def __init__(self, handle):
        self.h = handle
        self.n = 0

    def inc(self, amount=1):
        self.n += amount
        return self.n

    def tok(self):
        return (self, self.n)


class Prog:

    ENGS = ("sync", "vector", "scalar", "tensor", "gpsimd")

    def __init__(self):
        self.streams = {e: [] for e in self.ENGS}

    def wait(self, eng, tok):
        if tok is None:
            return
        sem, val = tok
        if val <= 0:
            return
        self.streams[eng].append(lambda e, s=sem.h, v=val: e.wait_ge(s, v))

    def waits(self, eng, toks):
        best = {}
        for t in toks:
            if t is None:
                continue
            s, v = t
            if v > 0 and (id(s) not in best or best[id(s)][1] < v):
                best[id(s)] = (s, v)
        for s, v in best.values():
            self.wait(eng, (s, v))

    def emit(self, eng, fn, inc=None, inc_amt=1):
        if inc is None:
            self.streams[eng].append(fn)
            return None
        sem = inc
        val = sem.inc(inc_amt)

        def run(e, fn=fn, s=sem.h, a=inc_amt):
            inst = fn(e)
            inst.then_inc(s, a)
        self.streams[eng].append(run)
        return (sem, val)


class PsumBank:
    def __init__(self, tile):
        self.t = tile
        self.wr = None
        self.rd = None



def build_nc():
    nc = bass.Bass("TRN2", target_bir_lowering=False, detect_race_conditions=False)

    d_x = nc.dram_tensor("x_fm", [NSEQ * C, T], F32, kind="ExternalInput").ap()
    d_wq = nc.dram_tensor("wq", [128, NIC * C], F32, kind="ExternalInput").ap()
    d_wk = nc.dram_tensor("wk", [128, NIC * C], F32, kind="ExternalInput").ap()
    d_wv = nc.dram_tensor("wv", [128, NIC * C], F32, kind="ExternalInput").ap()
    d_wp = nc.dram_tensor("wp", [128, NIC * C], F32, kind="ExternalInput").ap()
    d_brows = nc.dram_tensor("brows", [1, 3 * C], F32, kind="ExternalInput").ap()
    d_bp = nc.dram_tensor("bp_c", [128, NIC], F32, kind="ExternalInput").ap()
    d_w1 = nc.dram_tensor("w1", [FF, C], F32, kind="ExternalInput").ap()
    d_b1 = nc.dram_tensor("b1_c", [128, NFC], F32, kind="ExternalInput").ap()
    d_w2 = nc.dram_tensor("w2", [C, FF], F32, kind="ExternalInput").ap()
    d_b2 = nc.dram_tensor("b2_c", [128, NIC], F32, kind="ExternalInput").ap()
    d_wt = nc.dram_tensor("wt", [H * 128, 2688], F32, kind="ExternalInput").ap()
    d_cos = nc.dram_tensor("cosTM", [128, 16 * 64], F32, kind="ExternalInput").ap()
    d_ssw = nc.dram_tensor("sswTM", [128, 16 * 64], F32, kind="ExternalInput").ap()
    d_idb = nc.dram_tensor("identb", [128, 128], BF16, kind="ExternalInput").ap()
    d_idf = nc.dram_tensor("identf", [128, 128], F32, kind="ExternalInput").ap()
    d_eps = nc.dram_tensor("epscol", [128, 1], F32, kind="ExternalInput").ap()

    d_out = nc.dram_tensor("out_fm", [NSEQ * C, T], F32, kind="ExternalOutput").ap()
    if HWDBG:
        dbg_o1 = nc.dram_tensor("dbg_o1", [NSEQ * C, T], F32, kind="ExternalOutput").ap()
        dbg_y = nc.dram_tensor("dbg_y", [NSEQ * 2 * C, T], BF16, kind="ExternalOutput").ap()
        dbg_h = nc.dram_tensor("dbg_h", [FF, NSEQ * T], F32, kind="ExternalOutput").ap()
        dbg_s = nc.dram_tensor("dbg_s", [1, 8], F32, kind="ExternalOutput").ap()
        dbg_g = nc.dram_tensor("dbg_g", [128, 16], F32, kind="ExternalOutput").ap()
        dbg_e = nc.dram_tensor("dbg_e", [128, 64], F32, kind="ExternalOutput").ap()
        dbg_q = nc.dram_tensor("dbg_q", [NSEQ * 2 * T, C], BF16, kind="ExternalOutput").ap()

    d_wsp = nc.dram_tensor("wsplit", [4 * 2 * 128, NIC * C], BF16)
    d_qs = nc.dram_tensor("q_tm", [NSEQ * 2 * T, C], BF16)
    d_ks = nc.dram_tensor("k_tm", [NSEQ * 2 * T, C], BF16)
    d_vs = nc.dram_tensor("v_tm", [NSEQ * T, H * 65], F32)
    d_yt = nc.dram_tensor("y_fm", [NSEQ * 2 * C, T], BF16)
    d_o1 = nc.dram_tensor("out1_fm", [NSEQ * C, T], F32)
    d_mr = nc.dram_tensor("m2r2", [NSEQ * NTT * 128, 2 * TTOK], F32)
    d_h = nc.dram_tensor("h_fm", [FF, NSEQ * T], F32)
    d_cci = nc.dram_tensor("cc_in", [1, 8], F32)
    d_cco = nc.dram_tensor("cc_out", [1, 8], F32)
    d_cci2 = nc.dram_tensor("cc_in2", [1, 8], F32)
    d_cco2 = nc.dram_tensor("cc_out2", [1, 8], F32)
    if DEBUG:
        d_dbg = nc.dram_tensor("dbg", [128, 8192], F32)
        d_dbgb = nc.dram_tensor("dbgb", [128, 8192], BF16)
        d_dbgw = nc.dram_tensor("dbgw", [128, 16384], BF16)
        d_dbgb2 = nc.dram_tensor("dbgb2", [128, 16384], BF16)

    prog = Prog()

    with ExitStack() as ctx:
        AW = 52992
        arena = ctx.enter_context(nc.sbuf_tensor("arena", [128, AW], F32))

        alloc_off = [0]

        def alloc32(n, align=16):
            o = (alloc_off[0] + align - 1) // align * align
            alloc_off[0] = o + n
            assert alloc_off[0] <= AW, f"arena overflow: {alloc_off[0]} > {AW}"
            return o

        def f32(n, align=16):
            o = alloc32(n, align)
            return arena[:, o:o + n]

        def b16(n, align=16):
            o = alloc32((n + 1) // 2, align)
            return arena[:, o:o + (n + 1) // 2].bitcast(BF16)

        s_ones = f32(128)
        s_i1024 = f32(128)
        s_idf = f32(128)
        s_idb = b16(128)
        s_eps = f32(1)
        s_bp = f32(NIC)
        s_b1 = f32(NFC)
        s_b2 = f32(NIC)
        s_sw1 = f32(NFC)
        s_rw1 = f32(NFC)
        s_sw2 = f32(NIC)
        s_rw2 = f32(NIC)
        s_sc1 = f32(NFC)
        s_sb1 = f32(NFC)
        s_sc2 = f32(NIC)
        s_sb2 = f32(NIC)
        s_gmax = f32(2)
        s_mc1 = f32(64)
        s_mc2 = f32(256)
        s_s1r = f32(2)
        s_s2r = f32(2)
        s_rows = f32(2048)
        s_park = f32(16)
        s_park2 = f32(64)
        s_cc = f32(8)

        PERSIST_END = alloc_off[0]

        pbanks = [PsumBank(ctx.enter_context(nc.psum_tensor(f"ps{i}", [128, 512], F32)))
                  for i in range(7)]
        psbf_t = ctx.enter_context(nc.psum_tensor("psbf", [128, 1024], BF16))
        psbf = PsumBank(psbf_t)

        LD = Sem(ctx.enter_context(nc.semaphore("ld")))
        GQ = Sem(ctx.enter_context(nc.semaphore("gq")))
        PE = Sem(ctx.enter_context(nc.semaphore("pe")))
        DV = Sem(ctx.enter_context(nc.semaphore("dv")))
        AC = Sem(ctx.enter_context(nc.semaphore("ac")))
        CCS1 = Sem(ctx.enter_context(nc.semaphore("ccs1")))
        CCS2 = Sem(ctx.enter_context(nc.semaphore("ccs2")))

        em = prog.emit
        wt = prog.waits

        def ld_dma(dst, src, waits=()):
            wt("sync", waits)
            return em("sync", lambda e, d=dst, s=src: e.dma_start(out=d, in_=s),
                      inc=LD, inc_amt=16)

        def ld_dma_tr(dst, src, waits=()):
            wt("sync", waits)
            return em("sync", lambda e, d=dst, s=src: e.dma_start(out=d, in_=s, transpose=True),
                      inc=LD, inc_amt=16)

        def st_dma(dst, src, waits=()):
            wt("gpsimd", waits)
            return em("gpsimd", lambda e, d=dst, s=src: e.dma_start(out=d, in_=s),
                      inc=GQ, inc_amt=16)

        def v_op(fn, waits=(), inc=False, drain=False):
            wt("vector", waits)
            if drain:
                tok = em("vector", fn, inc=DV)
                prog.wait("vector", tok)
                return tok
            return em("vector", fn, inc=DV if inc else None)

        def a_op(fn, waits=(), inc=False):
            wt("scalar", waits)
            return em("scalar", fn, inc=AC if inc else None)

        def split_hilo(src32, dst_hi, dst_lo, waits=(), inc=True):
            v_op(lambda e, a=dst_hi, b=src32: e.tensor_copy(a, b), waits=waits)
            return v_op(lambda e, o=dst_lo, a=src32, b=dst_hi:
                        e.tensor_tensor(o, a, b, ALU.subtract), inc=inc)

        tk_idf = ld_dma(s_idf, d_idf)
        tk_idb = ld_dma(s_idb, d_idb)
        tk_eps = ld_dma(s_eps, d_eps)
        tk_bp = ld_dma(s_bp, d_bp)
        tk_b1 = ld_dma(s_b1, d_b1)
        tk_b2 = ld_dma(s_b2, d_b2)
        v_op(lambda e: e.memset(s_ones, 1.0))
        v_op(lambda e: e.memset(s_i1024, 1.0 / 1024.0))

        def pool():
            toks = [(PE, PE.n), (DV, DV.n), (AC, AC.n), (GQ, GQ.n), (LD, LD.n)]
            for eng in ("sync", "vector", "scalar", "tensor", "gpsimd"):
                wt(eng, toks)
            alloc_off[0] = PERSIST_END
            return alloc_off[0]

        pool()
        wtmp = [f32(NIC * C) for _ in range(2)]
        whi_st = b16(NIC * C)
        wlo_st = b16(NIC * C)
        wload = [None, None]
        wins = [d_wq, d_wk, d_wv, d_wp]
        prev_done = None
        prev_stores = []
        for wi in range(4):
            par = wi % 2
            if wi == 0:
                wload[0] = ld_dma(wtmp[0], wins[0])
            if wi + 1 < 4:
                wload[(wi + 1) % 2] = ld_dma(wtmp[(wi + 1) % 2], wins[wi + 1],
                                             waits=[] if wi == 0 else [prev_done])
            tk = split_hilo(wtmp[par], whi_st, wlo_st,
                            waits=[wload[par]] + prev_stores)
            t1 = st_dma(d_wsp[(wi * 2 + 0) * 128:(wi * 2 + 1) * 128, :], whi_st, waits=[tk])
            t2 = st_dma(d_wsp[(wi * 2 + 1) * 128:(wi * 2 + 2) * 128, :], wlo_st, waits=[tk])
            prev_done = tk
            prev_stores = [t1, t2]
        wsp_tok = prev_stores[1]

        def emit_ln(chunks, xsq, m_sb, r_sb, negm2, st_b1, st_b2, src_toks,
                    r_free=None):
            pe_sq = [(PE, PE.n), (PE, PE.n)]
            for ic in range(8):
                par = ic % 2
                wl = [src_toks[ic], pe_sq[par]]
                tk = a_op(lambda e, o=xsq[par], i=chunks[ic]:
                          e.activation(o, i, AF.Square), waits=wl, inc=True)
                wt("tensor", [st_b1.rd if ic == 0 else None, src_toks[ic]])
                em("tensor", lambda e, p=st_b1.t, r=chunks[ic], s=(ic == 0), t=(ic == 7):
                   e.matmul(p[0:1, :], s_ones[:, 0:1], r, start=s, stop=t))
                wt("tensor", [st_b2.rd if ic == 0 else None, tk])
                tk2 = em("tensor", lambda e, p=st_b2.t, r=xsq[par], s=(ic == 0), t=(ic == 7):
                         e.matmul(p[0:1, :], s_ones[:, 0:1], r, start=s, stop=t),
                         inc=PE)
                pe_sq[par] = tk2
            tk2 = em("tensor", lambda e: e.drain(), inc=PE)
            st_b1.wr = tk2
            st_b2.wr = tk2
            r1 = s_rows[0:1, 0:512]
            r2 = s_rows[0:1, 512:1024]
            v_op(lambda e: e.tensor_copy(r1, st_b1.t[0:1, :]), waits=[st_b1.wr])
            tk_rows = v_op(lambda e: e.tensor_copy(r2, st_b2.t[0:1, :]), inc=True)
            st_b1.rd = tk_rows
            st_b2.rd = tk_rows
            wt("tensor", [tk_rows, st_b1.rd])
            em("tensor", lambda e: e.matmul(st_b1.t[:, :], s_i1024[0:1, 0:128], r1,
                                            start=True, stop=True))
            em("tensor", lambda e: e.matmul(st_b2.t[:, :], s_i1024[0:1, 0:128], r2,
                                            start=True, stop=True))
            tkb = em("tensor", lambda e: e.drain(), inc=PE)
            st_b1.wr = tkb
            st_b2.wr = tkb
            v_op(lambda e: e.tensor_copy(m_sb, st_b1.t[:, :]), waits=[tkb])
            v_op(lambda e: e.scalar_tensor_tensor(negm2, st_b1.t[:, :], -1.0, m_sb,
                                                  ALU.mult, ALU.mult))
            tk_var = v_op(lambda e: e.tensor_tensor(negm2, st_b2.t[:, :], negm2, ALU.add),
                          inc=True)
            st_b1.rd = tk_var
            st_b2.rd = tk_var
            a_op(lambda e: e.activation(r_sb, negm2, AF.Ln, bias=s_eps),
                 waits=[tk_var, r_free])
            tk_r = a_op(lambda e: e.activation(r_sb, r_sb, AF.Exp, scale=-0.5), inc=True)
            return tk_r

        yt_store_toks = []
        o1_store_toks = []
        mr_store_toks = []
        cmax_tok = None

        for s in range(NSEQ if LIMIT in ("", "D") else (0 if LIMIT == "W" else 1)):
            pool()
            s_cos = f32(1024)
            s_ssw = f32(1024)
            s_brow = f32(3 * C)
            xb = [f32(NIC * 512) for _ in range(2)]
            xsq = [f32(512) for _ in range(2)]
            m_sb = f32(512); r_sb = f32(512); negm2 = f32(512)
            xlnh = b16(NIC * 512); xlnl = b16(NIC * 512)
            t1b = f32(1024); ub = f32(1024); qp32 = f32(1024)
            sth = b16(1024); stl = b16(1024)
            vstage = f32(H * 65)
            wqh = b16(NIC * C); wql = b16(NIC * C)
            wkh = b16(NIC * C); wkl = b16(NIC * C)
            wvh = b16(NIC * C); wvl = b16(NIC * C)
            a_end = alloc_off[0]

            wspw = [wsp_tok] if s == 0 else []
            def wsp_rows(wi, hl):
                r0 = (wi * 2 + hl) * 128
                return d_wsp[r0:r0 + 128, :]
            tk_wq = [ld_dma(wqh, wsp_rows(0, 0), waits=wspw),
                     ld_dma(wql, wsp_rows(0, 1))]
            tk_wk = [ld_dma(wkh, wsp_rows(1, 0)),
                     ld_dma(wkl, wsp_rows(1, 1))]
            tk_wv = [ld_dma(wvh, wsp_rows(2, 0)),
                     ld_dma(wvl, wsp_rows(2, 1))]
            tk_cos = ld_dma(s_cos, d_cos)
            tk_ssw = ld_dma(s_ssw, d_ssw)
            tk_brow = ld_dma(s_brow[0:1, :], d_brows)

            v_op(lambda e: e.memset(vstage, 1.0))

            x_tok = [None, None]
            x_free = [None, None]

            def load_x(tt, dst_par):
                sap = d_x[s * C:(s + 1) * C, tt * 512:(tt + 1) * 512] \
                    .rearrange("(ic p) j -> p ic j", p=128)
                return ld_dma(xb[dst_par][:, :].rearrange("p (ic j) -> p ic j", ic=8),
                              sap, waits=[x_free[dst_par]] if x_free[dst_par] else [])

            x_tok[0] = load_x(0, 0)

            for tt in range(NTT):
                par = tt % 2
                if tt + 1 < NTT:
                    x_tok[(tt + 1) % 2] = load_x(tt + 1, (tt + 1) % 2)
                chunks = [xb[par][:, ic * 512:(ic + 1) * 512] for ic in range(8)]
                src_toks = [x_tok[par]] * 8
                tk_r = emit_ln(chunks, xsq, m_sb, r_sb, negm2,
                               pbanks[5], pbanks[6], src_toks,
                               r_free=(DV, DV.n))
                if DEBUG and s == 0 and tt == 0:
                    dtk = DV.tok()
                    atk = AC.tok()
                    st_dma(d_dbg[:, 0:4096], xb[par], waits=[x_tok[par]])
                    st_dma(d_dbg[:, 4096:4608], m_sb, waits=[dtk, atk])
                    st_dma(d_dbg[:, 4608:5120], r_sb, waits=[dtk, atk])
                    st_dma(d_dbg[:, 5120:5632], negm2, waits=[dtk, atk])
                if tt > 0:
                    wt("vector", [(PE, PE.n)])
                last_split = None
                for ic in range(8):
                    xc = chunks[ic]
                    xh = xlnh[:, ic * 512:(ic + 1) * 512]
                    xl = xlnl[:, ic * 512:(ic + 1) * 512]
                    v_op(lambda e, o=qp32[:, 0:512], a=xc, b=m_sb:
                         e.tensor_tensor(o, a, b, ALU.subtract),
                         waits=[tk_r] if ic == 0 else [])
                    v_op(lambda e, o=qp32[:, 0:512], a=qp32[:, 0:512], b=r_sb:
                         e.tensor_tensor(o, a, b, ALU.mult))
                    last_split = split_hilo(qp32[:, 0:512], xh, xl,
                                            inc=(ic == 7))
                x_free[par] = last_split
                xln_tok = last_split

                if DEBUG and s == 0 and tt == 0:
                    st_dma(d_dbgb[:, 0:4096], xlnh, waits=[xln_tok])
                    st_dma(d_dbgb[:, 4096:8192], xlnl, waits=[xln_tok])
                    st_dma(d_dbgw[:, 0:8192], wvh, waits=tk_wv)
                    st_dma(d_dbgw[:, 8192:16384], wvl, waits=tk_wv)
                for tc in range(4):
                    tcg = tt * 4 + tc
                    for pi, (wh, wl, wtk, brow_off, is_v) in enumerate([
                            (wqh, wql, tk_wq, 0, False),
                            (wkh, wkl, tk_wk, C, False),
                            (wvh, wvl, tk_wv, 2 * C, True)]):
                        for fh in range(2):
                            bank = pbanks[((tc * 3 + pi) * 2 + fh) % 4]
                            wt("tensor", [bank.rd, xln_tok] + wtk + [tk_brow])
                            nmm = 0
                            for ic in range(8):
                                lh = xlnh[:, ic * 512 + tc * 128: ic * 512 + tc * 128 + 128]
                                ll = xlnl[:, ic * 512 + tc * 128: ic * 512 + tc * 128 + 128]
                                rh = wh[:, ic * 1024 + fh * 512: ic * 1024 + fh * 512 + 512]
                                rl = wl[:, ic * 1024 + fh * 512: ic * 1024 + fh * 512 + 512]
                                for (a_, b_) in ((lh, rh), (lh, rl), (ll, rh)):
                                    em("tensor", lambda e, p=bank.t, A=a_, B=b_, st=(nmm == 0):
                                       e.matmul(p[:, :], A, B, start=st, stop=False))
                                    nmm += 1
                            brow = s_brow[0:1, brow_off + fh * 512: brow_off + fh * 512 + 512]
                            bank.wr = em("tensor", lambda e, p=bank.t, r=brow:
                                         e.matmul(p[:, :], s_ones[0:1, 0:128], r,
                                                  start=False, stop=True), inc=PE)
                            if not is_v:
                                cosv = s_cos[:, tcg * 64:(tcg + 1) * 64] \
                                    .unsqueeze(1).broadcast_to((128, 8, 64))
                                sswv = s_ssw[:, tcg * 64:(tcg + 1) * 64] \
                                    .unsqueeze(1).broadcast_to((128, 8, 64))
                                t1v = t1b[:, fh * 512:(fh + 1) * 512] \
                                    .rearrange("p (h m) -> p h m", h=8)
                                uv = ub[:, fh * 512:(fh + 1) * 512] \
                                    .rearrange("p (h m) -> p h m", h=8)
                                pv = bank.t[:, :].rearrange("p (h m) -> p h m", h=8)
                                v_op(lambda e, o=t1v, a=pv, b=cosv:
                                     e.tensor_tensor(o, a, b, ALU.mult),
                                     waits=[bank.wr, tk_cos, tk_ssw])
                                bank.rd = v_op(lambda e, o=uv, a=pv, b=sswv:
                                               e.tensor_tensor(o, a, b, ALU.mult), inc=True)
                                t4 = t1b[:, fh * 512:(fh + 1) * 512] \
                                    .rearrange("p (h z m) -> p h z m", h=8, z=2)
                                u4 = ub[:, fh * 512:(fh + 1) * 512] \
                                    .rearrange("p (h z m) -> p h z m", h=8, z=2)
                                q4 = qp32[:, fh * 512:(fh + 1) * 512] \
                                    .rearrange("p (h z m) -> p h z m", h=8, z=2)
                                v_op(lambda e, o=q4[:, :, 0, :], a=t4[:, :, 0, :],
                                     b=u4[:, :, 1, :]: e.tensor_tensor(o, a, b, ALU.add))
                                v_op(lambda e, o=q4[:, :, 1, :], a=t4[:, :, 1, :],
                                     b=u4[:, :, 0, :]: e.tensor_tensor(o, a, b, ALU.add))
                            else:
                                dstv = vstage[:, fh * 520: fh * 520 + 520] \
                                    .rearrange("p (h m) -> p h m", h=8)[:, :, 0:64]
                                bank.rd = v_op(lambda e, o=dstv,
                                               a=bank.t[:, :].rearrange("p (h m) -> p h m", h=8):
                                               e.tensor_copy(o, a), waits=[bank.wr], inc=True)
                        if not is_v:
                            tkq = split_hilo(qp32, sth, stl, inc=True)
                            dram = d_qs if pi == 0 else d_ks
                            row0 = tt * 512 + tc * 128
                            tk1 = st_dma(dram[(s * 2 + 0) * T + row0:
                                               (s * 2 + 0) * T + row0 + 128, :],
                                         sth, waits=[tkq])
                            tk2 = st_dma(dram[(s * 2 + 1) * T + row0:
                                               (s * 2 + 1) * T + row0 + 128, :],
                                         stl, waits=[tkq])
                            wt("vector", [tk1, tk2])
                        else:
                            row0 = s * T + tt * 512 + tc * 128
                            tkv = st_dma(d_vs[row0:row0 + 128, :], vstage,
                                         waits=[bank.rd])
                            wt("vector", [tkv])

            if LIMIT == "A":
                break
            pool()
            if DEBUG:
                LAYOUT[f"phaseB_start_s{s}"] = alloc_off[0]
            qf = [[b16(T), b16(T)] for _ in range(2)]
            if DEBUG:
                LAYOUT[f"qf_end_s{s}"] = alloc_off[0]
            kf = [[b16(T), b16(T)] for _ in range(2)]
            vsb = [f32(16 * 130) for _ in range(2)]
            wtp = [f32(2 * 2688) for _ in range(2)]
            tbuf = [f32(512) for _ in range(4)]
            ebuf = [f32(512) for _ in range(4)]
            bcb = f32(512)
            zb = f32(512)
            ysh = [b16(T), b16(T)]
            ysl = [b16(T), b16(T)]
            b_end = alloc_off[0]
            assert b_end <= AW

            def load_hp(hp, par, guards):
                toks = []
                for dram, buf in ((d_qs, qf[par]), (d_ks, kf[par])):
                    for hl in range(2):
                        srcap = dram[(s * 2 + hl) * T:(s * 2 + hl + 1) * T,
                                     hp * 128:(hp + 1) * 128]
                        toks.append(ld_dma_tr(buf[hl], srcap, waits=guards))
                sap = d_vs[s * T:(s + 1) * T, hp * 130:(hp + 1) * 130] \
                    .rearrange("(jt p) c -> p jt c", p=128)
                toks.append(ld_dma(vsb[par][:, :].rearrange("p (jt c) -> p jt c", jt=16),
                                   sap, waits=guards))
                toks.append(ld_dma(wtp[par][:, 0:2688],
                                   d_wt[2 * hp * 128:(2 * hp + 1) * 128, :], waits=guards))
                toks.append(ld_dma(wtp[par][:, 2688:5376],
                                   d_wt[(2 * hp + 1) * 128:(2 * hp + 2) * 128, :],
                                   waits=guards))
                return toks

            wt("sync", [GQ.tok()])
            hp_tok = [None, None]
            hp_done = [(), ()]
            av_pe = [None, None, None, None]
            tb_ac = [None, None, None, None]
            hp_tok[0] = load_hp(0, 0, [])

            for hp in range(8):
                par = hp % 2
                if hp + 1 < 8:
                    hp_tok[(hp + 1) % 2] = load_hp(hp + 1, (hp + 1) % 2,
                                                   list(hp_done[(hp + 1) % 2]))
                wt("tensor", hp_tok[par])
                qh, ql = qf[par]
                kh, kl = kf[par]
                av0, av1 = pbanks[4], pbanks[5]
                for qb in range(4):
                    q0 = qb * 512
                    njt = 4 * qb + 4
                    pend = None

                    def emit_av(entry, njt=None):
                        jt2, items = entry
                        for (hh, ebx, tk_exp, slot) in items:
                            avb = av0 if hh == 0 else av1
                            vsl = vsb[par][:, jt2 * 130 + hh * 65:
                                           jt2 * 130 + hh * 65 + 65]
                            wt("tensor", [tk_exp] + ([avb.rd] if jt2 == 0 else []))
                            tk_av = em("tensor", lambda e, p=avb.t, A=vsl, B=ebx,
                                       st=(jt2 == 0), sp=(jt2 == njt - 1):
                                       e.matmul(p[0:65, :], A, B, start=st, stop=sp),
                                       inc=PE)
                            avb.wr = tk_av
                            av_pe[slot] = tk_av

                    for jt in range(njt):
                        sb0 = pbanks[jt % 2]
                        sb1 = pbanks[2 + jt % 2]
                        wt("tensor", [sb0.rd, sb1.rd])
                        for ip, (a_, b_) in enumerate((("h", "h"), ("h", "l"), ("l", "h"))):
                            ka = kh if a_ == "h" else kl
                            qa = qh if b_ == "h" else ql
                            em("tensor", lambda e, p=sb0.t,
                               A=ka[0:64, jt * 128:(jt + 1) * 128],
                               B=qa[0:64, q0:q0 + 512], st=(ip == 0), sp=(ip == 2):
                               e.matmul(p[:, :], A, B, start=st, stop=sp,
                                        tile_position=(0, 0)))
                            tk_qk = em("tensor", lambda e, p=sb1.t,
                                       A=ka[64:128, jt * 128:(jt + 1) * 128],
                                       B=qa[64:128, q0:q0 + 512], st=(ip == 0), sp=(ip == 2):
                                       e.matmul(p[:, :], A, B, start=st, stop=sp,
                                                tile_position=(64, 0)),
                                       inc=PE if ip == 2 else None)
                        sb0.wr = tk_qk
                        sb1.wr = tk_qk
                        off = (q0 - 128 * jt) + 511
                        items = []
                        for hh, sb in ((0, sb0), (1, sb1)):
                            slot = 2 * (jt % 2) + hh
                            tb = tbuf[slot]
                            wslice = wtp[par][:, hh * 2688 + off: hh * 2688 + off + 512]
                            tk_stt = v_op(lambda e, o=tb, p=sb.t, w=wslice:
                                          e.scalar_tensor_tensor(o, p[:, :], 1.0, w,
                                                                 ALU.mult, ALU.add),
                                          waits=[sb.wr, tb_ac[slot]]
                                          + (hp_tok[par] if jt == 0 else []),
                                          inc=True)
                            sb.rd = tk_stt
                            ebx = ebuf[slot]
                            tk_exp = a_op(lambda e, o=ebx, i=tb:
                                          e.activation(o, i, AF.Exp, scale=0.125),
                                          waits=[tk_stt, av_pe[slot]],
                                          inc=True)
                            tb_ac[slot] = tk_exp
                            items.append((hh, ebx, tk_exp, slot))
                        if pend is not None:
                            emit_av(pend, njt)
                        pend = (jt, items)
                    emit_av(pend, njt)
                    for hh, avb in ((0, av0), (1, av1)):
                        srow = s_rows[64:65, hh * 512: hh * 512 + 512]
                        tk_s = v_op(lambda e, o=srow, p=avb.t:
                                    e.tensor_copy(o, p[64:65, :]), waits=[avb.wr], inc=True)
                        a_op(lambda e, o=srow, i=srow: e.activation(o, i, AF.Ln),
                             waits=[tk_s])
                        tk_rr = a_op(lambda e, o=srow, i=srow:
                                     e.activation(o, i, AF.Exp, scale=-1.0), inc=True)
                        ub6 = pbanks[6]
                        wt("tensor", [tk_rr, ub6.rd])
                        ub6.wr = em("tensor", lambda e, p=ub6.t, r=srow:
                                    e.matmul(p[:, :], s_ones[64:65, 0:128], r,
                                             start=True, stop=True), inc=PE)
                        v_op(lambda e, o=bcb, p=ub6.t: e.tensor_copy(o, p[:, :]),
                             waits=[ub6.wr])
                        ub6.rd = v_op(lambda e, o=zb[0:64, :], a=avb.t[0:64, :], b=bcb[0:64, :]:
                                      e.tensor_tensor(o, a, b, ALU.mult), inc=True)
                        avb.rd = ub6.rd
                        ysh_t = ysh[hh][0:64, q0:q0 + 512]
                        ysl_t = ysl[hh][0:64, q0:q0 + 512]
                        v_op(lambda e, o=ysh_t, i=zb[0:64, :]: e.tensor_copy(o, i))
                        v_op(lambda e, o=ysl_t, a=zb[0:64, :], b=ysh_t:
                             e.tensor_tensor(o, a, b, ALU.subtract), inc=True)
                hp_done[par] = ((PE, PE.n), (DV, DV.n), (AC, AC.n))
                tk_y = DV.tok()
                for hh in range(2):
                    rows0 = hp * 128 + hh * 64
                    yh_d = d_yt[(s * 2 + 0) * C + rows0:(s * 2 + 0) * C + rows0 + 64, :]
                    yl_d = d_yt[(s * 2 + 1) * C + rows0:(s * 2 + 1) * C + rows0 + 64, :]
                    yt_store_toks.append(st_dma(yh_d, ysh[hh][0:64, :], waits=[tk_y]))
                    yt_store_toks.append(st_dma(yl_d, ysl[hh][0:64, :], waits=[tk_y]))
                wt("vector", [yt_store_toks[-1]])

            if LIMIT == "B":
                break
            pool()
            wph = b16(NIC * C); wpl = b16(NIC * C)
            ytb = [[b16(NIC * 512), b16(NIC * 512)] for _ in range(2)]
            aob = f32(NIC * 512)
            xcb = [f32(NIC * 512) for _ in range(2)]
            o1b = f32(NIC * 512)
            xsqc = [f32(512) for _ in range(2)]
            m_sbc = f32(512); r_sbc = f32(512); negm2c = f32(512)
            zt = f32(512)
            gmaxcol = s_gmax[:, 0:1]
            c_end = alloc_off[0]

            tk_wph = ld_dma(wph, d_wsp[(3 * 2 + 0) * 128:(3 * 2 + 1) * 128, :],
                            waits=[wsp_tok] if s == 0 else [])
            tk_wpl = ld_dma(wpl, d_wsp[(3 * 2 + 1) * 128:(3 * 2 + 2) * 128, :])
            wt("sync", [GQ.tok()])

            yt_tok = [None, None]
            pe_after_tt = [None, None]

            def load_yt(tt, p2, guards):
                toks = []
                for hl in range(2):
                    srcap = d_yt[(s * 2 + hl) * C:(s * 2 + hl + 1) * C,
                                 tt * 512:(tt + 1) * 512] \
                        .rearrange("(ic p) j -> p ic j", p=128)
                    toks.append(ld_dma(ytb[p2][hl][:, :].rearrange("p (ic j) -> p ic j", ic=8),
                                       srcap, waits=guards))
                return toks

            def load_xc(tt, p2, guards):
                return ld_dma(xcb[p2][:, :].rearrange("p (ic j) -> p ic j", ic=8),
                              d_x[s * C:(s + 1) * C, tt * 512:(tt + 1) * 512]
                              .rearrange("(ic p) j -> p ic j", p=128), waits=guards)

            yt_tok[0] = load_yt(0, 0, [])
            xc_tok = [None, None]
            xc_free = [None, None]
            xc_tok[0] = load_xc(0, 0, [])

            for tt in range(NTT):
                par = tt % 2
                if tt + 1 < NTT:
                    gpar = (tt + 1) % 2
                    yt_tok[gpar] = load_yt(tt + 1, gpar,
                                           [pe_after_tt[gpar]] if pe_after_tt[gpar] else [])
                    xc_tok[gpar] = load_xc(tt + 1, gpar,
                                           [xc_free[gpar]] if xc_free[gpar] else [])
                ao_toks = []
                for co in range(8):
                    bank = pbanks[co % 4]
                    wt("tensor", [bank.rd, tk_wph, tk_wpl] + yt_tok[par])
                    nmm = 0
                    for ic in range(8):
                        for (A, B) in ((wph, ytb[par][0]), (wph, ytb[par][1]),
                                       (wpl, ytb[par][0])):
                            last = nmm == 23
                            tkmm = em("tensor", lambda e, p=bank.t,
                                      a=A[:, ic * 1024 + co * 128: ic * 1024 + co * 128 + 128],
                                      b=B[:, ic * 512:(ic + 1) * 512],
                                      st=(nmm == 0), sp=last:
                                      e.matmul(p[:, :], a, b, start=st, stop=sp),
                                      inc=PE if last else None)
                            nmm += 1
                    bank.wr = tkmm
                    tk_ao = v_op(lambda e, o=aob[:, co * 512:(co + 1) * 512], p=bank.t,
                                 b=s_bp[:, co:co + 1]:
                                 e.tensor_scalar(o, p[:, :], b, None, ALU.add, ALU.bypass),
                                 waits=[bank.wr], inc=True)
                    bank.rd = tk_ao
                    ao_toks.append(tk_ao)
                chunks = [aob[:, ic * 512:(ic + 1) * 512] for ic in range(8)]
                tk_r = emit_ln(chunks, xsqc, m_sbc, r_sbc, negm2c,
                               pbanks[5], pbanks[6], ao_toks, r_free=(DV, DV.n))
                o1_toks = []
                for ic in range(8):
                    oc = o1b[:, ic * 512:(ic + 1) * 512]
                    v_op(lambda e, o=zt, a=chunks[ic], b=m_sbc:
                         e.tensor_tensor(o, a, b, ALU.subtract),
                         waits=[tk_r] if ic == 0 else [])
                    v_op(lambda e, o=zt, a=zt, b=r_sbc:
                         e.tensor_tensor(o, a, b, ALU.mult))
                    tko = v_op(lambda e, o=oc, a=zt, b=xcb[par][:, ic * 512:(ic + 1) * 512]:
                               e.tensor_tensor(o, a, b, ALU.add),
                               waits=[xc_tok[par]], inc=True)
                    o1_toks.append(tko)
                xc_free[par] = o1_toks[-1]
                tk_o1 = st_dma(d_o1[s * C:(s + 1) * C, tt * 512:(tt + 1) * 512]
                               .rearrange("(ic p) j -> p ic j", p=128),
                               o1b[:, :].rearrange("p (ic j) -> p ic j", ic=8),
                               waits=[o1_toks[-1]])
                o1_store_toks.append(tk_o1)
                chunks2 = [o1b[:, ic * 512:(ic + 1) * 512] for ic in range(8)]
                tk_r2 = emit_ln(chunks2, xsqc, m_sbc, r_sbc, negm2c,
                                pbanks[5], pbanks[6], o1_toks, r_free=(DV, DV.n))
                row = (s * NTT + tt) * 128
                tk_mr = st_dma(d_mr[row:row + 128, 0:512], m_sbc, waits=[tk_r2])
                tk_mr2 = st_dma(d_mr[row:row + 128, 512:1024], r_sbc, waits=[tk_r2])
                mr_store_toks += [tk_mr, tk_mr2]
                for ic in range(8):
                    mccol = (s * NTT + tt) * 8 + ic
                    v_op(lambda e, o=zt, a=chunks2[ic], b=m_sbc:
                         e.tensor_tensor(o, a, b, ALU.subtract),
                         waits=[tk_r2] if ic == 0 else [])
                    v_op(lambda e, o=zt, a=zt, b=r_sbc:
                         e.tensor_tensor(o, a, b, ALU.mult))
                    tk_m = v_op(lambda e, o=s_mc1[:, mccol:mccol + 1], i=zt:
                                e.tensor_reduce(o, i, AXL.X, ALU.max),
                                inc=(ic == 7))
                cmax_tok = tk_m
                pe_after_tt[par] = (PE, PE.n)
                wt("vector", [tk_o1])

        if LIMIT and LIMIT != "D":
            pool()
            if LIMIT == "C":
                st_dma(d_out[:, :], d_o1[:, :], waits=[(GQ, GQ.n), (DV, DV.n)])
            elif LIMIT == "B":
                st_dma(d_out[0:1024, :].bitcast(BF16)[:, 0:2048], d_yt[0:1024, :],
                       waits=[(GQ, GQ.n), (DV, DV.n)])
                st_dma(d_out[1024:2048, :].bitcast(BF16)[:, 0:2048], d_yt[2048:3072, :])
            elif LIMIT == "A":
                st_dma(d_out[0:1024, 0:512].bitcast(BF16), d_qs[0:1024, :],
                       waits=[(GQ, GQ.n), (DV, DV.n)])
                st_dma(d_out[0:1024, 512:1024].bitcast(BF16), d_vs[0:1024, 0:1024],
                       waits=[])
                st_dma(d_out[1024:2048, 0:512].bitcast(BF16), d_qs[2048:3072, :])
                st_dma(d_out[1024:2048, 512:1024].bitcast(BF16), d_vs[2048:3072, 0:1024])
            wt("gpsimd", [GQ.tok()])
            wt("sync", [LD.tok()])
            with nc.Block() as block:
                @block.sync
                def _(e):
                    for fn in prog.streams["sync"]:
                        fn(e)

                @block.vector
                def _(e):
                    for fn in prog.streams["vector"]:
                        fn(e)

                @block.scalar
                def _(e):
                    for fn in prog.streams["scalar"]:
                        fn(e)

                @block.tensor
                def _(e):
                    for fn in prog.streams["tensor"]:
                        fn(e)

                @block.gpsimd
                def _(e):
                    for fn in prog.streams["gpsimd"]:
                        fn(e)
            return nc
        def emit_barrier(gcol, cci, cco, dst_pair, prev_tok, ccs):
            if not hasattr(emit_barrier, "park"):
                emit_barrier.park = 0
                emit_barrier.park2 = 0
            """gcol [128,1] local max -> AllReduce max -> dst_pair [128, 2] (s, 1/s)."""
            ub = pbanks[6]
            wt("tensor", [prev_tok, ub.rd, tk_idf])
            ub.wr = em("tensor", lambda e, g=gcol: e.transpose(ub.t[0:1, 0:128], g, s_idf),
                       inc=PE)
            row = s_rows[0:1, 0:128]
            tk1 = v_op(lambda e, o=row[0:1, 0:1], i=ub.t[0:1, 0:128]:
                       e.tensor_reduce(o, i, AXL.X, ALU.max), waits=[ub.wr], inc=True, drain=True)
            ub.rd = tk1
            if HWDBG:
                pk = emit_barrier.park
                v_op(lambda e, o=s_park[0:1, pk:pk + 1], i=row[0:1, 0:1]:
                     e.tensor_copy(o, i))
                v_op(lambda e, o=s_park[0:1, 4 + pk * 2: 6 + pk * 2],
                     i=ub.t[0:1, 0:2]: e.tensor_copy(o, i))
                emit_barrier.park += 1
            tkc = v_op(lambda e, o=s_cc[0:1, :], i=row[0:1, 0:1].broadcast_to((1, 8)):
                       e.tensor_copy(o, i), inc=True, drain=True)
            if HWDBG:
                pk2 = emit_barrier.park2
                v_op(lambda e, o=s_park[0:1, 2 + pk2:3 + pk2], i=s_cc[0:1, 0:1]:
                     e.tensor_copy(o, i))
                emit_barrier.park2 += 1
            tkw = st_dma(cci.ap(), s_cc[0:1, :], waits=[tkc])
            wt("gpsimd", [tkw])
            tk_cc = em("gpsimd", lambda e: e.collective_compute(
                "AllReduce", ALU.max, replica_groups=[list(range(NCORES))],
                ins=[cci.ap()], outs=[cco.ap()]), inc=ccs)
            wt("gpsimd", [tk_cc])
            tk_rd = em("gpsimd", lambda e: e.dma_start(out=s_cc[0:1, :], in_=cco.ap()),
                       inc=GQ, inc_amt=16)
            r0 = s_rows[0:1, 0:1]
            r1 = s_rows[0:1, 1:2]
            r2 = s_rows[0:1, 2:4]
            v_op(lambda e: e.tensor_scalar(r0, s_cc[0:1, 0:1], EPS_Q, 1.0 / 255.0,
                                           ALU.max, ALU.mult), waits=[tk_rd], drain=True)
            v_op(lambda e: e.tensor_scalar(r1, s_cc[0:1, 0:1], EPS_Q, None,
                                           ALU.max, ALU.bypass), drain=True)
            v_op(lambda e: e.reciprocal(r1, r1), drain=True)
            v_op(lambda e: e.tensor_scalar(r1, r1, 255.0, None, ALU.mult, ALU.bypass),
                 drain=True)
            v_op(lambda e: e.tensor_copy(r2[0:1, 0:1], r0), drain=True)
            tk2 = v_op(lambda e: e.tensor_copy(r2[0:1, 1:2], r1), inc=True, drain=True)
            wt("tensor", [tk2, ub.rd])
            ub.wr = em("tensor", lambda e: e.matmul(ub.t[:, 0:2], s_ones[0:1, 0:128], r2,
                                                    start=True, stop=True), inc=PE)
            tk3 = v_op(lambda e, o=dst_pair: e.tensor_copy(o, ub.t[:, 0:2]),
                       waits=[ub.wr], inc=True)
            ub.rd = tk3
            return tk3

        cmax_tok = v_op(lambda e: e.tensor_reduce(s_gmax[:, 0:1], s_mc1,
                                                  AXL.X, ALU.max),
                        waits=[cmax_tok], inc=True, drain=True)
        tk_s1 = emit_barrier(s_gmax[:, 0:1], d_cci, d_cco, s_s1r, cmax_tok, CCS1)

        pool()
        w1qT = b16(NIC * FF)
        xq2 = b16(NIC * NSEQ * T)
        wld = [f32(1024), f32(1024)]
        wqi = b16(1024)
        o1l = [f32(NIC * 512) for _ in range(2)]
        mrb = [[f32(512), f32(512)] for _ in range(2)]
        mr_free = [None, None]
        zq = f32(512)
        hst = [f32(512) for _ in range(2)]
        hcp = [f32(512) for _ in range(2)]
        hst_free = [(), ()]
        hcp_free = [None, None]
        d_end = alloc_off[0]
        assert d_end <= AW, d_end

        w1fc_tok = [None] * NFC
        wl_tok = [None, None]
        wl_free = [None, None]
        wl_tok[0] = ld_dma(wld[0], d_w1[0:128, :])
        for fc in range(NFC):
            par = fc % 2
            if fc + 1 < NFC:
                wl_tok[(fc + 1) % 2] = ld_dma(wld[(fc + 1) % 2],
                                              d_w1[(fc + 1) * 128:(fc + 2) * 128, :],
                                              waits=[wl_free[(fc + 1) % 2]]
                                              if wl_free[(fc + 1) % 2] else [])
            amax = s_rows[:, 0:1]
            v_op(lambda e, o=amax, i=wld[par]:
                 e.tensor_reduce(o, i, AXL.X, ALU.max, apply_absolute_value=True),
                 waits=[wl_tok[par], tk_b1], drain=True)
            v_op(lambda e, o=s_sw1[:, fc:fc + 1], i=amax:
                 e.tensor_scalar(o, i, EPS_Q, (1.0 / 127.0), ALU.max, ALU.mult),
                 drain=True)
            v_op(lambda e, o=s_rw1[:, fc:fc + 1], i=amax:
                 e.tensor_scalar(o, i, EPS_Q, None, ALU.max, ALU.bypass), drain=True)
            v_op(lambda e, o=s_rw1[:, fc:fc + 1], i=s_rw1[:, fc:fc + 1]:
                 e.reciprocal(o, i), drain=True)
            v_op(lambda e, o=wld[par], i=wld[par], r=s_rw1[:, fc:fc + 1]:
                 e.tensor_scalar(o, i, r, 127.0, ALU.mult, ALU.mult))
            v_op(lambda e, o=wld[par], i=wld[par]:
                 e.tensor_scalar(o, i, MAGIC, MAGIC, ALU.add, ALU.subtract))
            v_op(lambda e, o=wld[par], i=wld[par]:
                 e.tensor_scalar(o, i, 127.0, -128.0, ALU.min, ALU.max))
            tk_q = v_op(lambda e, o=wqi, i=wld[par]: e.tensor_copy(o, i),
                        waits=[(PE, PE.n)], inc=True)
            wl_free[par] = tk_q
            v_op(lambda e, o=s_sc1[:, fc:fc + 1], a=s_sw1[:, fc:fc + 1], b=s_s1r[:, 0:1]:
                 e.tensor_tensor(o, a, b, ALU.mult), waits=[tk_s1], drain=True)
            v_op(lambda e, o=s_rows[:, 1:2], i=s_sc1[:, fc:fc + 1]: e.reciprocal(o, i), drain=True)
            v_op(lambda e, o=s_rows[:, 2:3], a=s_b1[:, fc:fc + 1], b=s_rows[:, 1:2]:
                 e.tensor_tensor(o, a, b, ALU.mult), drain=True)
            v_op(lambda e, o=s_rows[:, 2:3], i=s_rows[:, 2:3]:
                 e.tensor_scalar(o, i, MAGIC, MAGIC, ALU.add, ALU.subtract), drain=True)
            v_op(lambda e, o=s_sb1[:, fc:fc + 1], a=s_rows[:, 2:3], b=s_sc1[:, fc:fc + 1]:
                 e.tensor_tensor(o, a, b, ALU.mult), drain=True)
            for g in range(2):
                wt("tensor", [tk_q, psbf.rd, tk_idb])
                for q4 in range(4):
                    ic = g * 4 + q4
                    tktr = em("tensor", lambda e, i=wqi[:, ic * 128:(ic + 1) * 128],
                              o=psbf.t[:, q4 * 128:(q4 + 1) * 128]:
                              e.transpose(o, i, s_idb), inc=PE if q4 == 3 else None)
                psbf.wr = tktr
                tk_e = None
                for q4 in range(4):
                    ic = g * 4 + q4
                    tk_e = v_op(lambda e, o=w1qT[:, ic * FF + fc * 128: ic * FF + fc * 128 + 128],
                                i=psbf.t[:, q4 * 128:(q4 + 1) * 128]:
                                e.tensor_copy(o, i),
                                waits=[psbf.wr] if q4 == 0 else [], inc=(q4 == 3))
                psbf.rd = tk_e
            w1fc_tok[fc] = tk_e
        w1q_tok = DV.tok()

        o1_tok = [None, None]
        o1_free = [None, None]

        def load_o1(si, tti, p2):
            return ld_dma(o1l[p2][:, :].rearrange("p (ic j) -> p ic j", ic=8),
                          d_o1[si * C:(si + 1) * C, tti * 512:(tti + 1) * 512]
                          .rearrange("(ic p) j -> p ic j", p=128),
                          waits=[o1_free[p2]] if o1_free[p2] else [])

        wt("sync", [GQ.tok()])
        o1_tok[0] = load_o1(0, 0, 0)
        tiles = [(si, tti) for si in range(NSEQ) for tti in range(NTT)]
        xq2_tile_tok = [None] * len(tiles)

        def emit_xq2(i):
            si, tti = tiles[i]
            par = i % 2
            if i + 1 < len(tiles):
                o1_tok[(i + 1) % 2] = load_o1(*tiles[i + 1], (i + 1) % 2)
            rrow = (si * NTT + tti) * 128
            tk_m2a = ld_dma(mrb[par][0], d_mr[rrow:rrow + 128, 0:512],
                            waits=[mr_free[par]] if mr_free[par] else [])
            tk_m2 = ld_dma(mrb[par][1], d_mr[rrow:rrow + 128, 512:1024])
            tk_xq = None
            for ic in range(8):
                col = ic * (NSEQ * T) + si * T + tti * 512
                v_op(lambda e, o=zq, a=o1l[par][:, ic * 512:(ic + 1) * 512], b=mrb[par][0]:
                     e.tensor_tensor(o, a, b, ALU.subtract),
                     waits=[o1_tok[par], tk_m2a, tk_m2] if ic == 0 else [])
                v_op(lambda e, o=zq, a=zq, b=mrb[par][1]:
                     e.tensor_tensor(o, a, b, ALU.mult))
                v_op(lambda e, o=zq, i=zq, r=s_s1r[:, 1:2]:
                     e.tensor_scalar(o, i, r, MAGIC, ALU.mult, ALU.add))
                v_op(lambda e, o=zq, i=zq:
                     e.tensor_scalar(o, i, MAGIC, 255.0, ALU.subtract, ALU.min))
                tk_xq = v_op(lambda e, o=xq2[:, col:col + 512], i=zq:
                             e.tensor_scalar(o, i, 0.0, None, ALU.max, ALU.bypass),
                             inc=(ic == 7))
            o1_free[par] = tk_xq
            mr_free[par] = tk_xq
            xq2_tile_tok[i] = tk_xq

        emit_xq2(0)

        dmax_tok = None
        for tt in range(NSEQ * NTT):
            if SERIAL_DE:
                toks = [(PE, PE.n), (DV, DV.n), (AC, AC.n), (GQ, GQ.n), (LD, LD.n)]
                for engx in ("sync", "vector", "scalar", "tensor", "gpsimd"):
                    wt(engx, toks)
            if tt + 1 < NSEQ * NTT:
                emit_xq2(tt + 1)
            for fc in range(NFC):
                bank = pbanks[(tt * NFC + fc) % 4]
                wt("tensor", [bank.rd, xq2_tile_tok[tt], w1fc_tok[fc]])
                for ic in range(8):
                    em("tensor", lambda e, p=bank.t,
                       A=w1qT[:, ic * FF + fc * 128: ic * FF + fc * 128 + 128],
                       B=xq2[:, ic * (NSEQ * T) + tt * 512:
                             ic * (NSEQ * T) + tt * 512 + 512],
                       st=(ic == 0), sp=(ic == 7):
                       e.matmul(p[:, :], A, B, start=st, stop=sp))
                bank.wr = em("tensor", lambda e: e.drain(), inc=PE)
                par = (tt * NFC + fc) % 2
                tk_h = a_op(lambda e, o=hst[par], p=bank.t, sc=s_sc1[:, fc:fc + 1],
                            b=s_sb1[:, fc:fc + 1]:
                            e.activation(o, p[:, :], AF.Relu, bias=b, scale=sc),
                            waits=[bank.wr, w1q_tok, tk_s1] + list(hst_free[par]),
                            inc=True)
                bank.rd = tk_h
                gcol = tt * NFC + fc
                tk_cp = v_op(lambda e, o=hcp[par], i=hst[par]: e.tensor_copy(o, i),
                             waits=[tk_h] + ([hcp_free[par]] if hcp_free[par] else []))
                dmax_tok = v_op(lambda e, o=s_mc2[:, gcol:gcol + 1], i=hst[par]:
                                e.tensor_reduce(o, i, AXL.X, ALU.max),
                                inc=True)
                tk_st = st_dma(d_h[fc * 128:(fc + 1) * 128, tt * 512:(tt + 1) * 512],
                               hcp[par], waits=[dmax_tok])
                hcp_free[par] = tk_st
                hst_free[par] = (dmax_tok,)

        dmax_tok = v_op(lambda e: e.tensor_reduce(s_gmax[:, 1:2], s_mc2,
                                                  AXL.X, ALU.max),
                        waits=[dmax_tok], inc=True, drain=True)
        tk_s2 = emit_barrier(s_gmax[:, 1:2], d_cci2, d_cco2, s_s2r, dmax_tok, CCS2)

        if LIMIT == "D":
            pool()
            st_dma(d_out[:, :], d_h[0:2048, 0:2048], waits=[(GQ, GQ.n), (DV, DV.n)])
            _t = st_dma(d_out[0:1, 0:2].bitcast(F32), None, waits=[]) if False else None
            st_dma(d_out[2047:2048, 0:2], s_s1r[0:1, 0:2], waits=[tk_s2])
        pool()
        w2qT = b16(NFC * C)
        hq = b16(NFC * 512)
        w2ld = [f32(FF // 2), f32(FF // 2)]
        w2qi = b16(FF)
        o2b = f32(NIC * 512)
        o1e = [f32(NIC * 512) for _ in range(2)]
        hqe = [f32(512) for _ in range(4)]
        xsqe = [f32(512) for _ in range(2)]
        m_e = f32(512); r_e = f32(512); negm2e = f32(512)
        zte = f32(512)
        amax2 = f32(2)
        e_end = alloc_off[0]
        assert e_end <= AW, e_end

        w2_free = None
        w2cc_tok = [None] * NIC
        for cc in range(NIC if LIMIT != "D" else 0):
            tk_l0 = ld_dma(w2ld[0], d_w2[cc * 128:(cc + 1) * 128, 0:2048],
                           waits=[w2_free] if w2_free else [])
            tk_l1 = ld_dma(w2ld[1], d_w2[cc * 128:(cc + 1) * 128, 2048:4096],
                           waits=[w2_free] if w2_free else [])
            v_op(lambda e, o=amax2[:, 0:1], i=w2ld[0]:
                 e.tensor_reduce(o, i, AXL.X, ALU.max, apply_absolute_value=True),
                 waits=[tk_l0, tk_b2], drain=True)
            v_op(lambda e, o=amax2[:, 1:2], i=w2ld[1]:
                 e.tensor_reduce(o, i, AXL.X, ALU.max, apply_absolute_value=True),
                 waits=[tk_l1], drain=True)
            amax = s_rows[:, 0:1]
            v_op(lambda e, o=amax, a=amax2[:, 0:1], b=amax2[:, 1:2]:
                 e.tensor_tensor(o, a, b, ALU.max), drain=True)
            v_op(lambda e, o=s_sw2[:, cc:cc + 1], i=amax:
                 e.tensor_scalar(o, i, EPS_Q, (1.0 / 127.0), ALU.max, ALU.mult),
                 drain=True)
            v_op(lambda e, o=s_rw2[:, cc:cc + 1], i=amax:
                 e.tensor_scalar(o, i, EPS_Q, None, ALU.max, ALU.bypass), drain=True)
            v_op(lambda e, o=s_rw2[:, cc:cc + 1], i=s_rw2[:, cc:cc + 1]:
                 e.reciprocal(o, i), drain=True)
            tk_q = None
            for hq2 in range(2):
                wbuf = w2ld[hq2]
                v_op(lambda e, o=wbuf, i=wbuf, r=s_rw2[:, cc:cc + 1]:
                     e.tensor_scalar(o, i, r, 127.0, ALU.mult, ALU.mult))
                v_op(lambda e, o=wbuf, i=wbuf:
                     e.tensor_scalar(o, i, MAGIC, MAGIC, ALU.add, ALU.subtract))
                v_op(lambda e, o=wbuf, i=wbuf:
                     e.tensor_scalar(o, i, 127.0, -128.0, ALU.min, ALU.max))
                tk_q = v_op(lambda e, o=w2qi[:, hq2 * 2048:(hq2 + 1) * 2048], i=wbuf:
                            e.tensor_copy(o, i), waits=[(PE, PE.n)] if hq2 == 0 else [],
                            inc=(hq2 == 1))
            w2_free = tk_q
            v_op(lambda e, o=s_sc2[:, cc:cc + 1], a=s_sw2[:, cc:cc + 1], b=s_s2r[:, 0:1]:
                 e.tensor_tensor(o, a, b, ALU.mult), waits=[tk_s2], drain=True)
            v_op(lambda e, o=s_rows[:, 1:2], i=s_sc2[:, cc:cc + 1]: e.reciprocal(o, i), drain=True)
            v_op(lambda e, o=s_rows[:, 2:3], a=s_b2[:, cc:cc + 1], b=s_rows[:, 1:2]:
                 e.tensor_tensor(o, a, b, ALU.mult), drain=True)
            v_op(lambda e, o=s_rows[:, 2:3], i=s_rows[:, 2:3]:
                 e.tensor_scalar(o, i, MAGIC, MAGIC, ALU.add, ALU.subtract), drain=True)
            v_op(lambda e, o=s_sb2[:, cc:cc + 1], a=s_rows[:, 2:3], b=s_sc2[:, cc:cc + 1]:
                 e.tensor_tensor(o, a, b, ALU.mult), drain=True)
            for g in range(8):
                wt("tensor", [tk_q, psbf.rd])
                for q4 in range(4):
                    fc = g * 4 + q4
                    tktr = em("tensor", lambda e, i=w2qi[:, fc * 128:(fc + 1) * 128],
                              o=psbf.t[:, q4 * 128:(q4 + 1) * 128]:
                              e.transpose(o, i, s_idb), inc=PE if q4 == 3 else None)
                psbf.wr = tktr
                tk_e = None
                for q4 in range(4):
                    fc = g * 4 + q4
                    tk_e = v_op(lambda e, o=w2qT[:, fc * C + cc * 128: fc * C + cc * 128 + 128],
                                i=psbf.t[:, q4 * 128:(q4 + 1) * 128]:
                                e.tensor_copy(o, i),
                                waits=[psbf.wr] if q4 == 0 else [], inc=(q4 == 3))
                psbf.rd = tk_e
            w2cc_tok[cc] = tk_e
        w2q_tok = DV.tok()

        if DEBUG:
            st_dma(d_dbgw[:, 0:16384], w2qT[:, 0:16384], waits=[w2q_tok])
            st_dma(d_dbgb[:, 0:8192], w2qT[:, 16384:24576], waits=[w2q_tok])
        h_tok = [None] * 4
        h_free = [None] * 4
        oe_tok = [None, None]
        oe_free = [None, None]

        def load_h(tt, fc, p2):
            return ld_dma(hqe[p2], d_h[fc * 128:(fc + 1) * 128, tt * 512:(tt + 1) * 512],
                          waits=[h_free[p2]] if h_free[p2] else [])

        NHB = 4

        def load_o1e(i, p2):
            si, tti = divmod(i, NTT)
            return ld_dma(o1e[p2][:, :].rearrange("p (ic j) -> p ic j", ic=8),
                          d_o1[si * C:(si + 1) * C, tti * 512:(tti + 1) * 512]
                          .rearrange("(ic p) j -> p ic j", p=128),
                          waits=[oe_free[p2]] if oe_free[p2] else [])

        wt("sync", [GQ.tok()])
        for pre in range(3):
            h_tok[pre] = load_h(pre // NFC, pre % NFC, pre)
        oe_tok[0] = load_o1e(0, 0)
        out_toks = []
        ffn2_pe = None
        for tt in range(NSEQ * NTT if LIMIT != "D" else 0):
            if SERIAL_DE:
                toks = [(PE, PE.n), (DV, DV.n), (AC, AC.n), (GQ, GQ.n), (LD, LD.n)]
                for engx in ("sync", "vector", "scalar", "tensor", "gpsimd"):
                    wt(engx, toks)
            if ffn2_pe is not None:
                wt("vector", [ffn2_pe])
            for fc in range(NFC):
                par = fc % 4
                nxt = tt * NFC + fc + 3
                if nxt < NSEQ * NTT * NFC:
                    h_tok[nxt % 4] = load_h(nxt // NFC, nxt % NFC, nxt % 4)
                v_op(lambda e, o=zte, i=hqe[par], r=s_s2r[:, 1:2]:
                     e.tensor_scalar(o, i, r, MAGIC, ALU.mult, ALU.add),
                     waits=[h_tok[par], tk_s2])
                v_op(lambda e, o=zte, i=zte:
                     e.tensor_scalar(o, i, MAGIC, 255.0, ALU.subtract, ALU.min))
                tk_hq = v_op(lambda e, o=hq[:, fc * 512:(fc + 1) * 512], i=zte:
                             e.tensor_scalar(o, i, 0.0, None, ALU.max, ALU.bypass),
                             inc=True)
                h_free[par] = tk_hq
            hq_tok = tk_hq
            if tt + 1 < NSEQ * NTT:
                oe_tok[(tt + 1) % 2] = load_o1e(tt + 1, (tt + 1) % 2)
            par_o = tt % 2
            o2_toks = []
            for co in range(NIC):
                bank = pbanks[co % 4]
                wt("tensor", [bank.rd, hq_tok, w2cc_tok[co]])
                for fc in range(NFC):
                    em("tensor", lambda e, p=bank.t,
                       A=w2qT[:, fc * C + co * 128: fc * C + co * 128 + 128],
                       B=hq[:, fc * 512:(fc + 1) * 512],
                       st=(fc == 0), sp=(fc == NFC - 1):
                       e.matmul(p[:, :], A, B, start=st, stop=sp))
                bank.wr = em("tensor", lambda e: e.drain(), inc=PE)
                tk_o2 = v_op(lambda e, o=o2b[:, co * 512:(co + 1) * 512], p=bank.t,
                             sc=s_sc2[:, co:co + 1], b=s_sb2[:, co:co + 1]:
                             e.tensor_scalar(o, p[:, :], sc, b, ALU.mult, ALU.add),
                             waits=[bank.wr], inc=True)
                bank.rd = tk_o2
                o2_toks.append(tk_o2)
            if DEBUG and tt == 0:
                _dt1 = st_dma(d_dbg[:, 0:4096], o2b, waits=[o2_toks[-1]])
                _dt2 = st_dma(d_dbgb2[:, 0:16384], hq, waits=[hq_tok])
                wt("vector", [_dt1, _dt2])
            chunks = [o2b[:, ic * 512:(ic + 1) * 512] for ic in range(8)]
            tk_r = emit_ln(chunks, xsqe, m_e, r_e, negm2e,
                           pbanks[5], pbanks[6], o2_toks, r_free=(DV, DV.n))
            if HWDBG:
                v_op(lambda e, o=s_park2[:, tt * 4: tt * 4 + 1], i=m_e[:, 0:1]:
                     e.tensor_copy(o, i), waits=[tk_r])
                v_op(lambda e, o=s_park2[:, tt * 4 + 1: tt * 4 + 2], i=o2b[:, 0:1]:
                     e.tensor_copy(o, i))
                v_op(lambda e, o=s_park2[:, tt * 4 + 2: tt * 4 + 3], i=hq[:, 0:1]:
                     e.tensor_copy(o, i))
                v_op(lambda e, o=s_park2[:, tt * 4 + 3: tt * 4 + 4], i=hq[:, 512:513]:
                     e.tensor_copy(o, i))
            for ic in range(8):
                oc = o2b[:, ic * 512:(ic + 1) * 512]
                v_op(lambda e, o=zte, a=chunks[ic], b=m_e:
                     e.tensor_tensor(o, a, b, ALU.subtract),
                     waits=[tk_r, oe_tok[par_o]] if ic == 0 else [])
                v_op(lambda e, o=zte, a=zte, b=r_e:
                     e.tensor_tensor(o, a, b, ALU.mult))
                tk_f = v_op(lambda e, o=oc, a=zte, b=o1e[par_o][:, ic * 512:(ic + 1) * 512]:
                            e.tensor_tensor(o, a, b, ALU.add), inc=(ic == 7))
            oe_free[par_o] = tk_f
            si, tti = divmod(tt, NTT)
            tk_out = st_dma(d_out[si * C:(si + 1) * C, tti * 512:(tti + 1) * 512]
                            .rearrange("(ic p) j -> p ic j", p=128),
                            o2b[:, :].rearrange("p (ic j) -> p ic j", ic=8),
                            waits=[tk_f])
            out_toks.append(tk_out)
            wt("vector", [tk_out])
            ffn2_pe = (PE, PE.n)

        if HWDBG:
            wt("gpsimd", [GQ.tok(), (DV, DV.n), (PE, PE.n), (AC, AC.n)])
            st_dma(dbg_o1, d_o1[:, :])
            st_dma(dbg_y, d_yt[:, :])
            st_dma(dbg_h, d_h[:, :])
            st_dma(dbg_q, d_qs[:, :])
            st_dma(dbg_s[0:1, 0:2], s_s1r[0:1, 0:2])
            st_dma(dbg_s[0:1, 2:4], s_s2r[0:1, 0:2])
            st_dma(dbg_s[0:1, 4:6].bitcast(F32), d_cco[0:1, 0:2])
            st_dma(dbg_s[0:1, 6:8].bitcast(F32), d_cci[0:1, 0:2])
            st_dma(dbg_g[:, 0:2], s_gmax[:, 0:2])
            st_dma(dbg_g[0:1, 2:10], s_park[0:1, 0:8])
            st_dma(dbg_e, s_park2)
        wt("gpsimd", [GQ.tok()])
        wt("sync", [LD.tok()])

        with nc.Block() as block:
            @block.sync
            def _(e):
                for fn in prog.streams["sync"]:
                    fn(e)

            @block.vector
            def _(e):
                for fn in prog.streams["vector"]:
                    fn(e)

            @block.scalar
            def _(e):
                for fn in prog.streams["scalar"]:
                    fn(e)

            @block.tensor
            def _(e):
                for fn in prog.streams["tensor"]:
                    fn(e)

            @block.gpsimd
            def _(e):
                for fn in prog.streams["gpsimd"]:
                    fn(e)

    return nc



def make_inputs(core, inp):
    x = np.asarray(inp["L0_attn_x"], dtype=np.float32)
    wq = np.asarray(inp["L0_attn_Wq"], dtype=np.float32)
    wk = np.asarray(inp["L0_attn_Wk"], dtype=np.float32)
    wv = np.asarray(inp["L0_attn_Wv"], dtype=np.float32)
    wp = np.asarray(inp["L0_attn_Wp"], dtype=np.float32)

    def wlay(w):
        return np.ascontiguousarray(
            w.reshape(NIC, 128, C).transpose(1, 0, 2).reshape(128, NIC * C))

    def blay(b, nch):
        return np.ascontiguousarray(np.asarray(b, dtype=np.float32)
                                    .reshape(nch, 128).T)

    cosTM, sswTM = _rotary_tables()
    wt = _wt_tables(inp["L0_attn_rel_bias_table"])
    brows = np.concatenate([np.asarray(inp["L0_attn_bq"], dtype=np.float32),
                            np.asarray(inp["L0_attn_bk"], dtype=np.float32),
                            np.asarray(inp["L0_attn_bv"], dtype=np.float32)])[None, :]
    return {
        "x_fm": np.ascontiguousarray(
            x[core * NSEQ:(core + 1) * NSEQ].transpose(0, 2, 1).reshape(NSEQ * C, T)),
        "wq": wlay(wq), "wk": wlay(wk), "wv": wlay(wv), "wp": wlay(wp),
        "brows": np.ascontiguousarray(brows),
        "bp_c": blay(inp["L0_attn_bp"], NIC),
        "w1": np.ascontiguousarray(np.asarray(inp["L0_ffn_w1"], dtype=np.float32)),
        "b1_c": blay(inp["L0_ffn_b1"], NFC),
        "w2": np.ascontiguousarray(np.asarray(inp["L0_ffn_w2"], dtype=np.float32)),
        "b2_c": blay(inp["L0_ffn_b2"], NIC),
        "wt": wt.reshape(H * 128, 2688),
        "cosTM": cosTM, "sswTM": sswTM,
        "identb": np.eye(128, dtype=ml_dtypes.bfloat16),
        "identf": np.eye(128, dtype=np.float32),
        "epscol": np.full((128, 1), EPS_LN, dtype=np.float32),
    }


def kernel(**inputs):
    nc = build_nc()
    in_maps = [make_inputs(c, inputs) for c in range(NCORES)]
    res = run_bass_kernel_spmd(nc, in_maps, core_ids=list(range(NCORES)))
    outs = [np.asarray(r["out_fm"]).reshape(NSEQ, C, T).transpose(0, 2, 1)
            for r in res.results]
    return np.ascontiguousarray(np.concatenate(outs, axis=0), dtype=np.float32)
